# version 06
# baseline (speedup 1.0000x reference)
import math
import contextlib
import numpy as np
import ml_dtypes
import concourse.bass as bass
import concourse.mybir as mybir
from concourse.bass_utils import run_bass_kernel_spmd

F32 = mybir.dt.float32
BF16 = mybir.dt.bfloat16
AF = mybir.ActivationFunctionType
ALU = mybir.AluOpType
AX = mybir.AxisListType

S = 2048
H = 4096
NH, NKV, G, D = 32, 8, 4, 128
INTER = 11008
HC = H // 128
IC = INTER // 128
TB = 4
SCALE = D ** -0.5
INV_H = 1.0 / H
EPS = 1e-5
MASK_D = 30000.0

ENGINES = ("sync", "gpsimd", "tensor", "scalar", "vector")
SUPERS = [list(range(s, min(s + 8, IC))) for s in range(0, IC, 8)]


class Sem:
    __slots__ = ("h", "n", "name")

    def __init__(self, h, name):
        self.h = h
        self.n = 0
        self.name = name

    def bump(self, k=1):
        self.n += k
        return self.n

    def take(self, k=1):
        self.n += k
        return k


class Plan:
    def __init__(self):
        self.ev = {e: [] for e in ENGINES}

    def add(self, e, f):
        self.ev[e].append(f)

    def emit(self, nc):
        with nc.Block() as blk:
            for e in ENGINES:
                evs = self.ev[e]
                if not evs:
                    continue

                def mk(evs=evs):
                    def body(eng):
                        for f in evs:
                            f(eng)
                    return body
                getattr(blk, e)(mk())


def I(plan, e, build, waits=(), inc=None, inc2=None):
    waits = tuple((s.h, v) for (s, v) in waits)

    def f(eng, build=build, waits=waits, inc=inc, inc2=inc2):
        for h, v in waits:
            eng.wait_ge(h, v)
        ins = build(eng)
        if inc is not None:
            ins = ins.then_inc(inc[0].h, inc[1])
        if inc2 is not None:
            ins = ins.then_inc(inc2[0].h, inc2[1])
    plan.add(e, f)


def sp3(ap, w):
    return ap.rearrange("p (c t) -> p c t", t=w)


def build_program(n_phases=99, debug=False):
    nc = bass.Bass("TRN2", target_bir_lowering=False, debug=False)
    dbg_kind = "ExternalOutput" if debug else "Internal"

    x_t = nc.dram_tensor("x_t", [HC, 128, S], F32, kind="ExternalInput").ap()
    wqk_t = nc.dram_tensor("wqk_t", [40, 128, HC, 128], F32, kind="ExternalInput").ap()
    wv_t = nc.dram_tensor("wv_t", [128, HC, 1024], F32, kind="ExternalInput").ap()
    wo_t = nc.dram_tensor("wo_t", [HC, 128, HC, 128], F32, kind="ExternalInput").ap()
    wg_t = nc.dram_tensor("wg_t", [IC, 128, HC, 128], F32, kind="ExternalInput").ap()
    wu_t = nc.dram_tensor("wu_t", [IC, 128, HC, 128], F32, kind="ExternalInput").ap()
    wd_t = nc.dram_tensor("wd_t", [HC, 128, IC, 128], F32, kind="ExternalInput").ap()
    dT_c = nc.dram_tensor("dT_c", [16, 128, S], F32, kind="ExternalInput").ap()
    dneg_c = nc.dram_tensor("dneg_c", [128, NH * 128], BF16, kind="ExternalInput").ap()
    onesM_c = nc.dram_tensor("onesM_c", [128, 128], BF16, kind="ExternalInput").ap()
    mhalfM_c = nc.dram_tensor("mhalfM_c", [128, 128], BF16, kind="ExternalInput").ap()

    q_s = nc.dram_tensor("q_s", [NH, 128, S], BF16, kind=dbg_kind).ap()
    k_s = nc.dram_tensor("k_s", [NKV, 128, S], BF16, kind=dbg_kind).ap()
    v_s = nc.dram_tensor("v_s", [16, 128, 1024], BF16, kind=dbg_kind).ap()
    ao_s = nc.dram_tensor("ao_s", [NH, 128, S], BF16, kind=dbg_kind).ap()
    pj_s = nc.dram_tensor("pj_s", [HC, 128, S], BF16, kind=dbg_kind).ap()
    x1_s = nc.dram_tensor("x1_s", [HC, 128, S], F32, kind=dbg_kind).ap()
    ac_s = nc.dram_tensor("ac_s", [HC, 128, S], F32, kind=dbg_kind).ap()
    y_t = nc.dram_tensor("y_t", [HC, 128, S], F32, kind="ExternalOutput").ap()
    h1_d = h2_d = None
    if debug:
        h1_d = nc.dram_tensor("h1_d", [128, HC * S], BF16, kind="ExternalOutput").ap()
        h2_d = nc.dram_tensor("h2_d", [128, HC * S], BF16, kind="ExternalOutput").ap()

    ctx = contextlib.ExitStack()
    with ctx:
        semnames = ["sW", "sWf", "sSP", "sSPa", "sPL", "sY",
                    "sPE", "sPEn", "sPEs", "sPEp", "sPEr", "sPEd", "sPEg", "sPEu",
                    "sACT", "sACTn", "sACTs", "sACTr", "sACTc", "sACTd",
                    "sDVE", "sDVEb", "sDVEo", "sDVEa",
                    "sWq0", "sWq1", "sSt0", "sSt1", "sAo0", "sAo1",
                    "sXl0", "sXl1", "sXr0", "sXr1",
                    "sWg0", "sWg1", "sWu0", "sWu1", "sWd0", "sWd1",
                    "sAc0", "sAc1", "sXs0", "sXs1"]
        sm = {n: Sem(ctx.enter_context(nc.semaphore(n)), n) for n in semnames}
        sW, sWf, sSP, sSPa, sPL, sY = (sm[n] for n in ("sW", "sWf", "sSP", "sSPa", "sPL", "sY"))
        sPE, sPEn, sPEs, sPEp, sPEr, sPEd, sPEg, sPEu = (
            sm[n] for n in ("sPE", "sPEn", "sPEs", "sPEp", "sPEr", "sPEd", "sPEg", "sPEu"))
        sACT, sACTn, sACTs, sACTr, sACTc, sACTd = (
            sm[n] for n in ("sACT", "sACTn", "sACTs", "sACTr", "sACTc", "sACTd"))
        sDVE, sDVEb, sDVEo, sDVEa = (sm[n] for n in ("sDVE", "sDVEb", "sDVEo", "sDVEa"))
        sWq = [sm["sWq0"], sm["sWq1"]]
        sSt = [sm["sSt0"], sm["sSt1"]]
        sAo = [sm["sAo0"], sm["sAo1"]]
        sXl = [sm["sXl0"], sm["sXl1"]]
        sXr = [sm["sXr0"], sm["sXr1"]]
        sWg = [sm["sWg0"], sm["sWg1"]]
        sWu = [sm["sWu0"], sm["sWu1"]]
        sWd = [sm["sWd0"], sm["sWd1"]]
        sAc = [sm["sAc0"], sm["sAc1"]]
        sXs = [sm["sXs0"], sm["sXs1"]]

        onesM = ctx.enter_context(nc.sbuf_tensor("onesM", [128, 128], BF16))
        mhalfM = ctx.enter_context(nc.sbuf_tensor("mhalfM", [128, 128], BF16))
        cst = ctx.enter_context(nc.sbuf_tensor("cst", [128, 48], F32))
        onescol = onesM[:, 0:1]

        slopes_f = [float(s) for s in _alibi_slopes(NH)]
        cvals = [EPS, SCALE, INV_H, -0.5, -1.0] + slopes_f
        for i, v in enumerate(cvals):
            nc.const_aps.register(F32, v, cst[:, i:i + 1])

        p = Plan()
        for i, v in enumerate(cvals):
            I(p, "gpsimd", lambda g, i=i, v=v: g.memset(cst[:, i:i + 1], v))
        I(p, "sync", lambda s: s.dma_start(onesM[:], onesM_c), inc=(sSP, sSP.take(16)))
        I(p, "sync", lambda s: s.dma_start(mhalfM[:], mhalfM_c), inc=(sSP, sSP.take(16)))
        p.add("sync", lambda eng, v=sSP.n, h=sSP.h: eng.wait_ge(h, v))
        p.emit(nc)

        def ln_rows(p, ps1, ps2, rA, rB, rC, waits):
            I(p, "scalar", lambda a: a.copy(rA, ps1[:]), waits=waits)
            I(p, "scalar", lambda a: a.copy(rB, ps2[:]))
            I(p, "scalar", lambda a: a.mul(rA, rA, INV_H))
            I(p, "scalar", lambda a: a.mul(rB, rB, INV_H), inc=(sACT, sACT.take()))
            se = sACT.n
            I(p, "vector", lambda v: v.tensor_mul(rC, rA, rA), waits=[(sACT, se)])
            I(p, "vector", lambda v: v.tensor_sub(rB, rB, rC), inc=(sDVE, sDVE.take()))
            vd = sDVE.n
            I(p, "scalar", lambda a: a.activation(rB, rB, AF.Ln, bias=EPS),
              waits=[(sDVE, vd)])
            I(p, "scalar", lambda a: a.activation(rB, rB, AF.Exp, scale=-0.5),
              inc=(sACT, sACT.take()))
            return sACT.n

        if n_phases >= 1:
            big_ctx = contextlib.ExitStack()
            with big_ctx:
                big = big_ctx.enter_context(nc.sbuf_tensor("big1", [128, HC * S], BF16))
                with contextlib.ExitStack() as _es:
                    xblk = _es.enter_context(nc.sbuf_tensor("xblk", [128, HC * 512], BF16))
                    sqf = _es.enter_context(nc.sbuf_tensor("sqf", [128, HC * 512], BF16))
                    rows1 = _es.enter_context(nc.sbuf_tensor("rows1", [1, 512 * 3], F32))
                    rowM = _es.enter_context(nc.sbuf_tensor("rowM", [128, 512], BF16))
                    rowR = _es.enter_context(nc.sbuf_tensor("rowR", [128, 512], BF16))
                    Mt = _es.enter_context(nc.sbuf_tensor("Mt", [128, 512], BF16))
                    Rt = _es.enter_context(nc.sbuf_tensor("Rt", [128, 512], BF16))
                    t1b = _es.enter_context(nc.sbuf_tensor("t1b", [128, 512], BF16))
                    ps_st1 = _es.enter_context(nc.psum_tensor("ps_st1", [1, 512], F32))
                    ps_st2 = _es.enter_context(nc.psum_tensor("ps_st2", [1, 512], F32))
                    ps_bM = _es.enter_context(nc.psum_tensor("ps_bM", [128, 512], F32))
                    ps_bR = _es.enter_context(nc.psum_tensor("ps_bR", [128, 512], F32))
                    rA = rows1[:, 0:512]
                    rB = rows1[:, 512:1024]
                    rC = rows1[:, 1024:1536]
                    p = Plan()
                    I(p, "gpsimd", lambda g: g.memset(rowM[:], 0.0))
                    I(p, "gpsimd", lambda g: g.memset(rowR[:], 0.0), inc=(sPL, sPL.take()))
                    pad_ready = sPL.n
                    norm_end = 0
                    rows_evac = 0
                    bcast_done = 0
                    bcast_evac = 0
                    for tb in range(TB):
                        t0 = tb * 512
                        lw = [(sDVE, norm_end)] if tb else []
                        for g4 in range(4):
                            src = x_t[g4 * 8:(g4 + 1) * 8, :, t0:t0 + 512].rearrange(
                                "c p t -> p c t")
                            I(p, "gpsimd",
                              lambda g, d=sp3(xblk[:, g4 * 4096:(g4 + 1) * 4096], 512), s=src:
                              g.dma_start(d, s),
                              waits=lw if g4 == 0 else (), inc=(sW, sW.take(16)))
                        ld = sW.n
                        for c in range(HC):
                            cw = [(sW, ld)] if c == 0 else []
                            xs = xblk[:, c * 512:(c + 1) * 512]
                            sq = sqf[:, c * 512:(c + 1) * 512]
                            inc = (sDVE, sDVE.take()) if c == HC - 1 else None
                            I(p, "vector", lambda v, a=xs, b=sq: v.tensor_mul(b, a, a),
                              waits=cw, inc=inc)
                        sq_done = sDVE.n
                        for c in range(HC):
                            cw = []
                            if c == 0:
                                cw = [(sDVE, sq_done)]
                                if rows_evac:
                                    cw.append((sACT, rows_evac))
                            xs = xblk[:, c * 512:(c + 1) * 512]
                            sq = sqf[:, c * 512:(c + 1) * 512]
                            I(p, "tensor", lambda t, r=xs, c0=(c == 0), c1=(c == HC - 1):
                              t.matmul(ps_st1[:], onescol, r, start=c0, stop=c1), waits=cw)
                            inc = (sPE, sPE.take()) if c == HC - 1 else None
                            I(p, "tensor", lambda t, r=sq, c0=(c == 0), c1=(c == HC - 1):
                              t.matmul(ps_st2[:], onescol, r, start=c0, stop=c1), inc=inc)
                        stats_done = sPE.n
                        rows_evac = ln_rows(p, ps_st1, ps_st2, rA, rB, rC,
                                            [(sPE, stats_done)])
                        rw = [(sACT, rows_evac)] + \
                             ([(sPE, bcast_done)] if tb else [(sPL, pad_ready)])
                        I(p, "scalar", lambda a: a.copy(rowM[0:1, :], rA), waits=rw)
                        I(p, "scalar", lambda a: a.copy(rowR[0:1, :], rB),
                          inc=(sACT, sACT.take()))
                        rows_b = sACT.n
                        bw = [(sACT, rows_b)] + ([(sACT, bcast_evac)] if tb else [])
                        I(p, "tensor", lambda t: t.matmul(ps_bM[:], onesM[:], rowM[:],
                                                          start=True, stop=True), waits=bw)
                        I(p, "tensor", lambda t: t.matmul(ps_bR[:], onesM[:], rowR[:],
                                                          start=True, stop=True),
                          inc=(sPE, sPE.take()))
                        bcast_done = sPE.n
                        ew = [(sPE, bcast_done)] + ([(sDVE, norm_end)] if tb else [])
                        I(p, "scalar", lambda a: a.copy(Mt[:], ps_bM[:]), waits=ew)
                        I(p, "scalar", lambda a: a.copy(Rt[:], ps_bR[:]),
                          inc=(sACT, sACT.take()))
                        bcast_evac = sACT.n
                        for c in range(HC):
                            cw = [(sACT, bcast_evac)] if c == 0 else []
                            xs = xblk[:, c * 512:(c + 1) * 512]
                            dst = big[:, c * S + t0:c * S + t0 + 512]
                            I(p, "vector", lambda v, a=xs: v.tensor_sub(t1b[:], a, Mt[:]),
                              waits=cw)
                            inc = (sDVE, sDVE.take()) if c == HC - 1 else None
                            I(p, "vector", lambda v, d=dst: v.tensor_mul(d, t1b[:], Rt[:]),
                              inc=inc)
                        norm_end = sDVE.n
                    p.emit(nc)

                if n_phases >= 2:
                    with contextlib.ExitStack() as _es:
                        wqb0 = _es.enter_context(nc.sbuf_tensor("wqb0", [128, HC * 128], BF16))
                        wqb1 = _es.enter_context(nc.sbuf_tensor("wqb1", [128, HC * 128], BF16))
                        stg0 = _es.enter_context(nc.sbuf_tensor("stg0", [128, 2048], BF16))
                        stg1 = _es.enter_context(nc.sbuf_tensor("stg1", [128, 2048], BF16))
                        ps_q0 = _es.enter_context(nc.psum_tensor("ps_q0", [128, 512], F32))
                        ps_q1 = _es.enter_context(nc.psum_tensor("ps_q1", [128, 512], F32))
                        ps_q2 = _es.enter_context(nc.psum_tensor("ps_q2", [128, 512], F32))
                        ps_q3 = _es.enter_context(nc.psum_tensor("ps_q3", [128, 512], F32))
                        wqb = [wqb0, wqb1]
                        stg = [stg0, stg1]
                        psq = [ps_q0, ps_q1, ps_q2, ps_q3]
                        p = Plan()
                        wfree = [0, 0]
                        pe_done = {}
                        evac = {}
                        store = {}
                        for oc in range(40):
                            par = oc % 2
                            lw = [(sACT, evac[(oc - 2, TB - 1)])] if oc >= 2 else []
                            src = wqk_t[oc:oc + 1].rearrange("o p c i -> (o p) (c i)")
                            I(p, "gpsimd", lambda g, d=wqb[par][:], s=src: g.dma_start(d, s),
                              waits=lw, inc=(sWq[par], sWq[par].take(16)))
                            wld = (sWq[par], sWq[par].n)
                            for kc in range(HC):
                                for tb in range(TB):
                                    cw = []
                                    if kc == 0:
                                        cw = [wld]
                                        if oc >= 1:
                                            cw.append((sACT, evac[(oc - 1, tb)]))
                                    lhs = wqb[par][:, kc * 128:(kc + 1) * 128]
                                    rhs = big[:, kc * S + tb * 512:kc * S + tb * 512 + 512]
                                    inc = None
                                    if kc == HC - 1:
                                        inc = (sPE, sPE.take())
                                    I(p, "tensor", lambda t, o=psq[tb][:], l=lhs, r=rhs,
                                      c0=(kc == 0), c1=(kc == HC - 1):
                                      t.matmul(o, l, r, start=c0, stop=c1),
                                      waits=cw, inc=inc)
                                    if kc == HC - 1:
                                        pe_done[(oc, tb)] = sPE.n
                            for tb in range(TB):
                                cw = [(sPE, pe_done[(oc, tb)])]
                                if tb == 0 and oc >= 2:
                                    cw.append(store[oc - 2])
                                dst = stg[par][:, tb * 512:(tb + 1) * 512]
                                I(p, "scalar", lambda a, d=dst, s=psq[tb][:]: a.copy(d, s),
                                  waits=cw, inc=(sACT, sACT.take()))
                                evac[(oc, tb)] = sACT.n
                            tgt = (q_s[oc:oc + 1] if oc < NH else
                                   k_s[oc - NH:oc - NH + 1]).rearrange("o p t -> (o p) t")
                            I(p, "sync", lambda s, d=tgt, s2=stg[par][:]: s.dma_start(d, s2),
                              waits=[(sACT, evac[(oc, TB - 1)])],
                              inc=(sSt[par], sSt[par].take(16)))
                            store[oc] = (sSt[par], sSt[par].n)
                        p.emit(nc)

                    with contextlib.ExitStack() as _es:
                        wvb = _es.enter_context(nc.sbuf_tensor("wvb", [128, HC * 1024], BF16))
                        stv0 = _es.enter_context(nc.sbuf_tensor("stv0", [128, 1024], BF16))
                        stv1 = _es.enter_context(nc.sbuf_tensor("stv1", [128, 1024], BF16))
                        ps_vA0 = _es.enter_context(nc.psum_tensor("ps_vA0", [128, 512], F32))
                        ps_vB0 = _es.enter_context(nc.psum_tensor("ps_vB0", [128, 512], F32))
                        ps_vA1 = _es.enter_context(nc.psum_tensor("ps_vA1", [128, 512], F32))
                        ps_vB1 = _es.enter_context(nc.psum_tensor("ps_vB1", [128, 512], F32))
                        stv = [stv0, stv1]
                        psA = [ps_vA0, ps_vA1]
                        psB = [ps_vB0, ps_vB1]
                        p = Plan()
                        I(p, "gpsimd",
                          lambda g: g.dma_start(wvb[:], wv_t.rearrange("p c i -> p (c i)")),
                          inc=(sW, sW.take(16)))
                        wld = sW.n
                        vevac = {}
                        vstore = {}
                        for tt in range(16):
                            par = tt % 2
                            for kc in range(HC):
                                cw = []
                                if kc == 0:
                                    cw = [(sW, wld)]
                                    if tt >= 2:
                                        cw.append((sACT, vevac[tt - 2]))
                                lhs = big[:, kc * S + tt * 128:kc * S + tt * 128 + 128]
                                I(p, "tensor", lambda t, o=psA[par][:], l=lhs,
                                  r=wvb[:, kc * 1024:kc * 1024 + 512],
                                  c0=(kc == 0), c1=(kc == HC - 1):
                                  t.matmul(o, l, r, start=c0, stop=c1), waits=cw)
                                inc = (sPE, sPE.take()) if kc == HC - 1 else None
                                I(p, "tensor", lambda t, o=psB[par][:], l=lhs,
                                  r=wvb[:, kc * 1024 + 512:kc * 1024 + 1024],
                                  c0=(kc == 0), c1=(kc == HC - 1):
                                  t.matmul(o, l, r, start=c0, stop=c1), inc=inc)
                            vdone = sPE.n
                            cw = [(sPE, vdone)]
                            if tt >= 2:
                                cw.append(vstore[tt - 2])
                            I(p, "scalar",
                              lambda a, d=stv[par][:, 0:512], s=psA[par][:]: a.copy(d, s),
                              waits=cw)
                            I(p, "scalar",
                              lambda a, d=stv[par][:, 512:1024], s=psB[par][:]: a.copy(d, s),
                              inc=(sACT, sACT.take()))
                            vevac[tt] = sACT.n
                            dstv = v_s[tt:tt + 1].rearrange("o p t -> (o p) t")
                            I(p, "sync", lambda s, d=dstv, s2=stv[par][:]: s.dma_start(d, s2),
                              waits=[(sACT, vevac[tt])],
                              inc=(sSt[par], sSt[par].take(16)))
                            vstore[tt] = (sSt[par], sSt[par].n)
                        if debug:
                            I(p, "sync", lambda s: s.dma_start(h1_d, big[:]),
                              inc=(sSP, sSP.take(16)))
                            p.add("sync", lambda eng, v=sSP.n, h=sSP.h: eng.wait_ge(h, v))
                        p.emit(nc)

        if n_phases >= 3:
            KTOT = S // 128
            with contextlib.ExitStack() as _es:
                dnegs = _es.enter_context(nc.sbuf_tensor("dnegs", [128, NH * 128], BF16))
                Ksb = _es.enter_context(nc.sbuf_tensor("Ksb", [128, NKV * S], BF16))
                Vsb = _es.enter_context(nc.sbuf_tensor("Vsb", [128, 16 * 1024], BF16))
                Qb = _es.enter_context(nc.sbuf_tensor("Qb", [128, NH * 512], BF16))
                Db = _es.enter_context(nc.sbuf_tensor("Db", [128, KTOT * 512], F32))
                sqq = _es.enter_context(nc.sbuf_tensor("sqq", [128, 512], BF16))
                sqk = _es.enter_context(nc.sbuf_tensor("sqk", [128, S], BF16))
                nkr = _es.enter_context(nc.sbuf_tensor("nkr", [1, S], F32))
                nqr = _es.enter_context(nc.sbuf_tensor("nqr", [1, 512], F32))
                mk2s = _es.enter_context(nc.sbuf_tensor("mk2s", [1, 8], F32))
                rowC = _es.enter_context(nc.sbuf_tensor("rowC", [128, 512], BF16))
                colC = _es.enter_context(nc.sbuf_tensor("colC", [128, 1], BF16))
                bcol = _es.enter_context(nc.sbuf_tensor("bcol", [128, 1], F32))
                st0 = _es.enter_context(nc.sbuf_tensor("st0", [128, 512], F32))
                st1 = _es.enter_context(nc.sbuf_tensor("st1", [128, 512], F32))
                srow = _es.enter_context(nc.sbuf_tensor("srow", [1, 512 * 3], F32))
                recb = _es.enter_context(nc.sbuf_tensor("recb", [128, 512], F32))
                pt0 = _es.enter_context(nc.sbuf_tensor("pt0", [128, 512], BF16))
                pt1 = _es.enter_context(nc.sbuf_tensor("pt1", [128, 512], BF16))
                pt2 = _es.enter_context(nc.sbuf_tensor("pt2", [128, 512], BF16))
                aot0 = _es.enter_context(nc.sbuf_tensor("aot0", [128, 512], BF16))
                aot1 = _es.enter_context(nc.sbuf_tensor("aot1", [128, 512], BF16))
                ps_s0 = _es.enter_context(nc.psum_tensor("ps_s0", [128, 512], F32))
                ps_s1 = _es.enter_context(nc.psum_tensor("ps_s1", [128, 512], F32))
                ps_bc = _es.enter_context(nc.psum_tensor("ps_bc", [128, 8], F32))
                ps_o0 = _es.enter_context(nc.psum_tensor("ps_o0", [128, 512], F32))
                ps_o1 = _es.enter_context(nc.psum_tensor("ps_o1", [128, 512], F32))
                ps_sum = _es.enter_context(nc.psum_tensor("ps_sum", [1, 512], F32))
                ps_nq = _es.enter_context(nc.psum_tensor("ps_nq", [1, 512], F32))
                ps_rb = _es.enter_context(nc.psum_tensor("ps_rb", [128, 512], F32))
                pss = [ps_s0, ps_s1]
                stile = [st0, st1]
                pso = [ps_o0, ps_o1]
                pt = [pt0, pt1, pt2]
                aot = [aot0, aot1]
                sumr = srow[:, 0:512]
                lnr = srow[:, 512:1024]
                p = Plan()
                I(p, "gpsimd", lambda g: g.memset(colC[:], 0.0))
                I(p, "gpsimd", lambda g: g.memset(rowC[:], 0.0), inc=(sPL, sPL.take()))
                pad_ready = sPL.n
                I(p, "sync", lambda s: s.dma_start(dnegs[:], dneg_c),
                  inc=(sSP, sSP.take(16)))
                I(p, "sync", lambda s: s.dma_start(sp3(Ksb[:], S), k_s.rearrange("h p t -> p h t")),
                  inc=(sSP, sSP.take(16)))
                I(p, "sync", lambda s: s.dma_start(sp3(Vsb[:], 1024), v_s.rearrange("h p t -> p h t")),
                  inc=(sSP, sSP.take(16)))
                kv_ld = sSP.n
                nq_evac = 0
                for kv in range(NKV):
                    cw = [(sSP, kv_ld)] if kv == 0 else [(sPEn, sPEn.n)]
                    src = Ksb[:, kv * S:(kv + 1) * S]
                    I(p, "vector", lambda v, a=src: v.tensor_mul(sqk[:], a, a),
                      waits=cw, inc=(sDVE, sDVE.take()))
                    sqk_done = sDVE.n
                    for seg in range(4):
                        cw = [(sDVE, sqk_done)]
                        if nq_evac:
                            cw.append((sACTn, nq_evac))
                        I(p, "tensor", lambda t, r=sqk[:, seg * 512:(seg + 1) * 512]:
                          t.matmul(ps_nq[:], onescol, r, start=True, stop=True),
                          waits=cw, inc=(sPEn, sPEn.take()))
                        I(p, "scalar", lambda a, d=nkr[:, seg * 512:(seg + 1) * 512],
                          w=sPEn.n: a.copy(d, ps_nq[:]),
                          waits=[(sPEn, sPEn.n)], inc=(sACTn, sACTn.take()))
                        nq_evac = sACTn.n
                    I(p, "vector", lambda v, d=mk2s[:, kv:kv + 1]:
                      v.reduce_max(d, nkr[:], axis=AX.X),
                      waits=[(sACTn, nq_evac)], inc=(sDVE, sDVE.take()))

                bank_last_stt = [0, 0]
                stile_last_exp = [0, 0]
                pt_last_pv = [0, 0, 0]
                ao_done = {}
                ao_store = {}
                sum_evac = 0
                rb_mm = 0
                rb_evac = 0
                b2_done = 0
                bias_mm = 0
                sqq_free = 0
                last_scores = 0
                last_stt = 0
                for qb in range(TB):
                    KT = 4 * (qb + 1)
                    cw = [(sPEs, last_scores), (sDVEb, last_stt)] if qb else []
                    for g4 in range(4):
                        src = q_s[g4 * 8:(g4 + 1) * 8, :, qb * 512:(qb + 1) * 512].rearrange(
                            "c p t -> p c t")
                        I(p, "sync",
                          lambda s, d=sp3(Qb[:, g4 * 4096:(g4 + 1) * 4096], 512), s2=src:
                          s.dma_start(d, s2),
                          waits=cw if g4 == 0 else (), inc=(sSP, sSP.take(16)))
                    srcD = dT_c[0:KT, :, qb * 512:(qb + 1) * 512].rearrange("c p t -> p c t")
                    I(p, "sync", lambda s, d=sp3(Db[:, 0:KT * 512], 512), s2=srcD: s.dma_start(d, s2),
                      inc=(sSP, sSP.take(16)))
                    qld = sSP.n
                    for h in range(NH):
                        gidx = qb * NH + h
                        kv = h // G
                        qsl = Qb[:, h * 512:(h + 1) * 512]
                        cw = [(sSP, qld)]
                        if sqq_free:
                            cw.append((sPEn, sqq_free))
                        I(p, "vector", lambda v, a=qsl: v.tensor_mul(sqq[:], a, a),
                          waits=cw, inc=(sDVE, sDVE.take()))
                        sqq_done = sDVE.n
                        I(p, "tensor", lambda t: t.matmul(ps_nq[:], onescol, sqq[:],
                                                          start=True, stop=True),
                          waits=[(sDVE, sqq_done), (sACTn, nq_evac)],
                          inc=(sPEn, sPEn.take()))
                        sqq_free = sPEn.n
                        cw = [(sPEn, sqq_free)]
                        if b2_done:
                            cw.append((sDVEb, b2_done))
                        I(p, "scalar", lambda a: a.copy(nqr[:], ps_nq[:]),
                          waits=cw, inc=(sACTn, sACTn.take()))
                        nq_evac = sACTn.n
                        mx = srow[:, 1024:1025]
                        I(p, "vector", lambda v: v.reduce_max(mx, nqr[:], axis=AX.X),
                          waits=[(sACTn, nq_evac), (sPL, pad_ready),
                                 (sPEn, bias_mm)])
                        I(p, "vector", lambda v, sc=mk2s[:, kv:kv + 1]:
                          v.tensor_scalar(colC[0:1, 0:1], mx, sc, -0.5 * SCALE,
                                          op0=ALU.add, op1=ALU.mult),
                          inc=(sDVEb, sDVEb.take()))
                        b2_done = sDVEb.n
                        I(p, "tensor", lambda t: t.matmul(ps_bc[:, 0:1], onesM[:],
                                                          colC[:], start=True, stop=True),
                          waits=[(sDVEb, b2_done), (sACTn, nq_evac)],
                          inc=(sPEn, sPEn.take()))
                        bias_mm = sPEn.n
                        I(p, "scalar", lambda a: a.copy(bcol[:], ps_bc[:, 0:1]),
                          waits=[(sPEn, bias_mm)], inc=(sACTn, sACTn.take()))
                        nq_evac = sACTn.n
                        po = h % 2
                        sc_done_t = {}
                        stt_done_t = {}
                        exp_done_t = {}
                        slope_h = slopes_f[h]

                        def em_scores(kt, qsl=qsl, kv=kv, h=h, first=(h == 0)):
                            sb = kt % 2
                            cw = []
                            if kt == 0 and first:
                                cw.append((sSP, qld))
                            if bank_last_stt[sb]:
                                cw.append((sDVEb, bank_last_stt[sb]))
                            lhsK = Ksb[:, kv * S + kt * 128:kv * S + kt * 128 + 128]
                            I(p, "tensor", lambda t, o=pss[sb][:], l=lhsK, r=qsl:
                              t.matmul(o, l, r, start=True, stop=True),
                              waits=cw, inc=(sPEs, sPEs.take()))
                            sc_done_t[kt] = sPEs.n

                        def em_stt(kt, sl=slope_h):
                            sb = kt % 2
                            cw = [(sPEs, sc_done_t[kt])]
                            if stile_last_exp[sb]:
                                cw.append((sACTs, stile_last_exp[sb]))
                            I(p, "vector", lambda v, d=stile[sb][:], a=pss[sb][:],
                              b=Db[:, kt * 512:(kt + 1) * 512], sc=SCALE / sl:
                              v.scalar_tensor_tensor(d, a, sc, b, op0=ALU.mult,
                                                     op1=ALU.subtract),
                              waits=cw, inc=(sDVEb, sDVEb.take()))
                            bank_last_stt[sb] = sDVEb.n
                            stt_done_t[kt] = sDVEb.n

                        def em_exp(kt, sl=slope_h):
                            sb = kt % 2
                            pb = kt % 3
                            cw = [(sDVEb, stt_done_t[kt])]
                            if pt_last_pv[pb]:
                                cw.append((sPEp, pt_last_pv[pb]))
                            I(p, "scalar", lambda a, d=pt[pb][:], s=stile[sb][:], sc=sl:
                              a.activation(d, s, AF.Exp, scale=sc, bias=bcol[:]),
                              waits=cw, inc=(sACTs, sACTs.take()))
                            stile_last_exp[sb] = sACTs.n
                            exp_done_t[kt] = sACTs.n

                        def em_pv(kt, KT=KT, kv=kv, po=po, gidx=gidx,
                                  sum_evac=sum_evac):
                            sb = kt % 3
                            cw = [(sACTs, exp_done_t[kt])]
                            if kt == 0:
                                cw.append((sDVEo, ao_done.get(gidx - 2, 0)))
                                cw.append((sACTr, sum_evac))
                            lhsV = Vsb[:, kt * 1024 + kv * 128:kt * 1024 + kv * 128 + 128]
                            I(p, "tensor", lambda t, o=pso[po][:], l=lhsV, r=pt[sb][:],
                              c0=(kt == 0), c1=(kt == KT - 1):
                              t.matmul(o, l, r, start=c0, stop=c1), waits=cw)
                            I(p, "tensor", lambda t, r=pt[sb][:],
                              c0=(kt == 0), c1=(kt == KT - 1):
                              t.matmul(ps_sum[:], onescol, r, start=c0, stop=c1),
                              inc=(sPEp, sPEp.take()))
                            pt_last_pv[sb] = sPEp.n

                        em_scores(0)
                        em_stt(0)
                        em_exp(0)
                        em_scores(1)
                        em_stt(1)
                        em_exp(1)
                        for kt in range(2, KT):
                            em_scores(kt)
                            em_stt(kt)
                            em_exp(kt)
                            em_pv(kt - 2)
                        em_pv(KT - 2)
                        em_pv(KT - 1)
                        last_scores = sPEs.n
                        last_stt = sDVEb.n
                        pv_all = sPEp.n
                        I(p, "scalar", lambda a: a.copy(sumr, ps_sum[:]),
                          waits=[(sPEp, pv_all), (sPEr, rb_mm)])
                        I(p, "scalar", lambda a: a.activation(lnr, sumr, AF.Ln))
                        I(p, "scalar", lambda a: a.activation(lnr, lnr, AF.Exp, scale=-1.0))
                        I(p, "scalar", lambda a: a.copy(rowC[0:1, :], lnr),
                          inc=(sACTr, sACTr.take()))
                        sum_evac = sACTr.n
                        I(p, "tensor", lambda t: t.matmul(ps_rb[:], onesM[:], rowC[:],
                                                          start=True, stop=True),
                          waits=[(sACTr, sum_evac), (sACTc, rb_evac)],
                          inc=(sPEr, sPEr.take()))
                        rb_mm = sPEr.n
                        I(p, "scalar", lambda a: a.copy(recb[:], ps_rb[:]),
                          waits=[(sPEr, rb_mm), (sDVEo, ao_done.get(gidx - 1, 0))],
                          inc=(sACTc, sACTc.take()))
                        rb_evac = sACTc.n
                        po = h % 2
                        cwv = [(sACTc, rb_evac)]
                        if gidx >= 2:
                            cwv.append(ao_store[gidx - 2])
                        I(p, "vector", lambda v, d=aot[po][:], s=pso[po][:]:
                          v.tensor_mul(d, s, recb[:]),
                          waits=cwv, inc=(sDVEo, sDVEo.take()))
                        ao_done[gidx] = sDVEo.n
                        dsta = ao_s[h:h + 1, :, qb * 512:(qb + 1) * 512].rearrange(
                            "o p t -> (o p) t")
                        I(p, "sync", lambda s, d=dsta, s2=aot[po][:]: s.dma_start(d, s2),
                          waits=[(sDVEo, ao_done[gidx])],
                          inc=(sAo[po], sAo[po].take(16)))
                        ao_store[gidx] = (sAo[po], sAo[po].n)
                p.emit(nc)

        if n_phases >= 4:
            big_ctx = contextlib.ExitStack()
            with big_ctx:
                big = big_ctx.enter_context(nc.sbuf_tensor("big2", [128, HC * S], BF16))
                with contextlib.ExitStack() as _es:
                    wob0 = _es.enter_context(nc.sbuf_tensor("wob0", [128, HC * 128], BF16))
                    wob1 = _es.enter_context(nc.sbuf_tensor("wob1", [128, HC * 128], BF16))
                    stg0b = _es.enter_context(nc.sbuf_tensor("stg0b", [128, 2048], BF16))
                    stg1b = _es.enter_context(nc.sbuf_tensor("stg1b", [128, 2048], BF16))
                    ps_p0 = _es.enter_context(nc.psum_tensor("ps_p0", [128, 512], F32))
                    ps_p1 = _es.enter_context(nc.psum_tensor("ps_p1", [128, 512], F32))
                    ps_p2 = _es.enter_context(nc.psum_tensor("ps_p2", [128, 512], F32))
                    ps_p3 = _es.enter_context(nc.psum_tensor("ps_p3", [128, 512], F32))
                    wob = [wob0, wob1]
                    stg = [stg0b, stg1b]
                    psq = [ps_p0, ps_p1, ps_p2, ps_p3]
                    p = Plan()
                    for g4 in range(4):
                        src = ao_s[g4 * 8:(g4 + 1) * 8, :, :].rearrange("c p t -> p c t")
                        I(p, "sync", lambda s, d=sp3(big[:, g4 * 8 * S:(g4 + 1) * 8 * S], S), s2=src:
                          s.dma_start(d, s2), inc=(sSP, sSP.take(16)))
                    ao_ld = sSP.n
                    wfree = [0, 0]
                    pe_done = {}
                    evac = {}
                    store = {}
                    for oc in range(HC):
                        par = oc % 2
                        lw = [(sACT, evac[(oc - 2, TB - 1)])] if oc >= 2 else []
                        src = wo_t[oc:oc + 1].rearrange("o p c i -> (o p) (c i)")
                        I(p, "gpsimd", lambda g, d=wob[par][:], s=src: g.dma_start(d, s),
                          waits=lw, inc=(sWq[par], sWq[par].take(16)))
                        wld = (sWq[par], sWq[par].n)
                        for kc in range(HC):
                            for tb in range(TB):
                                cw = []
                                if kc == 0:
                                    cw = [wld, (sSP, ao_ld)]
                                    if oc >= 1:
                                        cw.append((sACT, evac[(oc - 1, tb)]))
                                lhs = wob[par][:, kc * 128:(kc + 1) * 128]
                                rhs = big[:, kc * S + tb * 512:kc * S + tb * 512 + 512]
                                inc = None
                                if kc == HC - 1:
                                    inc = (sPE, sPE.take())
                                I(p, "tensor", lambda t, o=psq[tb][:], l=lhs, r=rhs,
                                  c0=(kc == 0), c1=(kc == HC - 1):
                                  t.matmul(o, l, r, start=c0, stop=c1),
                                  waits=cw, inc=inc)
                                if kc == HC - 1:
                                    pe_done[(oc, tb)] = sPE.n
                        for tb in range(TB):
                            cw = [(sPE, pe_done[(oc, tb)])]
                            if tb == 0 and oc >= 2:
                                cw.append(store[oc - 2])
                            dst = stg[par][:, tb * 512:(tb + 1) * 512]
                            I(p, "scalar", lambda a, d=dst, s=psq[tb][:]: a.copy(d, s),
                              waits=cw, inc=(sACT, sACT.take()))
                            evac[(oc, tb)] = sACT.n
                        tgt = pj_s[oc:oc + 1].rearrange("o p t -> (o p) t")
                        I(p, "sync", lambda s, d=tgt, s2=stg[par][:]: s.dma_start(d, s2),
                          waits=[(sACT, evac[(oc, TB - 1)])],
                          inc=(sSt[par], sSt[par].take(16)))
                        store[oc] = (sSt[par], sSt[par].n)
                    p.emit(nc)

                if n_phases >= 5:
                    with contextlib.ExitStack() as _es:
                        pblock = _es.enter_context(nc.sbuf_tensor("pblock", [128, HC * 512], BF16))
                        xst0 = _es.enter_context(nc.sbuf_tensor("xst0", [128, 512], F32))
                        xst1 = _es.enter_context(nc.sbuf_tensor("xst1", [128, 512], F32))
                        x1c0 = _es.enter_context(nc.sbuf_tensor("x1c0", [128, 512], F32))
                        x1c1 = _es.enter_context(nc.sbuf_tensor("x1c1", [128, 512], F32))
                        sq3a = _es.enter_context(nc.sbuf_tensor("sq3a", [128, 512], BF16))
                        sq3b = _es.enter_context(nc.sbuf_tensor("sq3b", [128, 512], BF16))
                        t1c = _es.enter_context(nc.sbuf_tensor("t1c", [128, 512], BF16))
                        t2c = _es.enter_context(nc.sbuf_tensor("t2c", [128, 512], F32))
                        rows3 = _es.enter_context(nc.sbuf_tensor("rows3", [1, 512 * 3], F32))
                        rowM3 = _es.enter_context(nc.sbuf_tensor("rowM3", [128, 512], BF16))
                        rowR3 = _es.enter_context(nc.sbuf_tensor("rowR3", [128, 512], BF16))
                        Ma = _es.enter_context(nc.sbuf_tensor("Ma", [128, 512], BF16))
                        Ra = _es.enter_context(nc.sbuf_tensor("Ra", [128, 512], BF16))
                        M2 = _es.enter_context(nc.sbuf_tensor("M2", [128, 512], BF16))
                        R2 = _es.enter_context(nc.sbuf_tensor("R2", [128, 512], BF16))
                        ps_t1 = _es.enter_context(nc.psum_tensor("ps_t1", [1, 512], F32))
                        ps_t2 = _es.enter_context(nc.psum_tensor("ps_t2", [1, 512], F32))
                        ps_bM2 = _es.enter_context(nc.psum_tensor("ps_bM2", [128, 512], F32))
                        ps_bR2 = _es.enter_context(nc.psum_tensor("ps_bR2", [128, 512], F32))
                        sq3 = [sq3a, sq3b]
                        xst = [xst0, xst1]
                        x1c = [x1c0, x1c1]
                        rA = rows3[:, 0:512]
                        rB = rows3[:, 512:1024]
                        rC = rows3[:, 1024:1536]
                        p = Plan()
                        I(p, "gpsimd", lambda g: g.memset(rowM3[:], 0.0))
                        I(p, "gpsimd", lambda g: g.memset(rowR3[:], 0.0),
                          inc=(sPL, sPL.take()))
                        pad_ready = sPL.n
                        rows_evac = 0
                        h2_done = 0
                        bc_mm = 0
                        bc_evac = 0
                        x1m_prev = 0
                        x1done = {}
                        x1_store = {}
                        cast_done = {}
                        h2done = {}
                        for tb in range(TB):
                            t0 = tb * 512
                            lw = [(sDVE, h2_done)] if tb else []
                            for g4 in range(4):
                                src = pj_s[g4 * 8:(g4 + 1) * 8, :, t0:t0 + 512].rearrange(
                                    "c p t -> p c t")
                                I(p, "sync",
                                  lambda s, d=sp3(pblock[:, g4 * 4096:(g4 + 1) * 4096], 512), s2=src:
                                  s.dma_start(d, s2),
                                  waits=lw if g4 == 0 else (), inc=(sSP, sSP.take(16)))
                            blk_ld = sSP.n
                            pe_c = {}
                            for c in range(HC):
                                cw = [(sSP, blk_ld)] if c == 0 else []
                                if c >= 2:
                                    cw.append((sPE, pe_c[c - 2]))
                                psl = pblock[:, c * 512:(c + 1) * 512]
                                I(p, "vector",
                                  lambda v, a=psl, d=sq3[c % 2][:]: v.tensor_mul(d, a, a),
                                  waits=cw, inc=(sDVE, sDVE.take()))
                                sqd = sDVE.n
                                cw = [(sDVE, sqd)]
                                if c == 0 and rows_evac:
                                    cw.append((sACT, rows_evac))
                                I(p, "tensor", lambda t, r=psl,
                                  c0=(c == 0), c1=(c == HC - 1):
                                  t.matmul(ps_t1[:], onescol, r, start=c0, stop=c1),
                                  waits=cw)
                                I(p, "tensor", lambda t, r=sq3[c % 2][:],
                                  c0=(c == 0), c1=(c == HC - 1):
                                  t.matmul(ps_t2[:], onescol, r, start=c0, stop=c1),
                                  inc=(sPE, sPE.take()))
                                pe_c[c] = sPE.n
                            stats_done = sPE.n
                            rows_evac = ln_rows(p, ps_t1, ps_t2, rA, rB, rC,
                                                [(sPE, stats_done)])
                            rw = [(sACT, rows_evac)] + \
                                 ([(sPE, bc_mm)] if tb else [(sPL, pad_ready)])
                            I(p, "scalar", lambda a: a.copy(rowM3[0:1, :], rA), waits=rw)
                            I(p, "scalar", lambda a: a.copy(rowR3[0:1, :], rB),
                              inc=(sACT, sACT.take()))
                            rows_b = sACT.n
                            bw = [(sACT, rows_b)] + ([(sACT, bc_evac)] if tb else [])
                            I(p, "tensor", lambda t: t.matmul(
                                ps_bM2[:], onesM[:], rowM3[:], start=True, stop=True),
                              waits=bw)
                            I(p, "tensor", lambda t: t.matmul(
                                ps_bR2[:], onesM[:], rowR3[:], start=True, stop=True),
                              inc=(sPE, sPE.take()))
                            bc_mm = sPE.n
                            ew = [(sPE, bc_mm)] + ([(sDVE, x1m_prev)] if tb else [])
                            I(p, "scalar", lambda a: a.copy(Ma[:], ps_bM2[:]), waits=ew)
                            I(p, "scalar", lambda a: a.copy(Ra[:], ps_bR2[:]),
                              inc=(sACT, sACT.take()))
                            bc_evac = sACT.n
                            pe_c2 = {}
                            for c in range(HC):
                                gsl = tb * HC + c
                                j = c % 2
                                srcx = x_t[c:c + 1, :, t0:t0 + 512].rearrange(
                                    "o p t -> (o p) t")
                                lwx = []
                                if gsl >= 2:
                                    lwx.append((sDVE, x1done[gsl - 2]))
                                I(p, "sync", lambda s, d=xst[j][:], s2=srcx:
                                  s.dma_start(d, s2),
                                  waits=lwx, inc=(sXl[j], sXl[j].take(16)))
                                xld = (sXl[j], sXl[j].n)
                                psl = pblock[:, c * 512:(c + 1) * 512]
                                cw = [(sACT, bc_evac)] if c == 0 else []
                                I(p, "vector", lambda v, a=psl: v.tensor_sub(t1c[:], a, Ma[:]),
                                  waits=cw)
                                I(p, "vector", lambda v: v.tensor_mul(t2c[:], t1c[:], Ra[:]))
                                cw = [xld]
                                if gsl >= 2:
                                    cw.append(x1_store[gsl - 2])
                                    cw.append((sACT, cast_done[gsl - 2]))
                                I(p, "vector", lambda v, x=xst[j][:], d=x1c[j][:]:
                                  v.tensor_add(d, t2c[:], x),
                                  waits=cw, inc=(sDVE, sDVE.take()))
                                x1done[gsl] = sDVE.n
                                dstx1 = x1_s[c:c + 1, :, t0:t0 + 512].rearrange(
                                    "o p t -> (o p) t")
                                I(p, "gpsimd", lambda g, d=dstx1, s=x1c[j][:]:
                                  g.dma_start(d, s),
                                  waits=[(sDVE, x1done[gsl])],
                                  inc=(sXs[j], sXs[j].take(16)))
                                x1_store[gsl] = (sXs[j], sXs[j].n)
                                cwc = [(sDVE, x1done[gsl])]
                                if c >= 1:
                                    cwc.append((sPE, pe_c2[c - 1]))
                                I(p, "scalar", lambda a, s=x1c[j][:], d=psl: a.copy(d, s),
                                  waits=cwc, inc=(sACT, sACT.take()))
                                cast_done[gsl] = sACT.n
                                cd = sACT.n
                                I(p, "vector", lambda v, d=sq3[j][:], a=psl:
                                  v.tensor_mul(d, a, a),
                                  waits=[(sACT, cd)], inc=(sDVE, sDVE.take()))
                                sqd = sDVE.n
                                I(p, "tensor", lambda t, r=psl, c0=(c == 0), c1=(c == HC - 1):
                                  t.matmul(ps_t1[:], onescol, r, start=c0, stop=c1),
                                  waits=[(sDVE, sqd)])
                                I(p, "tensor", lambda t, r=sq3[j][:],
                                  c0=(c == 0), c1=(c == HC - 1):
                                  t.matmul(ps_t2[:], onescol, r, start=c0, stop=c1),
                                  inc=(sPE, sPE.take()))
                                pe_c2[c] = sPE.n
                            x1m_prev = sDVE.n
                            stats2_done = sPE.n
                            rows_evac = ln_rows(p, ps_t1, ps_t2, rA, rB, rC,
                                                [(sPE, stats2_done)])
                            rw = [(sACT, rows_evac), (sPE, bc_mm)]
                            I(p, "scalar", lambda a: a.copy(rowM3[0:1, :], rA), waits=rw)
                            I(p, "scalar", lambda a: a.copy(rowR3[0:1, :], rB),
                              inc=(sACT, sACT.take()))
                            rows_b = sACT.n
                            bw = [(sACT, rows_b), (sACT, bc_evac)]
                            I(p, "tensor", lambda t: t.matmul(
                                ps_bM2[:], onesM[:], rowM3[:], start=True, stop=True),
                              waits=bw)
                            I(p, "tensor", lambda t: t.matmul(
                                ps_bR2[:], onesM[:], rowR3[:], start=True, stop=True),
                              inc=(sPE, sPE.take()))
                            bc_mm = sPE.n
                            ew = [(sPE, bc_mm)] + ([(sDVE, h2_done)] if tb else [])
                            I(p, "scalar", lambda a: a.copy(M2[:], ps_bM2[:]), waits=ew)
                            I(p, "scalar", lambda a: a.copy(R2[:], ps_bR2[:]),
                              inc=(sACT, sACT.take()))
                            bc_evac = sACT.n
                            for c in range(HC):
                                gsl = tb * HC + c
                                psl = pblock[:, c * 512:(c + 1) * 512]
                                cw = [(sACT, bc_evac)] if c == 0 else []
                                dst = big[:, c * S + t0:c * S + t0 + 512]
                                I(p, "vector", lambda v, a=psl:
                                  v.tensor_sub(t1c[:], a, M2[:]), waits=cw)
                                I(p, "vector", lambda v, d=dst: v.tensor_mul(d, t1c[:], R2[:]),
                                  inc=(sDVE, sDVE.take()))
                                h2done[gsl] = sDVE.n
                            h2_done = sDVE.n
                        if debug:
                            I(p, "sync", lambda s: s.dma_start(h2_d, big[:]),
                              waits=[(sDVE, h2_done)], inc=(sSP, sSP.take(16)))
                            p.add("sync", lambda eng, v=sSP.n, h=sSP.h: eng.wait_ge(h, v))
                        p.emit(nc)

                if n_phases >= 6:
                    with contextlib.ExitStack() as _es:
                        bact = _es.enter_context(nc.sbuf_tensor("bact", [128, 8 * S], BF16))
                        wgb0 = _es.enter_context(nc.sbuf_tensor("wgb0", [128, 16 * 128], BF16))
                        wgb1 = _es.enter_context(nc.sbuf_tensor("wgb1", [128, 16 * 128], BF16))
                        wub0 = _es.enter_context(nc.sbuf_tensor("wub0", [128, 16 * 128], BF16))
                        wub1 = _es.enter_context(nc.sbuf_tensor("wub1", [128, 16 * 128], BF16))
                        wdf0 = _es.enter_context(nc.sbuf_tensor("wdf0", [128, 8 * 128], F32))
                        wdf1 = _es.enter_context(nc.sbuf_tensor("wdf1", [128, 8 * 128], F32))
                        wdb0 = _es.enter_context(nc.sbuf_tensor("wdb0", [128, 8 * 128], BF16))
                        wdb1 = _es.enter_context(nc.sbuf_tensor("wdb1", [128, 8 * 128], BF16))
                        s1a = _es.enter_context(nc.sbuf_tensor("s1a", [128, 512], BF16))
                        s1b = _es.enter_context(nc.sbuf_tensor("s1b", [128, 512], BF16))
                        pst0 = _es.enter_context(nc.sbuf_tensor("pst0", [128, 1024], F32))
                        pst1 = _es.enter_context(nc.sbuf_tensor("pst1", [128, 1024], F32))
                        ps_g0 = _es.enter_context(nc.psum_tensor("ps_g0", [128, 512], F32))
                        ps_g1 = _es.enter_context(nc.psum_tensor("ps_g1", [128, 512], F32))
                        ps_g2 = _es.enter_context(nc.psum_tensor("ps_g2", [128, 512], F32))
                        ps_g3 = _es.enter_context(nc.psum_tensor("ps_g3", [128, 512], F32))
                        ps_u0 = _es.enter_context(nc.psum_tensor("ps_u0", [128, 512], F32))
                        ps_u1 = _es.enter_context(nc.psum_tensor("ps_u1", [128, 512], F32))
                        ps_u2 = _es.enter_context(nc.psum_tensor("ps_u2", [128, 512], F32))
                        ps_u3 = _es.enter_context(nc.psum_tensor("ps_u3", [128, 512], F32))
                        wgb = [wgb0, wgb1]
                        wub = [wub0, wub1]
                        wdf = [wdf0, wdf1]
                        wdb = [wdb0, wdb1]
                        s1 = [s1a, s1b]
                        pst = [pst0, pst1]
                        gb = [ps_g0, ps_g1, ps_g2, ps_g3]
                        ub = [ps_u0, ps_u1, ps_u2, ps_u3]
                        dbanks = [gb, ub]
                        p = Plan()
                        silu_cnt = {}
                        act_cnt = {}
                        g_half = {}
                        u_half = {}
                        wgld = {}
                        wuld = {}
                        pd_evac = {}
                        cast_cnt = {}
                        wdb_free = {}
                        pst_free = [0, 0]
                        pd_last_super = 0
                        pd_evac_tail = 0
                        for isup, sup in enumerate(SUPERS):
                            nIc = len(sup)
                            for icl, Ic in enumerate(sup):
                                for half in range(2):
                                    u = Ic * 2 + half
                                    par = u % 2
                                    ksl = slice(half * 16, half * 16 + 16)
                                    lw = [(sPEg, g_half[u - 2])] if u >= 2 else []
                                    I(p, "gpsimd", lambda g, d=wgb[par][:],
                                      s=wg_t[Ic:Ic + 1, :, ksl, :].rearrange(
                                          "o p c i -> (o p) (c i)"):
                                      g.dma_start(d, s), waits=lw,
                                      inc=(sWg[par], sWg[par].take(16)))
                                    wgld[u] = (sWg[par], sWg[par].n)
                                    lw = [(sPEu, u_half[u - 2])] if u >= 2 else []
                                    I(p, "gpsimd", lambda g, d=wub[par][:],
                                      s=wu_t[Ic:Ic + 1, :, ksl, :].rearrange(
                                          "o p c i -> (o p) (c i)"):
                                      g.dma_start(d, s), waits=lw,
                                      inc=(sWu[par], sWu[par].take(16)))
                                    wuld[u] = (sWu[par], sWu[par].n)
                                for kc in range(HC):
                                    u = Ic * 2 + kc // 16
                                    lhs = wgb[u % 2][:, (kc % 16) * 128:(kc % 16) * 128 + 128]
                                    for tb in range(TB):
                                        cw = []
                                        if kc % 16 == 0 and tb == 0:
                                            cw.append(wgld[u])
                                        if kc == 0:
                                            prev = silu_cnt.get((Ic - 1, tb), 0)
                                            if prev:
                                                cw.append((sACTs, prev))
                                            if icl == 0 and isup >= 1 and tb == 0:
                                                cw.append((sACTd, pd_evac_tail))
                                        rhs = big[:, kc * S + tb * 512:kc * S + tb * 512 + 512]
                                        inc = None
                                        if (kc == 15 and tb == TB - 1) or kc == HC - 1:
                                            inc = (sPEg, sPEg.take())
                                        I(p, "tensor", lambda t, o=gb[tb][:], l=lhs, r=rhs,
                                          c0=(kc == 0), c1=(kc == HC - 1):
                                          t.matmul(o, l, r, start=c0, stop=c1),
                                          waits=cw, inc=inc)
                                        if kc == 15 and tb == TB - 1:
                                            g_half[Ic * 2] = sPEg.n
                                        if kc == HC - 1:
                                            if tb == TB - 1:
                                                g_half[Ic * 2 + 1] = sPEg.n
                                            silu_cnt[("pe", Ic, tb)] = sPEg.n
                                for kc in range(HC):
                                    u = Ic * 2 + kc // 16
                                    lhs = wub[u % 2][:, (kc % 16) * 128:(kc % 16) * 128 + 128]
                                    for tb in range(TB):
                                        cw = []
                                        if kc % 16 == 0 and tb == 0:
                                            cw.append(wuld[u])
                                        if kc == 0:
                                            prev = act_cnt.get((Ic - 1, tb), 0)
                                            if prev:
                                                cw.append((sDVEa, prev))
                                            if icl == 0 and isup >= 1 and tb == 0:
                                                cw.append((sACTd, pd_evac_tail))
                                        rhs = big[:, kc * S + tb * 512:kc * S + tb * 512 + 512]
                                        inc = None
                                        if (kc == 15 and tb == TB - 1) or kc == HC - 1:
                                            inc = (sPEu, sPEu.take())
                                        I(p, "tensor", lambda t, o=ub[tb][:], l=lhs, r=rhs,
                                          c0=(kc == 0), c1=(kc == HC - 1):
                                          t.matmul(o, l, r, start=c0, stop=c1),
                                          waits=cw, inc=inc)
                                        if kc == 15 and tb == TB - 1:
                                            u_half[Ic * 2] = sPEu.n
                                        if kc == HC - 1:
                                            if tb == TB - 1:
                                                u_half[Ic * 2 + 1] = sPEu.n
                                            act_cnt[("pe", Ic, tb)] = sPEu.n
                                for tb in range(TB):
                                    js = tb % 2
                                    cw = [(sPEg, silu_cnt[("pe", Ic, tb)])]
                                    prevd = act_cnt.get((Ic, tb - 2) if tb >= 2 else
                                                        (Ic - 1, tb + 2), 0)
                                    if prevd:
                                        cw.append((sDVEa, prevd))
                                    I(p, "scalar", lambda a, d=s1[js][:], s=gb[tb][:]:
                                      a.activation(d, s, AF.Silu),
                                      waits=cw, inc=(sACTs, sACTs.take()))
                                    silu_cnt[(Ic, tb)] = sACTs.n
                                    cw = [(sACTs, silu_cnt[(Ic, tb)]),
                                          (sPEu, act_cnt[("pe", Ic, tb)])]
                                    if isup >= 1:
                                        cw.append((sPEd, pd_last_super))
                                    dst = bact[:, icl * S + tb * 512:icl * S + tb * 512 + 512]
                                    I(p, "vector", lambda v, d=dst, a=s1[js][:], b=ub[tb][:]:
                                      v.tensor_mul(d, a, b),
                                      waits=cw, inc=(sDVEa, sDVEa.take()))
                                    act_cnt[(Ic, tb)] = sDVEa.n
                            act_all = sDVEa.n

                            def em_wd(hc, isup=isup, sup=sup, nIc=nIc):
                                par = hc % 2
                                gw = isup * HC + hc
                                lw = [(sACTn, cast_cnt[gw - 2])] if gw >= 2 else []
                                src = wd_t[hc:hc + 1, :, sup[0]:sup[0] + nIc, :].rearrange(
                                    "o p c i -> (o p) (c i)")
                                I(p, "sync", lambda s, d=wdf[par][:, 0:nIc * 128], s2=src:
                                  s.dma_start(d, s2), waits=lw,
                                  inc=(sWd[par], sWd[par].take(16)))
                                wld_d = (sWd[par], sWd[par].n)
                                cw = [wld_d]
                                if gw >= 2:
                                    cw.append((sPEd, wdb_free[gw - 2]))
                                I(p, "scalar", lambda a, d=wdb[par][:, 0:nIc * 128],
                                  s=wdf[par][:, 0:nIc * 128]: a.copy(d, s),
                                  waits=cw, inc=(sACTn, sACTn.take()))
                                cast_cnt[gw] = sACTn.n

                            em_wd(0)
                            for hc in range(HC):
                                par = hc % 2
                                gw = isup * HC + hc
                                if hc + 1 < HC:
                                    em_wd(hc + 1)
                                banks = dbanks[hc % 2]
                                d_done = {}
                                for icl in range(nIc):
                                    lhs = wdb[par][:, icl * 128:(icl + 1) * 128]
                                    for tb in range(TB):
                                        dslot = gw * TB + tb
                                        cw = []
                                        if icl == 0:
                                            if tb == 0:
                                                cw.append((sACTn, cast_cnt[gw]))
                                                cw.append((sDVEa, act_all))
                                            prev = pd_evac.get(dslot - 2 * TB, 0)
                                            if prev:
                                                cw.append((sACTd, prev))
                                        rhs = bact[:, icl * S + tb * 512:icl * S + tb * 512 + 512]
                                        inc = None
                                        if icl == nIc - 1:
                                            inc = (sPEd, sPEd.take())
                                        I(p, "tensor", lambda t, o=banks[tb][:], l=lhs, r=rhs,
                                          c0=(icl == 0), c1=(icl == nIc - 1):
                                          t.matmul(o, l, r, start=c0, stop=c1),
                                          waits=cw, inc=inc)
                                        if icl == nIc - 1:
                                            d_done[tb] = sPEd.n
                                wdb_free[gw] = sPEd.n
                                for tb in range(TB):
                                    dslot = gw * TB + tb
                                    pp = (dslot // 2) % 2
                                    cw = [(sPEd, d_done[tb])]
                                    if tb % 2 == 0 and pst_free[pp]:
                                        cw.append(pst_free[pp])
                                    I(p, "scalar",
                                      lambda a, d=pst[pp][:, (tb % 2) * 512:(tb % 2) * 512 + 512],
                                      s=banks[tb][:]: a.copy(d, s),
                                      waits=cw, inc=(sACTd, sACTd.take()))
                                    pd_evac[dslot] = sACTd.n
                                    if tb % 2 == 1:
                                        dsta = ac_s[hc:hc + 1, :,
                                                    (tb // 2) * 1024:(tb // 2) * 1024 + 1024
                                                    ].rearrange("o p t -> (o p) t")
                                        op = ALU.add if isup >= 1 else ALU.bypass
                                        I(p, "gpsimd", lambda g, d=dsta, s=pst[pp][:], o=op:
                                          g.dma_start(d, s, accum_op=o),
                                          waits=[(sACTd, pd_evac[dslot])],
                                          inc=(sAc[pp], sAc[pp].take(16)))
                                        pst_free[pp] = (sAc[pp], sAc[pp].n)
                            pd_last_super = sPEd.n
                            pd_evac_tail = sACTd.n
                        p.add("gpsimd", lambda eng, v=sAc[0].n, h=sAc[0].h: eng.wait_ge(h, v))
                        p.add("gpsimd", lambda eng, v=sAc[1].n, h=sAc[1].h: eng.wait_ge(h, v))
                        p.emit(nc)

        if n_phases >= 7:
            with contextlib.ExitStack() as _es:
                outb = _es.enter_context(nc.sbuf_tensor("outb", [128, HC * 512], F32))
                x1b6 = _es.enter_context(nc.sbuf_tensor("x1b6", [128, HC * 512], F32))
                ob16 = _es.enter_context(nc.sbuf_tensor("ob16", [128, 512], BF16))
                sq6a = _es.enter_context(nc.sbuf_tensor("sq6a", [128, 512], BF16))
                sq6b = _es.enter_context(nc.sbuf_tensor("sq6b", [128, 512], BF16))
                t6 = _es.enter_context(nc.sbuf_tensor("t6", [128, 512], F32))
                rows6 = _es.enter_context(nc.sbuf_tensor("rows6", [1, 512 * 3], F32))
                rowM6 = _es.enter_context(nc.sbuf_tensor("rowM6", [128, 512], BF16))
                rowR6 = _es.enter_context(nc.sbuf_tensor("rowR6", [128, 512], BF16))
                M6 = _es.enter_context(nc.sbuf_tensor("M6", [128, 512], F32))
                R6 = _es.enter_context(nc.sbuf_tensor("R6", [128, 512], F32))
                ps_61 = _es.enter_context(nc.psum_tensor("ps_61", [1, 512], F32))
                ps_62 = _es.enter_context(nc.psum_tensor("ps_62", [1, 512], F32))
                ps_6M = _es.enter_context(nc.psum_tensor("ps_6M", [128, 512], F32))
                ps_6R = _es.enter_context(nc.psum_tensor("ps_6R", [128, 512], F32))
                sq6 = [sq6a, sq6b]
                rA = rows6[:, 0:512]
                rB = rows6[:, 512:1024]
                rC = rows6[:, 1024:1536]
                p = Plan()
                I(p, "gpsimd", lambda g: g.memset(rowM6[:], 0.0))
                I(p, "gpsimd", lambda g: g.memset(rowR6[:], 0.0), inc=(sPL, sPL.take()))
                pad_ready = sPL.n
                rows_evac = 0
                y_math = 0
                bc_mm = 0
                bc_evac = 0
                y_store = 0
                for tb in range(TB):
                    t0 = tb * 512
                    lw = [(sY, y_store)] if tb else []
                    for g4 in range(4):
                        src = ac_s[g4 * 8:(g4 + 1) * 8, :, t0:t0 + 512].rearrange(
                            "c p t -> p c t")
                        I(p, "sync",
                          lambda s, d=sp3(outb[:, g4 * 4096:(g4 + 1) * 4096], 512), s2=src:
                          s.dma_start(d, s2),
                          waits=lw if g4 == 0 else (), inc=(sSP, sSP.take(16)))
                        src2 = x1_s[g4 * 8:(g4 + 1) * 8, :, t0:t0 + 512].rearrange(
                            "c p t -> p c t")
                        I(p, "sync",
                          lambda s, d=sp3(x1b6[:, g4 * 4096:(g4 + 1) * 4096], 512), s2=src2:
                          s.dma_start(d, s2), inc=(sSP, sSP.take(16)))
                    blk_ld = sSP.n
                    pe_c = {}
                    dve_c = {}
                    for c in range(HC):
                        osl = outb[:, c * 512:(c + 1) * 512]
                        cw = [(sSP, blk_ld)] if c == 0 else \
                            [(sDVE, dve_c[c - 1]), (sPE, pe_c[c - 1])]
                        I(p, "scalar", lambda a, s=osl: a.copy(ob16[:], s),
                          waits=cw, inc=(sACT, sACT.take()))
                        cd = sACT.n
                        I(p, "vector", lambda v, d=sq6[c % 2][:]:
                          v.tensor_mul(d, ob16[:], ob16[:]),
                          waits=[(sACT, cd)], inc=(sDVE, sDVE.take()))
                        dve_c[c] = sDVE.n
                        cw = [(sDVE, dve_c[c])]
                        if c == 0 and rows_evac:
                            cw.append((sACT, rows_evac))
                        I(p, "tensor", lambda t, c0=(c == 0), c1=(c == HC - 1):
                          t.matmul(ps_61[:], onescol, ob16[:], start=c0, stop=c1),
                          waits=cw)
                        I(p, "tensor", lambda t, r=sq6[c % 2][:],
                          c0=(c == 0), c1=(c == HC - 1):
                          t.matmul(ps_62[:], onescol, r, start=c0, stop=c1),
                          inc=(sPE, sPE.take()))
                        pe_c[c] = sPE.n
                    stats_done = sPE.n
                    rows_evac = ln_rows(p, ps_61, ps_62, rA, rB, rC, [(sPE, stats_done)])
                    rw = [(sACT, rows_evac)] + \
                         ([(sPE, bc_mm)] if tb else [(sPL, pad_ready)])
                    I(p, "scalar", lambda a: a.copy(rowM6[0:1, :], rA), waits=rw)
                    I(p, "scalar", lambda a: a.copy(rowR6[0:1, :], rB),
                      inc=(sACT, sACT.take()))
                    rows_b = sACT.n
                    bw = [(sACT, rows_b)] + ([(sACT, bc_evac)] if tb else [])
                    I(p, "tensor", lambda t: t.matmul(ps_6M[:], onesM[:], rowM6[:],
                                                      start=True, stop=True), waits=bw)
                    I(p, "tensor", lambda t: t.matmul(ps_6R[:], onesM[:], rowR6[:],
                                                      start=True, stop=True),
                      inc=(sPE, sPE.take()))
                    bc_mm = sPE.n
                    ew = [(sPE, bc_mm)] + ([(sDVE, y_math)] if tb else [])
                    I(p, "scalar", lambda a: a.copy(M6[:], ps_6M[:]), waits=ew)
                    I(p, "scalar", lambda a: a.copy(R6[:], ps_6R[:]),
                      inc=(sACT, sACT.take()))
                    bc_evac = sACT.n
                    for c in range(HC):
                        cw = [(sACT, bc_evac)] if c == 0 else []
                        osl = outb[:, c * 512:(c + 1) * 512]
                        xsl = x1b6[:, c * 512:(c + 1) * 512]
                        I(p, "vector", lambda v, a=osl: v.tensor_sub(t6[:], a, M6[:]),
                          waits=cw)
                        I(p, "vector", lambda v: v.tensor_mul(t6[:], t6[:], R6[:]))
                        inc = (sDVE, sDVE.take()) if c == HC - 1 else None
                        I(p, "vector", lambda v, d=osl, x=xsl: v.tensor_add(d, t6[:], x),
                          inc=inc)
                    y_math = sDVE.n
                    for g4 in range(4):
                        dst = y_t[g4 * 8:(g4 + 1) * 8, :, t0:t0 + 512].rearrange(
                            "c p t -> p c t")
                        I(p, "gpsimd",
                          lambda g, d=dst, s=sp3(outb[:, g4 * 4096:(g4 + 1) * 4096], 512):
                          g.dma_start(d, s),
                          waits=[(sDVE, y_math)] if g4 == 0 else (),
                          inc=(sY, sY.take(16)))
                    y_store = sY.n
                p.add("sync", lambda eng, v=sY.n, h=sY.h: eng.wait_ge(h, v))
                p.emit(nc)

    nc.finalize()
    return nc


def _alibi_slopes(n):
    cp2 = 2 ** math.floor(math.log2(n))
    base = 2.0 ** (-8.0 / cp2)
    slopes = base ** np.arange(1, cp2 + 1, dtype=np.float32)
    if cp2 != n:
        extra_base = 2.0 ** (-4.0 / cp2)
        extra = extra_base ** np.arange(1, 1 + 2 * (n - cp2), 2, dtype=np.float32)
        slopes = np.concatenate([slopes, extra])
    return slopes.astype(np.float32)


def host_inputs(L0_attn_w_qkv, L0_attn_w_o, L0_ffn_w_gate, L0_ffn_w_up, L0_ffn_w_down):
    bf = ml_dtypes.bfloat16
    wqkv = np.asarray(L0_attn_w_qkv, dtype=np.float32)
    wo = np.asarray(L0_attn_w_o, dtype=np.float32)
    wg = np.asarray(L0_ffn_w_gate, dtype=np.float32)
    wu = np.asarray(L0_ffn_w_up, dtype=np.float32)
    wd = np.asarray(L0_ffn_w_down, dtype=np.float32)
    shared = {}
    shared["wqk_t"] = np.ascontiguousarray(
        wqkv[:, :40 * 128].reshape(HC, 128, 40, 128).transpose(2, 1, 0, 3))
    shared["wv_t"] = np.ascontiguousarray(
        wqkv[:, 40 * 128:].reshape(HC, 128, 1024).transpose(1, 0, 2))
    shared["wo_t"] = np.ascontiguousarray(
        wo.reshape(HC, 128, HC, 128).transpose(2, 1, 0, 3))
    shared["wg_t"] = np.ascontiguousarray(
        wg.reshape(IC, 128, HC, 128).transpose(0, 3, 2, 1))
    shared["wu_t"] = np.ascontiguousarray(
        wu.reshape(IC, 128, HC, 128).transpose(0, 3, 2, 1))
    shared["wd_t"] = np.ascontiguousarray(
        wd.reshape(HC, 128, IC, 128).transpose(0, 3, 2, 1))
    q = np.arange(S, dtype=np.float32)
    rel = q[None, :] - q[:, None]
    Dt = np.sqrt(np.maximum(rel, 0.0))
    Dt[rel < 0] = MASK_D
    shared["dT_c"] = np.ascontiguousarray(Dt.reshape(16, 128, S))
    slopes = _alibi_slopes(NH)
    dneg = np.zeros((128, NH, 128), dtype=np.float32)
    for h in range(NH):
        dneg[np.arange(128), h, np.arange(128)] = -slopes[h] / SCALE
    shared["dneg_c"] = np.ascontiguousarray(dneg.reshape(128, NH * 128).astype(bf))
    shared["onesM_c"] = np.ones((128, 128), dtype=bf)
    shared["mhalfM_c"] = np.full((128, 128), -0.5, dtype=bf)
    return shared


def kernel(L0_attn_hidden_states, L0_attn_w_qkv, L0_attn_w_o,
           L0_ffn_w_gate, L0_ffn_w_up, L0_ffn_w_down):
    x = np.asarray(L0_attn_hidden_states, dtype=np.float32)
    B = x.shape[0]
    shared = host_inputs(L0_attn_w_qkv, L0_attn_w_o, L0_ffn_w_gate,
                         L0_ffn_w_up, L0_ffn_w_down)
    nc = build_program()
    in_maps = []
    for c in range(B):
        m = dict(shared)
        m["x_t"] = np.ascontiguousarray(x[c].T).reshape(HC, 128, S)
        in_maps.append(m)
    res = run_bass_kernel_spmd(nc, in_maps, core_ids=list(range(B)))
    y = np.empty((B, S, H), dtype=np.float32)
    for c in range(B):
        y[c] = res.results[c]["y_t"].reshape(H, S).T
    return y


# version 08
# speedup vs baseline: 1.0114x; 1.0114x over previous
import math
import contextlib
import numpy as np
import ml_dtypes
import concourse.bass as bass
import concourse.mybir as mybir
from concourse.bass_utils import run_bass_kernel_spmd

F32 = mybir.dt.float32
BF16 = mybir.dt.bfloat16
AF = mybir.ActivationFunctionType
ALU = mybir.AluOpType
AX = mybir.AxisListType

S = 2048
H = 4096
NH, NKV, G, D = 32, 8, 4, 128
INTER = 11008
HC = H // 128
IC = INTER // 128
TB = 4
SCALE = D ** -0.5
INV_H = 1.0 / H
EPS = 1e-5
MASK_D = 30000.0

ENGINES = ("sync", "gpsimd", "tensor", "scalar", "vector")
SUPERS = [list(range(s, min(s + 8, IC))) for s in range(0, IC, 8)]


class Sem:
    __slots__ = ("h", "n", "name")

    def __init__(self, h, name):
        self.h = h
        self.n = 0
        self.name = name

    def bump(self, k=1):
        self.n += k
        return self.n

    def take(self, k=1):
        self.n += k
        return k


class Plan:
    def __init__(self):
        self.ev = {e: [] for e in ENGINES}

    def add(self, e, f):
        self.ev[e].append(f)

    def emit(self, nc):
        with nc.Block() as blk:
            for e in ENGINES:
                evs = self.ev[e]
                if not evs:
                    continue

                def mk(evs=evs):
                    def body(eng):
                        for f in evs:
                            f(eng)
                    return body
                getattr(blk, e)(mk())


def I(plan, e, build, waits=(), inc=None, inc2=None):
    waits = tuple((s.h, v) for (s, v) in waits)

    def f(eng, build=build, waits=waits, inc=inc, inc2=inc2):
        for h, v in waits:
            eng.wait_ge(h, v)
        ins = build(eng)
        if inc is not None:
            ins = ins.then_inc(inc[0].h, inc[1])
        if inc2 is not None:
            ins = ins.then_inc(inc2[0].h, inc2[1])
    plan.add(e, f)


def sp3(ap, w):
    return ap.rearrange("p (c t) -> p c t", t=w)


def build_program(n_phases=99, debug=False):
    nc = bass.Bass("TRN2", target_bir_lowering=False, debug=False)
    dbg_kind = "ExternalOutput" if debug else "Internal"

    x_t = nc.dram_tensor("x_t", [HC, 128, S], F32, kind="ExternalInput").ap()
    wqk_t = nc.dram_tensor("wqk_t", [40, 128, HC, 128], F32, kind="ExternalInput").ap()
    wv_t = nc.dram_tensor("wv_t", [128, HC, 1024], F32, kind="ExternalInput").ap()
    wo_t = nc.dram_tensor("wo_t", [HC, 128, HC, 128], F32, kind="ExternalInput").ap()
    wg_t = nc.dram_tensor("wg_t", [IC, 128, HC, 128], F32, kind="ExternalInput").ap()
    wu_t = nc.dram_tensor("wu_t", [IC, 128, HC, 128], F32, kind="ExternalInput").ap()
    wd_t = nc.dram_tensor("wd_t", [HC, 128, IC, 128], F32, kind="ExternalInput").ap()
    dT_c = nc.dram_tensor("dT_c", [16, 128, S], BF16, kind="ExternalInput").ap()
    onesM_c = nc.dram_tensor("onesM_c", [128, 128], BF16, kind="ExternalInput").ap()

    q_s = nc.dram_tensor("q_s", [NH, 128, S], BF16, kind=dbg_kind).ap()
    k_s = nc.dram_tensor("k_s", [NKV, 128, S], BF16, kind=dbg_kind).ap()
    v_s = nc.dram_tensor("v_s", [16, 128, 1024], BF16, kind=dbg_kind).ap()
    ao_s = nc.dram_tensor("ao_s", [NH, 128, S], BF16, kind=dbg_kind).ap()
    pj_s = nc.dram_tensor("pj_s", [HC, 128, S], BF16, kind=dbg_kind).ap()
    x1_s = nc.dram_tensor("x1_s", [HC, 128, S], F32, kind=dbg_kind).ap()
    ac_s = nc.dram_tensor("ac_s", [HC, 128, S], F32, kind=dbg_kind).ap()
    y_t = nc.dram_tensor("y_t", [HC, 128, S], F32, kind="ExternalOutput").ap()
    h1_d = h2_d = None
    if debug:
        h1_d = nc.dram_tensor("h1_d", [128, HC * S], BF16, kind="ExternalOutput").ap()
        h2_d = nc.dram_tensor("h2_d", [128, HC * S], BF16, kind="ExternalOutput").ap()

    ctx = contextlib.ExitStack()
    with ctx:
        semnames = ["sW", "sWf", "sSP", "sSPa", "sPL", "sY",
                    "sPE", "sPEn", "sPEs", "sPEp", "sPEr", "sPEd", "sPEg", "sPEu",
                    "sACT", "sACTn", "sACTs", "sACTr", "sACTc", "sACTd",
                    "sDVE", "sDVEb", "sDVEo", "sDVEa", "sDVEr",
                    "sWq0", "sWq1", "sSt0", "sSt1", "sAo0", "sAo1",
                    "sXl0", "sXl1", "sXr0", "sXr1",
                    "sWg0", "sWg1", "sWu0", "sWu1", "sWd0", "sWd1",
                    "sAc0", "sAc1", "sXs0", "sXs1"]
        sm = {n: Sem(ctx.enter_context(nc.semaphore(n)), n) for n in semnames}
        sW, sWf, sSP, sSPa, sPL, sY = (sm[n] for n in ("sW", "sWf", "sSP", "sSPa", "sPL", "sY"))
        sPE, sPEn, sPEs, sPEp, sPEr, sPEd, sPEg, sPEu = (
            sm[n] for n in ("sPE", "sPEn", "sPEs", "sPEp", "sPEr", "sPEd", "sPEg", "sPEu"))
        sACT, sACTn, sACTs, sACTr, sACTc, sACTd = (
            sm[n] for n in ("sACT", "sACTn", "sACTs", "sACTr", "sACTc", "sACTd"))
        sDVE, sDVEb, sDVEo, sDVEa, sDVEr = (sm[n] for n in ("sDVE", "sDVEb", "sDVEo", "sDVEa", "sDVEr"))
        sWq = [sm["sWq0"], sm["sWq1"]]
        sSt = [sm["sSt0"], sm["sSt1"]]
        sAo = [sm["sAo0"], sm["sAo1"]]
        sXl = [sm["sXl0"], sm["sXl1"]]
        sXr = [sm["sXr0"], sm["sXr1"]]
        sWg = [sm["sWg0"], sm["sWg1"]]
        sWu = [sm["sWu0"], sm["sWu1"]]
        sWd = [sm["sWd0"], sm["sWd1"]]
        sAc = [sm["sAc0"], sm["sAc1"]]
        sXs = [sm["sXs0"], sm["sXs1"]]

        onesM = ctx.enter_context(nc.sbuf_tensor("onesM", [128, 128], BF16))
        cst = ctx.enter_context(nc.sbuf_tensor("cst", [128, 48], F32))
        onescol = onesM[:, 0:1]

        slopes_f = [float(s) for s in _alibi_slopes(NH)]
        cvals = [EPS, SCALE, INV_H, -0.5, -1.0] + slopes_f
        for i, v in enumerate(cvals):
            nc.const_aps.register(F32, v, cst[:, i:i + 1])

        p = Plan()
        for i, v in enumerate(cvals):
            I(p, "gpsimd", lambda g, i=i, v=v: g.memset(cst[:, i:i + 1], v))
        I(p, "sync", lambda s: s.dma_start(onesM[:], onesM_c), inc=(sSP, sSP.take(16)))
        p.add("sync", lambda eng, v=sSP.n, h=sSP.h: eng.wait_ge(h, v))
        p.emit(nc)

        def ln_rows(p, ps1, ps2, rA, rB, rC, waits):
            I(p, "scalar", lambda a: a.copy(rA, ps1[:]), waits=waits)
            I(p, "scalar", lambda a: a.copy(rB, ps2[:]))
            I(p, "scalar", lambda a: a.mul(rA, rA, INV_H))
            I(p, "scalar", lambda a: a.mul(rB, rB, INV_H), inc=(sACT, sACT.take()))
            se = sACT.n
            I(p, "vector", lambda v: v.tensor_mul(rC, rA, rA), waits=[(sACT, se)])
            I(p, "vector", lambda v: v.tensor_sub(rB, rB, rC), inc=(sDVE, sDVE.take()))
            vd = sDVE.n
            I(p, "scalar", lambda a: a.activation(rB, rB, AF.Ln, bias=EPS),
              waits=[(sDVE, vd)])
            I(p, "scalar", lambda a: a.activation(rB, rB, AF.Exp, scale=-0.5),
              inc=(sACT, sACT.take()))
            return sACT.n

        if n_phases >= 1:
            big_ctx = contextlib.ExitStack()
            with big_ctx:
                big = big_ctx.enter_context(nc.sbuf_tensor("big1", [128, HC * S], BF16))
                with contextlib.ExitStack() as _es:
                    xblk = _es.enter_context(nc.sbuf_tensor("xblk", [128, HC * 512], BF16))
                    sqf = _es.enter_context(nc.sbuf_tensor("sqf", [128, HC * 512], BF16))
                    rows1 = _es.enter_context(nc.sbuf_tensor("rows1", [1, 512 * 3], F32))
                    rowM = _es.enter_context(nc.sbuf_tensor("rowM", [128, 512], BF16))
                    rowR = _es.enter_context(nc.sbuf_tensor("rowR", [128, 512], BF16))
                    Mt = _es.enter_context(nc.sbuf_tensor("Mt", [128, 512], BF16))
                    Rt = _es.enter_context(nc.sbuf_tensor("Rt", [128, 512], BF16))
                    t1b = _es.enter_context(nc.sbuf_tensor("t1b", [128, 512], BF16))
                    ps_st1 = _es.enter_context(nc.psum_tensor("ps_st1", [1, 512], F32))
                    ps_st2 = _es.enter_context(nc.psum_tensor("ps_st2", [1, 512], F32))
                    ps_bM = _es.enter_context(nc.psum_tensor("ps_bM", [128, 512], F32))
                    ps_bR = _es.enter_context(nc.psum_tensor("ps_bR", [128, 512], F32))
                    rA = rows1[:, 0:512]
                    rB = rows1[:, 512:1024]
                    rC = rows1[:, 1024:1536]
                    p = Plan()
                    I(p, "gpsimd", lambda g: g.memset(rowM[:], 0.0))
                    I(p, "gpsimd", lambda g: g.memset(rowR[:], 0.0), inc=(sPL, sPL.take()))
                    pad_ready = sPL.n
                    norm_end = 0
                    rows_evac = 0
                    bcast_done = 0
                    bcast_evac = 0
                    for tb in range(TB):
                        t0 = tb * 512
                        lw = [(sDVE, norm_end)] if tb else []
                        for g4 in range(4):
                            src = x_t[g4 * 8:(g4 + 1) * 8, :, t0:t0 + 512].rearrange(
                                "c p t -> p c t")
                            I(p, "gpsimd",
                              lambda g, d=sp3(xblk[:, g4 * 4096:(g4 + 1) * 4096], 512), s=src:
                              g.dma_start(d, s),
                              waits=lw if g4 == 0 else (), inc=(sW, sW.take(16)))
                        ld = sW.n
                        for c in range(HC):
                            cw = [(sW, ld)] if c == 0 else []
                            xs = xblk[:, c * 512:(c + 1) * 512]
                            sq = sqf[:, c * 512:(c + 1) * 512]
                            inc = (sDVE, sDVE.take()) if c == HC - 1 else None
                            I(p, "vector", lambda v, a=xs, b=sq: v.tensor_mul(b, a, a),
                              waits=cw, inc=inc)
                        sq_done = sDVE.n
                        for c in range(HC):
                            cw = []
                            if c == 0:
                                cw = [(sDVE, sq_done)]
                                if rows_evac:
                                    cw.append((sACT, rows_evac))
                            xs = xblk[:, c * 512:(c + 1) * 512]
                            sq = sqf[:, c * 512:(c + 1) * 512]
                            I(p, "tensor", lambda t, r=xs, c0=(c == 0), c1=(c == HC - 1):
                              t.matmul(ps_st1[:], onescol, r, start=c0, stop=c1), waits=cw)
                            inc = (sPE, sPE.take()) if c == HC - 1 else None
                            I(p, "tensor", lambda t, r=sq, c0=(c == 0), c1=(c == HC - 1):
                              t.matmul(ps_st2[:], onescol, r, start=c0, stop=c1), inc=inc)
                        stats_done = sPE.n
                        rows_evac = ln_rows(p, ps_st1, ps_st2, rA, rB, rC,
                                            [(sPE, stats_done)])
                        rw = [(sACT, rows_evac)] + \
                             ([(sPE, bcast_done)] if tb else [(sPL, pad_ready)])
                        I(p, "scalar", lambda a: a.copy(rowM[0:1, :], rA), waits=rw)
                        I(p, "scalar", lambda a: a.copy(rowR[0:1, :], rB),
                          inc=(sACT, sACT.take()))
                        rows_b = sACT.n
                        bw = [(sACT, rows_b)] + ([(sACT, bcast_evac)] if tb else [])
                        I(p, "tensor", lambda t: t.matmul(ps_bM[:], onesM[:], rowM[:],
                                                          start=True, stop=True), waits=bw)
                        I(p, "tensor", lambda t: t.matmul(ps_bR[:], onesM[:], rowR[:],
                                                          start=True, stop=True),
                          inc=(sPE, sPE.take()))
                        bcast_done = sPE.n
                        ew = [(sPE, bcast_done)] + ([(sDVE, norm_end)] if tb else [])
                        I(p, "scalar", lambda a: a.copy(Mt[:], ps_bM[:]), waits=ew)
                        I(p, "scalar", lambda a: a.copy(Rt[:], ps_bR[:]),
                          inc=(sACT, sACT.take()))
                        bcast_evac = sACT.n
                        for c in range(HC):
                            cw = [(sACT, bcast_evac)] if c == 0 else []
                            xs = xblk[:, c * 512:(c + 1) * 512]
                            dst = big[:, c * S + t0:c * S + t0 + 512]
                            I(p, "vector", lambda v, a=xs: v.tensor_sub(t1b[:], a, Mt[:]),
                              waits=cw)
                            inc = (sDVE, sDVE.take()) if c == HC - 1 else None
                            I(p, "vector", lambda v, d=dst: v.tensor_mul(d, t1b[:], Rt[:]),
                              inc=inc)
                        norm_end = sDVE.n
                    p.emit(nc)

                if n_phases >= 2:
                    with contextlib.ExitStack() as _es:
                        wqb0 = _es.enter_context(nc.sbuf_tensor("wqb0", [128, HC * 128], BF16))
                        wqb1 = _es.enter_context(nc.sbuf_tensor("wqb1", [128, HC * 128], BF16))
                        stg0 = _es.enter_context(nc.sbuf_tensor("stg0", [128, 2048], BF16))
                        stg1 = _es.enter_context(nc.sbuf_tensor("stg1", [128, 2048], BF16))
                        ps_q0 = _es.enter_context(nc.psum_tensor("ps_q0", [128, 512], F32))
                        ps_q1 = _es.enter_context(nc.psum_tensor("ps_q1", [128, 512], F32))
                        ps_q2 = _es.enter_context(nc.psum_tensor("ps_q2", [128, 512], F32))
                        ps_q3 = _es.enter_context(nc.psum_tensor("ps_q3", [128, 512], F32))
                        wqb = [wqb0, wqb1]
                        stg = [stg0, stg1]
                        psq = [ps_q0, ps_q1, ps_q2, ps_q3]
                        p = Plan()
                        wfree = [0, 0]
                        pe_done = {}
                        evac = {}
                        store = {}
                        for oc in range(40):
                            par = oc % 2
                            lw = [(sACT, evac[(oc - 2, TB - 1)])] if oc >= 2 else []
                            src = wqk_t[oc:oc + 1].rearrange("o p c i -> (o p) (c i)")
                            I(p, "gpsimd", lambda g, d=wqb[par][:], s=src: g.dma_start(d, s),
                              waits=lw, inc=(sWq[par], sWq[par].take(16)))
                            wld = (sWq[par], sWq[par].n)
                            for kc in range(HC):
                                for tb in range(TB):
                                    cw = []
                                    if kc == 0:
                                        cw = [wld]
                                        if oc >= 1:
                                            cw.append((sACT, evac[(oc - 1, tb)]))
                                    lhs = wqb[par][:, kc * 128:(kc + 1) * 128]
                                    rhs = big[:, kc * S + tb * 512:kc * S + tb * 512 + 512]
                                    inc = None
                                    if kc == HC - 1:
                                        inc = (sPE, sPE.take())
                                    I(p, "tensor", lambda t, o=psq[tb][:], l=lhs, r=rhs,
                                      c0=(kc == 0), c1=(kc == HC - 1):
                                      t.matmul(o, l, r, start=c0, stop=c1),
                                      waits=cw, inc=inc)
                                    if kc == HC - 1:
                                        pe_done[(oc, tb)] = sPE.n
                            for tb in range(TB):
                                cw = [(sPE, pe_done[(oc, tb)])]
                                if tb == 0 and oc >= 2:
                                    cw.append(store[oc - 2])
                                dst = stg[par][:, tb * 512:(tb + 1) * 512]
                                I(p, "scalar", lambda a, d=dst, s=psq[tb][:]: a.copy(d, s),
                                  waits=cw, inc=(sACT, sACT.take()))
                                evac[(oc, tb)] = sACT.n
                            tgt = (q_s[oc:oc + 1] if oc < NH else
                                   k_s[oc - NH:oc - NH + 1]).rearrange("o p t -> (o p) t")
                            I(p, "sync", lambda s, d=tgt, s2=stg[par][:]: s.dma_start(d, s2),
                              waits=[(sACT, evac[(oc, TB - 1)])],
                              inc=(sSt[par], sSt[par].take(16)))
                            store[oc] = (sSt[par], sSt[par].n)
                        p.emit(nc)

                    with contextlib.ExitStack() as _es:
                        wvb = _es.enter_context(nc.sbuf_tensor("wvb", [128, HC * 1024], BF16))
                        stv0 = _es.enter_context(nc.sbuf_tensor("stv0", [128, 1024], BF16))
                        stv1 = _es.enter_context(nc.sbuf_tensor("stv1", [128, 1024], BF16))
                        ps_vA0 = _es.enter_context(nc.psum_tensor("ps_vA0", [128, 512], F32))
                        ps_vB0 = _es.enter_context(nc.psum_tensor("ps_vB0", [128, 512], F32))
                        ps_vA1 = _es.enter_context(nc.psum_tensor("ps_vA1", [128, 512], F32))
                        ps_vB1 = _es.enter_context(nc.psum_tensor("ps_vB1", [128, 512], F32))
                        stv = [stv0, stv1]
                        psA = [ps_vA0, ps_vA1]
                        psB = [ps_vB0, ps_vB1]
                        p = Plan()
                        I(p, "gpsimd",
                          lambda g: g.dma_start(wvb[:], wv_t.rearrange("p c i -> p (c i)")),
                          inc=(sW, sW.take(16)))
                        wld = sW.n
                        vevac = {}
                        vstore = {}
                        for tt in range(16):
                            par = tt % 2
                            for kc in range(HC):
                                cw = []
                                if kc == 0:
                                    cw = [(sW, wld)]
                                    if tt >= 2:
                                        cw.append((sACT, vevac[tt - 2]))
                                lhs = big[:, kc * S + tt * 128:kc * S + tt * 128 + 128]
                                I(p, "tensor", lambda t, o=psA[par][:], l=lhs,
                                  r=wvb[:, kc * 1024:kc * 1024 + 512],
                                  c0=(kc == 0), c1=(kc == HC - 1):
                                  t.matmul(o, l, r, start=c0, stop=c1), waits=cw)
                                inc = (sPE, sPE.take()) if kc == HC - 1 else None
                                I(p, "tensor", lambda t, o=psB[par][:], l=lhs,
                                  r=wvb[:, kc * 1024 + 512:kc * 1024 + 1024],
                                  c0=(kc == 0), c1=(kc == HC - 1):
                                  t.matmul(o, l, r, start=c0, stop=c1), inc=inc)
                            vdone = sPE.n
                            cw = [(sPE, vdone)]
                            if tt >= 2:
                                cw.append(vstore[tt - 2])
                            I(p, "scalar",
                              lambda a, d=stv[par][:, 0:512], s=psA[par][:]: a.copy(d, s),
                              waits=cw)
                            I(p, "scalar",
                              lambda a, d=stv[par][:, 512:1024], s=psB[par][:]: a.copy(d, s),
                              inc=(sACT, sACT.take()))
                            vevac[tt] = sACT.n
                            dstv = v_s[tt:tt + 1].rearrange("o p t -> (o p) t")
                            I(p, "sync", lambda s, d=dstv, s2=stv[par][:]: s.dma_start(d, s2),
                              waits=[(sACT, vevac[tt])],
                              inc=(sSt[par], sSt[par].take(16)))
                            vstore[tt] = (sSt[par], sSt[par].n)
                        if debug:
                            I(p, "sync", lambda s: s.dma_start(h1_d, big[:]),
                              inc=(sSP, sSP.take(16)))
                            p.add("sync", lambda eng, v=sSP.n, h=sSP.h: eng.wait_ge(h, v))
                        p.emit(nc)

        if n_phases >= 3:
            KTOT = S // 128
            with contextlib.ExitStack() as _es:
                Ksb = _es.enter_context(nc.sbuf_tensor("Ksb", [128, NKV * S], BF16))
                Vsb = _es.enter_context(nc.sbuf_tensor("Vsb", [128, 16 * 1024], BF16))
                Qb = _es.enter_context(nc.sbuf_tensor("Qb", [128, NH * 512], BF16))
                Db = _es.enter_context(nc.sbuf_tensor("Db", [128, KTOT * 512], BF16))
                sqq = _es.enter_context(nc.sbuf_tensor("sqq", [128, 2 * 512], BF16))
                sqk = _es.enter_context(nc.sbuf_tensor("sqk", [128, S], BF16))
                nkr = _es.enter_context(nc.sbuf_tensor("nkr", [1, S], F32))
                nqr = _es.enter_context(nc.sbuf_tensor("nqr", [1, 2 * 512], F32))
                qmx = _es.enter_context(nc.sbuf_tensor("qmx", [1, 512], F32))
                mk2s = _es.enter_context(nc.sbuf_tensor("mk2s", [1, 8], F32))
                scr = _es.enter_context(nc.sbuf_tensor("scr", [1, 8], F32))
                colC = _es.enter_context(nc.sbuf_tensor("colC", [128, 1], BF16))
                bcolT = _es.enter_context(nc.sbuf_tensor("bcolT", [128, 2], F32))
                rowC0 = _es.enter_context(nc.sbuf_tensor("rowC0", [128, 512], BF16))
                rowC1 = _es.enter_context(nc.sbuf_tensor("rowC1", [128, 512], BF16))
                stA = _es.enter_context(nc.sbuf_tensor("stA", [128, 1024], F32))
                stB = _es.enter_context(nc.sbuf_tensor("stB", [128, 1024], F32))
                ptA = _es.enter_context(nc.sbuf_tensor("ptA", [128, 1024], BF16))
                ptB = _es.enter_context(nc.sbuf_tensor("ptB", [128, 1024], BF16))
                recb0 = _es.enter_context(nc.sbuf_tensor("recb0", [128, 512], F32))
                recb1 = _es.enter_context(nc.sbuf_tensor("recb1", [128, 512], F32))
                aot0 = _es.enter_context(nc.sbuf_tensor("aot0", [128, 512], BF16))
                aot1 = _es.enter_context(nc.sbuf_tensor("aot1", [128, 512], BF16))
                ps_s0 = _es.enter_context(nc.psum_tensor("ps_s0", [128, 512], F32))
                ps_s1 = _es.enter_context(nc.psum_tensor("ps_s1", [128, 512], F32))
                ps_s2 = _es.enter_context(nc.psum_tensor("ps_s2", [128, 512], F32))
                ps_o0 = _es.enter_context(nc.psum_tensor("ps_o0", [128, 512], F32))
                ps_o1 = _es.enter_context(nc.psum_tensor("ps_o1", [128, 512], F32))
                ps_sumB = _es.enter_context(nc.psum_tensor("ps_sumB", [128, 512], F32))
                ps_rb = _es.enter_context(nc.psum_tensor("ps_rb", [128, 512], F32))
                ps_pre = _es.enter_context(nc.psum_tensor("ps_pre", [128, 512], F32))
                pss = [ps_s0, ps_s1, ps_s2]
                pso = [ps_o0, ps_o1]
                stile = [stA, stB]
                pt = [ptA, ptB]
                rowC = [rowC0, rowC1]
                recb = [recb0, recb1]
                aot = [aot0, aot1]
                p = Plan()
                I(p, "gpsimd", lambda g: g.memset(colC[:], 0.0))
                I(p, "gpsimd", lambda g: g.memset(rowC0[:], 0.0))
                I(p, "gpsimd", lambda g: g.memset(rowC1[:], 0.0), inc=(sPL, sPL.take()))
                pad_ready = sPL.n
                I(p, "sync", lambda s: s.dma_start(sp3(Ksb[:], S),
                                                   k_s.rearrange("h p t -> p h t")),
                  inc=(sSP, sSP.take(16)))
                I(p, "sync", lambda s: s.dma_start(sp3(Vsb[:], 1024),
                                                   v_s.rearrange("h p t -> p h t")),
                  inc=(sSP, sSP.take(16)))
                kv_ld = sSP.n

                row_evac = [0, 0, 0]
                rmax_done = 0
                for kv in range(NKV):
                    cw = [(sSP, kv_ld)] if kv == 0 else [(sPEn, sPEn.n)]
                    I(p, "vector", lambda v, a=Ksb[:, kv * S:(kv + 1) * S]:
                      v.tensor_mul(sqk[:], a, a), waits=cw, inc=(sDVE, sDVE.take()))
                    sqk_done = sDVE.n
                    ev = 0
                    for seg in range(4):
                        r = (kv * 4 + seg) % 3
                        cw = [(sDVE, sqk_done)]
                        if row_evac[r]:
                            cw.append((sACTn, row_evac[r]))
                        I(p, "tensor", lambda t, r=r,
                          s2=sqk[:, seg * 512:(seg + 1) * 512]:
                          t.matmul(ps_pre[r * 32:r * 32 + 1, :], onescol, s2,
                                   start=True, stop=True),
                          waits=cw, inc=(sPEn, sPEn.take()))
                        cw = [(sPEn, sPEn.n)]
                        if seg == 0 and rmax_done:
                            cw.append((sDVE, rmax_done))
                        I(p, "scalar", lambda a, r=r,
                          d=nkr[:, seg * 512:(seg + 1) * 512]:
                          a.copy(d, ps_pre[r * 32:r * 32 + 1, :]),
                          waits=cw, inc=(sACTn, sACTn.take()))
                        row_evac[r] = sACTn.n
                        ev = sACTn.n
                    I(p, "vector", lambda v, d=mk2s[:, kv:kv + 1]:
                      v.reduce_max(d, nkr[:], axis=AX.X),
                      waits=[(sACTn, ev)], inc=(sDVE, sDVE.take()))
                    rmax_done = sDVE.n
                I(p, "vector", lambda v: v.reduce_max(scr[:, 0:1], mk2s[:], axis=AX.X))

                bank_last_stt = [0, 0, 0]
                stile_last_exp = [0, 0]
                pt_last_pv = [0, 0]
                pv_all = {}
                recip_done = {}
                rb_mm = {}
                rb_evac = {}
                ao_done = {}
                ao_store = {}
                nq_mm = {}
                last_scores = 0
                last_stt = 0
                bc_mm_prev = 0

                def em_tail(g):
                    if g < 0:
                        return
                    j = g % 2
                    cw = [(sPEp, pv_all[g]), (sPL, pad_ready)]
                    if g >= 2 and rb_mm.get(g - 2):
                        cw.append((sPEr, rb_mm[g - 2]))
                    def _recip(v, j=j, rb=(g % 2) * 32):
                        with nc.allow_low_precision(
                                reason="softmax 1/sum cast to bf16 for bcast matmul"):
                            return v.reciprocal(rowC[j][0:1, :], ps_sumB[rb:rb + 1, :])
                    I(p, "vector", _recip, waits=cw, inc=(sDVEr, sDVEr.take()))
                    recip_done[g] = sDVEr.n
                    cw = [(sDVEr, recip_done[g])]
                    if g >= 1 and rb_evac.get(g - 1):
                        cw.append((sACTc, rb_evac[g - 1]))
                    I(p, "tensor", lambda t, j=j:
                      t.matmul(ps_rb[:], onesM[:], rowC[j][:], start=True, stop=True),
                      waits=cw, inc=(sPEr, sPEr.take()))
                    rb_mm[g] = sPEr.n
                    cw = [(sPEr, rb_mm[g])]
                    if g >= 2 and ao_done.get(g - 2):
                        cw.append((sDVEo, ao_done[g - 2]))
                    I(p, "scalar", lambda a, j=j: a.copy(recb[j][:], ps_rb[:]),
                      waits=cw, inc=(sACTc, sACTc.take()))
                    rb_evac[g] = sACTc.n
                    cw = [(sACTc, rb_evac[g])]
                    if g >= 2 and ao_store.get(g - 2):
                        cw.append(ao_store[g - 2])
                    I(p, "vector", lambda v, j=j:
                      v.tensor_mul(aot[j][:], pso[j][:], recb[j][:]),
                      waits=cw, inc=(sDVEo, sDVEo.take()))
                    ao_done[g] = sDVEo.n
                    qq, hh = divmod(g, NH)
                    dsta = ao_s[hh:hh + 1, :, qq * 512:(qq + 1) * 512].rearrange(
                        "o p t -> (o p) t")
                    I(p, "sync", lambda s, j=j, d=dsta: s.dma_start(d, aot[j][:]),
                      waits=[(sDVEo, ao_done[g])], inc=(sAo[j], sAo[j].take(16)))
                    ao_store[g] = (sAo[j], sAo[j].n)

                for qb in range(TB):
                    KT = 4 * (qb + 1)
                    cw = [(sPEs, last_scores)] if qb else []
                    for g4 in range(4):
                        src = q_s[g4 * 8:(g4 + 1) * 8, :,
                                  qb * 512:(qb + 1) * 512].rearrange("c p t -> p c t")
                        I(p, "sync",
                          lambda s, d=sp3(Qb[:, g4 * 4096:(g4 + 1) * 4096], 512), s2=src:
                          s.dma_start(d, s2),
                          waits=cw if g4 == 0 else (), inc=(sSP, sSP.take(16)))
                    srcD = dT_c[0:KT, :, qb * 512:(qb + 1) * 512].rearrange(
                        "c p t -> p c t")
                    I(p, "sync", lambda s, d=sp3(Db[:, 0:KT * 512], 512), s2=srcD:
                      s.dma_start(d, s2),
                      waits=[(sDVEb, last_stt)] if qb else (),
                      inc=(sSP, sSP.take(16)))
                    qld = sSP.n
                    max_acc = {}
                    for h in range(NH):
                        j = h % 2
                        r = h % 3
                        cw = []
                        if h == 0:
                            cw.append((sSP, qld))
                        if nq_mm.get(h - 2):
                            cw.append((sPEn, nq_mm[h - 2]))
                        I(p, "vector", lambda v, j=j, a=Qb[:, h * 512:(h + 1) * 512]:
                          v.tensor_mul(sqq[:, j * 512:(j + 1) * 512], a, a),
                          waits=cw, inc=(sDVE, sDVE.take()))
                        sq_done = sDVE.n
                        cw = [(sDVE, sq_done)]
                        if row_evac[r]:
                            cw.append((sACTn, row_evac[r]))
                        I(p, "tensor", lambda t, j=j, r=r:
                          t.matmul(ps_pre[r * 32:r * 32 + 1, :], onescol,
                                   sqq[:, j * 512:(j + 1) * 512],
                                   start=True, stop=True),
                          waits=cw, inc=(sPEn, sPEn.take()))
                        nq_mm[h] = sPEn.n
                        cw = [(sPEn, nq_mm[h])]
                        if max_acc.get(h - 2):
                            cw.append((sDVE, max_acc[h - 2]))
                        I(p, "scalar", lambda a, j=j, r=r:
                          a.copy(nqr[:, j * 512:(j + 1) * 512],
                                 ps_pre[r * 32:r * 32 + 1, :]),
                          waits=cw, inc=(sACTn, sACTn.take()))
                        row_evac[r] = sACTn.n
                        cw = [(sACTn, row_evac[r])]
                        if h == 0:
                            I(p, "vector", lambda v: v.tensor_copy(qmx[:], nqr[:, 0:512]),
                              waits=cw, inc=(sDVE, sDVE.take()))
                        else:
                            I(p, "vector", lambda v, j=j:
                              v.tensor_max(qmx[:], qmx[:],
                                           nqr[:, j * 512:(j + 1) * 512]),
                              waits=cw, inc=(sDVE, sDVE.take()))
                        max_acc[h] = sDVE.n
                    I(p, "vector", lambda v: v.reduce_max(scr[:, 1:2], qmx[:], axis=AX.X))
                    I(p, "vector", lambda v: v.tensor_add(scr[:, 2:3], scr[:, 0:1],
                                                          scr[:, 1:2]))
                    cw = [(sPEn, bc_mm_prev)] if bc_mm_prev else []
                    I(p, "vector", lambda v:
                      v.tensor_scalar(colC[0:1, 0:1], scr[:, 2:3], -0.5 * SCALE, None,
                                      op0=ALU.mult),
                      waits=cw, inc=(sDVE, sDVE.take()))
                    cC_done = sDVE.n
                    cw = [(sDVE, cC_done)] + [(sACTn, row_evac[r]) for r in range(3)]
                    I(p, "tensor", lambda t: t.matmul(ps_pre[:, 0:1], onesM[:], colC[:],
                                                      start=True, stop=True),
                      waits=cw, inc=(sPEn, sPEn.take()))
                    bc_mm_prev = sPEn.n
                    I(p, "scalar", lambda a, c=qb % 2:
                      a.copy(bcolT[:, c:c + 1], ps_pre[:, 0:1]),
                      waits=[(sPEn, bc_mm_prev)], inc=(sACTn, sACTn.take()))

                    for h in range(NH):
                        gidx = qb * NH + h
                        kv = h // G
                        qsl = Qb[:, h * 512:(h + 1) * 512]
                        sumr0 = (h % 2) * 32
                        slope_h = slopes_f[h]
                        sc_done = {}
                        stt_done = {}
                        j2p = {}

                        def em_scores(kt, qsl=qsl, kv=kv):
                            b = kt % 3
                            cw = []
                            if bank_last_stt[b]:
                                cw.append((sDVEb, bank_last_stt[b]))
                            lhsK = Ksb[:, kv * S + kt * 128:kv * S + kt * 128 + 128]
                            I(p, "tensor", lambda t, o=pss[b][:], l=lhsK, r=qsl:
                              t.matmul(o, l, r, start=True, stop=True),
                              waits=cw, inc=(sPEs, sPEs.take()))
                            sc_done[kt] = sPEs.n

                        def em_stt(kt, sl=slope_h):
                            b = kt % 3
                            j = (kt // 2) % 2
                            half = kt % 2
                            cw = [(sPEs, sc_done[kt])]
                            if half == 0 and stile_last_exp[j]:
                                cw.append((sACTs, stile_last_exp[j]))
                            I(p, "vector", lambda v, j=j, half=half, b=b,
                              bD=Db[:, kt * 512:(kt + 1) * 512], sc=SCALE / sl:
                              v.scalar_tensor_tensor(
                                  stile[j][:, half * 512:half * 512 + 512],
                                  pss[b][:], sc, bD, op0=ALU.mult, op1=ALU.subtract),
                              waits=cw, inc=(sDVEb, sDVEb.take()))
                            bank_last_stt[b] = sDVEb.n
                            stt_done[kt] = sDVEb.n

                        def em_exp(pj, sl=slope_h, c=qb % 2):
                            j = pj % 2
                            cw = [(sDVEb, stt_done[2 * pj + 1])]
                            if pt_last_pv[j]:
                                cw.append((sPEp, pt_last_pv[j]))
                            I(p, "scalar", lambda a, j=j, c=c:
                              a.activation(pt[j][:], stile[j][:], AF.Exp, scale=sl,
                                           bias=bcolT[:, c:c + 1]),
                              waits=cw, inc=(sACTs, sACTs.take()))
                            stile_last_exp[j] = sACTs.n
                            j2p[pj] = sACTs.n

                        def em_pv(kt, KT=KT, kv=kv, po=(h % 2), gidx=gidx,
                                  sumr0=sumr0):
                            j = (kt // 2) % 2
                            half = kt % 2
                            cw = []
                            if half == 0:
                                cw.append((sACTs, j2p[kt // 2]))
                            if kt == 0:
                                if ao_done.get(gidx - 2):
                                    cw.append((sDVEo, ao_done[gidx - 2]))
                                if recip_done.get(gidx - 2):
                                    cw.append((sDVEr, recip_done[gidx - 2]))
                            lhsV = Vsb[:, kt * 1024 + kv * 128:
                                       kt * 1024 + kv * 128 + 128]
                            rp = pt[j][:, half * 512:half * 512 + 512]
                            I(p, "tensor", lambda t, o=pso[po][:], l=lhsV, r=rp,
                              c0=(kt == 0), c1=(kt == KT - 1):
                              t.matmul(o, l, r, start=c0, stop=c1), waits=cw)
                            I(p, "tensor", lambda t, r=rp,
                              c0=(kt == 0), c1=(kt == KT - 1):
                              t.matmul(ps_sumB[sumr0:sumr0 + 1, :], onescol, r,
                                       start=c0, stop=c1),
                              inc=(sPEp, sPEp.take()))
                            if half == 1:
                                pt_last_pv[j] = sPEp.n
                            if kt == KT - 1:
                                pv_all[gidx] = sPEp.n

                        em_scores(0)
                        em_scores(1)
                        em_stt(0)
                        em_stt(1)
                        em_exp(0)
                        em_tail(gidx - 1)
                        for pj in range(1, KT // 2):
                            em_scores(2 * pj)
                            em_scores(2 * pj + 1)
                            em_stt(2 * pj)
                            em_stt(2 * pj + 1)
                            em_exp(pj)
                            em_pv(2 * pj - 2)
                            em_pv(2 * pj - 1)
                        em_pv(KT - 2)
                        em_pv(KT - 1)
                        last_scores = sPEs.n
                        last_stt = sDVEb.n
                em_tail(TB * NH - 1)
                p.add("sync", lambda eng, v=sAo[0].n, h=sAo[0].h: eng.wait_ge(h, v))
                p.add("sync", lambda eng, v=sAo[1].n, h=sAo[1].h: eng.wait_ge(h, v))
                p.emit(nc)

        if n_phases >= 4:
            big_ctx = contextlib.ExitStack()
            with big_ctx:
                big = big_ctx.enter_context(nc.sbuf_tensor("big2", [128, HC * S], BF16))
                with contextlib.ExitStack() as _es:
                    wob0 = _es.enter_context(nc.sbuf_tensor("wob0", [128, HC * 128], BF16))
                    wob1 = _es.enter_context(nc.sbuf_tensor("wob1", [128, HC * 128], BF16))
                    stg0b = _es.enter_context(nc.sbuf_tensor("stg0b", [128, 2048], BF16))
                    stg1b = _es.enter_context(nc.sbuf_tensor("stg1b", [128, 2048], BF16))
                    ps_p0 = _es.enter_context(nc.psum_tensor("ps_p0", [128, 512], F32))
                    ps_p1 = _es.enter_context(nc.psum_tensor("ps_p1", [128, 512], F32))
                    ps_p2 = _es.enter_context(nc.psum_tensor("ps_p2", [128, 512], F32))
                    ps_p3 = _es.enter_context(nc.psum_tensor("ps_p3", [128, 512], F32))
                    wob = [wob0, wob1]
                    stg = [stg0b, stg1b]
                    psq = [ps_p0, ps_p1, ps_p2, ps_p3]
                    p = Plan()
                    for g4 in range(4):
                        src = ao_s[g4 * 8:(g4 + 1) * 8, :, :].rearrange("c p t -> p c t")
                        I(p, "sync", lambda s, d=sp3(big[:, g4 * 8 * S:(g4 + 1) * 8 * S], S), s2=src:
                          s.dma_start(d, s2), inc=(sSP, sSP.take(16)))
                    ao_ld = sSP.n
                    wfree = [0, 0]
                    pe_done = {}
                    evac = {}
                    store = {}
                    for oc in range(HC):
                        par = oc % 2
                        lw = [(sACT, evac[(oc - 2, TB - 1)])] if oc >= 2 else []
                        src = wo_t[oc:oc + 1].rearrange("o p c i -> (o p) (c i)")
                        I(p, "gpsimd", lambda g, d=wob[par][:], s=src: g.dma_start(d, s),
                          waits=lw, inc=(sWq[par], sWq[par].take(16)))
                        wld = (sWq[par], sWq[par].n)
                        for kc in range(HC):
                            for tb in range(TB):
                                cw = []
                                if kc == 0:
                                    cw = [wld, (sSP, ao_ld)]
                                    if oc >= 1:
                                        cw.append((sACT, evac[(oc - 1, tb)]))
                                lhs = wob[par][:, kc * 128:(kc + 1) * 128]
                                rhs = big[:, kc * S + tb * 512:kc * S + tb * 512 + 512]
                                inc = None
                                if kc == HC - 1:
                                    inc = (sPE, sPE.take())
                                I(p, "tensor", lambda t, o=psq[tb][:], l=lhs, r=rhs,
                                  c0=(kc == 0), c1=(kc == HC - 1):
                                  t.matmul(o, l, r, start=c0, stop=c1),
                                  waits=cw, inc=inc)
                                if kc == HC - 1:
                                    pe_done[(oc, tb)] = sPE.n
                        for tb in range(TB):
                            cw = [(sPE, pe_done[(oc, tb)])]
                            if tb == 0 and oc >= 2:
                                cw.append(store[oc - 2])
                            dst = stg[par][:, tb * 512:(tb + 1) * 512]
                            I(p, "scalar", lambda a, d=dst, s=psq[tb][:]: a.copy(d, s),
                              waits=cw, inc=(sACT, sACT.take()))
                            evac[(oc, tb)] = sACT.n
                        tgt = pj_s[oc:oc + 1].rearrange("o p t -> (o p) t")
                        I(p, "sync", lambda s, d=tgt, s2=stg[par][:]: s.dma_start(d, s2),
                          waits=[(sACT, evac[(oc, TB - 1)])],
                          inc=(sSt[par], sSt[par].take(16)))
                        store[oc] = (sSt[par], sSt[par].n)
                    p.emit(nc)

                if n_phases >= 5:
                    with contextlib.ExitStack() as _es:
                        pblock = _es.enter_context(nc.sbuf_tensor("pblock", [128, HC * 512], BF16))
                        xst0 = _es.enter_context(nc.sbuf_tensor("xst0", [128, 512], F32))
                        xst1 = _es.enter_context(nc.sbuf_tensor("xst1", [128, 512], F32))
                        x1c0 = _es.enter_context(nc.sbuf_tensor("x1c0", [128, 512], F32))
                        x1c1 = _es.enter_context(nc.sbuf_tensor("x1c1", [128, 512], F32))
                        sq3a = _es.enter_context(nc.sbuf_tensor("sq3a", [128, 512], BF16))
                        sq3b = _es.enter_context(nc.sbuf_tensor("sq3b", [128, 512], BF16))
                        t1c = _es.enter_context(nc.sbuf_tensor("t1c", [128, 512], BF16))
                        t2c = _es.enter_context(nc.sbuf_tensor("t2c", [128, 512], F32))
                        rows3 = _es.enter_context(nc.sbuf_tensor("rows3", [1, 512 * 3], F32))
                        rowM3 = _es.enter_context(nc.sbuf_tensor("rowM3", [128, 512], BF16))
                        rowR3 = _es.enter_context(nc.sbuf_tensor("rowR3", [128, 512], BF16))
                        Ma = _es.enter_context(nc.sbuf_tensor("Ma", [128, 512], BF16))
                        Ra = _es.enter_context(nc.sbuf_tensor("Ra", [128, 512], BF16))
                        M2 = _es.enter_context(nc.sbuf_tensor("M2", [128, 512], BF16))
                        R2 = _es.enter_context(nc.sbuf_tensor("R2", [128, 512], BF16))
                        ps_t1 = _es.enter_context(nc.psum_tensor("ps_t1", [1, 512], F32))
                        ps_t2 = _es.enter_context(nc.psum_tensor("ps_t2", [1, 512], F32))
                        ps_bM2 = _es.enter_context(nc.psum_tensor("ps_bM2", [128, 512], F32))
                        ps_bR2 = _es.enter_context(nc.psum_tensor("ps_bR2", [128, 512], F32))
                        sq3 = [sq3a, sq3b]
                        xst = [xst0, xst1]
                        x1c = [x1c0, x1c1]
                        rA = rows3[:, 0:512]
                        rB = rows3[:, 512:1024]
                        rC = rows3[:, 1024:1536]
                        p = Plan()
                        I(p, "gpsimd", lambda g: g.memset(rowM3[:], 0.0))
                        I(p, "gpsimd", lambda g: g.memset(rowR3[:], 0.0),
                          inc=(sPL, sPL.take()))
                        pad_ready = sPL.n
                        rows_evac = 0
                        h2_done = 0
                        bc_mm = 0
                        bc_evac = 0
                        x1m_prev = 0
                        x1done = {}
                        x1_store = {}
                        cast_done = {}
                        h2done = {}
                        for tb in range(TB):
                            t0 = tb * 512
                            lw = [(sDVE, h2_done)] if tb else []
                            for g4 in range(4):
                                src = pj_s[g4 * 8:(g4 + 1) * 8, :, t0:t0 + 512].rearrange(
                                    "c p t -> p c t")
                                I(p, "sync",
                                  lambda s, d=sp3(pblock[:, g4 * 4096:(g4 + 1) * 4096], 512), s2=src:
                                  s.dma_start(d, s2),
                                  waits=lw if g4 == 0 else (), inc=(sSP, sSP.take(16)))
                            blk_ld = sSP.n
                            pe_c = {}
                            for c in range(HC):
                                cw = [(sSP, blk_ld)] if c == 0 else []
                                if c >= 2:
                                    cw.append((sPE, pe_c[c - 2]))
                                psl = pblock[:, c * 512:(c + 1) * 512]
                                I(p, "vector",
                                  lambda v, a=psl, d=sq3[c % 2][:]: v.tensor_mul(d, a, a),
                                  waits=cw, inc=(sDVE, sDVE.take()))
                                sqd = sDVE.n
                                cw = [(sDVE, sqd)]
                                if c == 0 and rows_evac:
                                    cw.append((sACT, rows_evac))
                                I(p, "tensor", lambda t, r=psl,
                                  c0=(c == 0), c1=(c == HC - 1):
                                  t.matmul(ps_t1[:], onescol, r, start=c0, stop=c1),
                                  waits=cw)
                                I(p, "tensor", lambda t, r=sq3[c % 2][:],
                                  c0=(c == 0), c1=(c == HC - 1):
                                  t.matmul(ps_t2[:], onescol, r, start=c0, stop=c1),
                                  inc=(sPE, sPE.take()))
                                pe_c[c] = sPE.n
                            stats_done = sPE.n
                            rows_evac = ln_rows(p, ps_t1, ps_t2, rA, rB, rC,
                                                [(sPE, stats_done)])
                            rw = [(sACT, rows_evac)] + \
                                 ([(sPE, bc_mm)] if tb else [(sPL, pad_ready)])
                            I(p, "scalar", lambda a: a.copy(rowM3[0:1, :], rA), waits=rw)
                            I(p, "scalar", lambda a: a.copy(rowR3[0:1, :], rB),
                              inc=(sACT, sACT.take()))
                            rows_b = sACT.n
                            bw = [(sACT, rows_b)] + ([(sACT, bc_evac)] if tb else [])
                            I(p, "tensor", lambda t: t.matmul(
                                ps_bM2[:], onesM[:], rowM3[:], start=True, stop=True),
                              waits=bw)
                            I(p, "tensor", lambda t: t.matmul(
                                ps_bR2[:], onesM[:], rowR3[:], start=True, stop=True),
                              inc=(sPE, sPE.take()))
                            bc_mm = sPE.n
                            ew = [(sPE, bc_mm)] + ([(sDVE, x1m_prev)] if tb else [])
                            I(p, "scalar", lambda a: a.copy(Ma[:], ps_bM2[:]), waits=ew)
                            I(p, "scalar", lambda a: a.copy(Ra[:], ps_bR2[:]),
                              inc=(sACT, sACT.take()))
                            bc_evac = sACT.n
                            pe_c2 = {}
                            for c in range(HC):
                                gsl = tb * HC + c
                                j = c % 2
                                srcx = x_t[c:c + 1, :, t0:t0 + 512].rearrange(
                                    "o p t -> (o p) t")
                                lwx = []
                                if gsl >= 2:
                                    lwx.append((sDVE, x1done[gsl - 2]))
                                I(p, "sync", lambda s, d=xst[j][:], s2=srcx:
                                  s.dma_start(d, s2),
                                  waits=lwx, inc=(sXl[j], sXl[j].take(16)))
                                xld = (sXl[j], sXl[j].n)
                                psl = pblock[:, c * 512:(c + 1) * 512]
                                cw = [(sACT, bc_evac)] if c == 0 else []
                                I(p, "vector", lambda v, a=psl: v.tensor_sub(t1c[:], a, Ma[:]),
                                  waits=cw)
                                I(p, "vector", lambda v: v.tensor_mul(t2c[:], t1c[:], Ra[:]))
                                cw = [xld]
                                if gsl >= 2:
                                    cw.append(x1_store[gsl - 2])
                                    cw.append((sACT, cast_done[gsl - 2]))
                                I(p, "vector", lambda v, x=xst[j][:], d=x1c[j][:]:
                                  v.tensor_add(d, t2c[:], x),
                                  waits=cw, inc=(sDVE, sDVE.take()))
                                x1done[gsl] = sDVE.n
                                dstx1 = x1_s[c:c + 1, :, t0:t0 + 512].rearrange(
                                    "o p t -> (o p) t")
                                I(p, "gpsimd", lambda g, d=dstx1, s=x1c[j][:]:
                                  g.dma_start(d, s),
                                  waits=[(sDVE, x1done[gsl])],
                                  inc=(sXs[j], sXs[j].take(16)))
                                x1_store[gsl] = (sXs[j], sXs[j].n)
                                cwc = [(sDVE, x1done[gsl])]
                                if c >= 1:
                                    cwc.append((sPE, pe_c2[c - 1]))
                                I(p, "scalar", lambda a, s=x1c[j][:], d=psl: a.copy(d, s),
                                  waits=cwc, inc=(sACT, sACT.take()))
                                cast_done[gsl] = sACT.n
                                cd = sACT.n
                                I(p, "vector", lambda v, d=sq3[j][:], a=psl:
                                  v.tensor_mul(d, a, a),
                                  waits=[(sACT, cd)], inc=(sDVE, sDVE.take()))
                                sqd = sDVE.n
                                I(p, "tensor", lambda t, r=psl, c0=(c == 0), c1=(c == HC - 1):
                                  t.matmul(ps_t1[:], onescol, r, start=c0, stop=c1),
                                  waits=[(sDVE, sqd)])
                                I(p, "tensor", lambda t, r=sq3[j][:],
                                  c0=(c == 0), c1=(c == HC - 1):
                                  t.matmul(ps_t2[:], onescol, r, start=c0, stop=c1),
                                  inc=(sPE, sPE.take()))
                                pe_c2[c] = sPE.n
                            x1m_prev = sDVE.n
                            stats2_done = sPE.n
                            rows_evac = ln_rows(p, ps_t1, ps_t2, rA, rB, rC,
                                                [(sPE, stats2_done)])
                            rw = [(sACT, rows_evac), (sPE, bc_mm)]
                            I(p, "scalar", lambda a: a.copy(rowM3[0:1, :], rA), waits=rw)
                            I(p, "scalar", lambda a: a.copy(rowR3[0:1, :], rB),
                              inc=(sACT, sACT.take()))
                            rows_b = sACT.n
                            bw = [(sACT, rows_b), (sACT, bc_evac)]
                            I(p, "tensor", lambda t: t.matmul(
                                ps_bM2[:], onesM[:], rowM3[:], start=True, stop=True),
                              waits=bw)
                            I(p, "tensor", lambda t: t.matmul(
                                ps_bR2[:], onesM[:], rowR3[:], start=True, stop=True),
                              inc=(sPE, sPE.take()))
                            bc_mm = sPE.n
                            ew = [(sPE, bc_mm)] + ([(sDVE, h2_done)] if tb else [])
                            I(p, "scalar", lambda a: a.copy(M2[:], ps_bM2[:]), waits=ew)
                            I(p, "scalar", lambda a: a.copy(R2[:], ps_bR2[:]),
                              inc=(sACT, sACT.take()))
                            bc_evac = sACT.n
                            for c in range(HC):
                                gsl = tb * HC + c
                                psl = pblock[:, c * 512:(c + 1) * 512]
                                cw = [(sACT, bc_evac)] if c == 0 else []
                                dst = big[:, c * S + t0:c * S + t0 + 512]
                                I(p, "vector", lambda v, a=psl:
                                  v.tensor_sub(t1c[:], a, M2[:]), waits=cw)
                                I(p, "vector", lambda v, d=dst: v.tensor_mul(d, t1c[:], R2[:]),
                                  inc=(sDVE, sDVE.take()))
                                h2done[gsl] = sDVE.n
                            h2_done = sDVE.n
                        if debug:
                            I(p, "sync", lambda s: s.dma_start(h2_d, big[:]),
                              waits=[(sDVE, h2_done)], inc=(sSP, sSP.take(16)))
                            p.add("sync", lambda eng, v=sSP.n, h=sSP.h: eng.wait_ge(h, v))
                        p.emit(nc)

                if n_phases >= 6:
                    with contextlib.ExitStack() as _es:
                        bact = _es.enter_context(nc.sbuf_tensor("bact", [128, 8 * S], BF16))
                        wgb0 = _es.enter_context(nc.sbuf_tensor("wgb0", [128, 16 * 128], BF16))
                        wgb1 = _es.enter_context(nc.sbuf_tensor("wgb1", [128, 16 * 128], BF16))
                        wub0 = _es.enter_context(nc.sbuf_tensor("wub0", [128, 16 * 128], BF16))
                        wub1 = _es.enter_context(nc.sbuf_tensor("wub1", [128, 16 * 128], BF16))
                        wdb0 = _es.enter_context(nc.sbuf_tensor("wdb0", [128, 8 * 128], BF16))
                        wdb1 = _es.enter_context(nc.sbuf_tensor("wdb1", [128, 8 * 128], BF16))
                        s1a = _es.enter_context(nc.sbuf_tensor("s1a", [128, 512], BF16))
                        s1b = _es.enter_context(nc.sbuf_tensor("s1b", [128, 512], BF16))
                        pst0 = _es.enter_context(nc.sbuf_tensor("pst0", [128, 1024], F32))
                        pst1 = _es.enter_context(nc.sbuf_tensor("pst1", [128, 1024], F32))
                        ps_g0 = _es.enter_context(nc.psum_tensor("ps_g0", [128, 512], F32))
                        ps_g1 = _es.enter_context(nc.psum_tensor("ps_g1", [128, 512], F32))
                        ps_g2 = _es.enter_context(nc.psum_tensor("ps_g2", [128, 512], F32))
                        ps_g3 = _es.enter_context(nc.psum_tensor("ps_g3", [128, 512], F32))
                        ps_u0 = _es.enter_context(nc.psum_tensor("ps_u0", [128, 512], F32))
                        ps_u1 = _es.enter_context(nc.psum_tensor("ps_u1", [128, 512], F32))
                        ps_u2 = _es.enter_context(nc.psum_tensor("ps_u2", [128, 512], F32))
                        ps_u3 = _es.enter_context(nc.psum_tensor("ps_u3", [128, 512], F32))
                        wgb = [wgb0, wgb1]
                        wub = [wub0, wub1]
                        wdb = [wdb0, wdb1]
                        s1 = [s1a, s1b]
                        pst = [pst0, pst1]
                        gb = [ps_g0, ps_g1, ps_g2, ps_g3]
                        ub = [ps_u0, ps_u1, ps_u2, ps_u3]
                        dbanks = [gb, ub]
                        p = Plan()
                        silu_cnt = {}
                        act_cnt = {}
                        g_half = {}
                        u_half = {}
                        wgld = {}
                        wuld = {}
                        pd_evac = {}
                        cast_cnt = {}
                        wdb_free = {}
                        pst_free = [0, 0]
                        pd_last_super = 0
                        pd_evac_tail = 0
                        for isup, sup in enumerate(SUPERS):
                            nIc = len(sup)
                            for icl, Ic in enumerate(sup):
                                for half in range(2):
                                    u = Ic * 2 + half
                                    par = u % 2
                                    ksl = slice(half * 16, half * 16 + 16)
                                    lw = [(sPEg, g_half[u - 2])] if u >= 2 else []
                                    I(p, "gpsimd", lambda g, d=wgb[par][:],
                                      s=wg_t[Ic:Ic + 1, :, ksl, :].rearrange(
                                          "o p c i -> (o p) (c i)"):
                                      g.dma_start(d, s), waits=lw,
                                      inc=(sWg[par], sWg[par].take(16)))
                                    wgld[u] = (sWg[par], sWg[par].n)
                                    lw = [(sPEu, u_half[u - 2])] if u >= 2 else []
                                    I(p, "gpsimd", lambda g, d=wub[par][:],
                                      s=wu_t[Ic:Ic + 1, :, ksl, :].rearrange(
                                          "o p c i -> (o p) (c i)"):
                                      g.dma_start(d, s), waits=lw,
                                      inc=(sWu[par], sWu[par].take(16)))
                                    wuld[u] = (sWu[par], sWu[par].n)
                                for kc in range(HC):
                                    u = Ic * 2 + kc // 16
                                    lhs = wgb[u % 2][:, (kc % 16) * 128:(kc % 16) * 128 + 128]
                                    for tb in range(TB):
                                        cw = []
                                        if kc % 16 == 0 and tb == 0:
                                            cw.append(wgld[u])
                                        if kc == 0:
                                            prev = silu_cnt.get((Ic - 1, tb), 0)
                                            if prev:
                                                cw.append((sACTs, prev))
                                            if icl == 0 and isup >= 1 and tb == 0:
                                                cw.append((sACTd, pd_evac_tail))
                                        rhs = big[:, kc * S + tb * 512:kc * S + tb * 512 + 512]
                                        inc = None
                                        if (kc == 15 and tb == TB - 1) or kc == HC - 1:
                                            inc = (sPEg, sPEg.take())
                                        I(p, "tensor", lambda t, o=gb[tb][:], l=lhs, r=rhs,
                                          c0=(kc == 0), c1=(kc == HC - 1):
                                          t.matmul(o, l, r, start=c0, stop=c1),
                                          waits=cw, inc=inc)
                                        if kc == 15 and tb == TB - 1:
                                            g_half[Ic * 2] = sPEg.n
                                        if kc == HC - 1:
                                            if tb == TB - 1:
                                                g_half[Ic * 2 + 1] = sPEg.n
                                            silu_cnt[("pe", Ic, tb)] = sPEg.n
                                for kc in range(HC):
                                    u = Ic * 2 + kc // 16
                                    lhs = wub[u % 2][:, (kc % 16) * 128:(kc % 16) * 128 + 128]
                                    for tb in range(TB):
                                        cw = []
                                        if kc % 16 == 0 and tb == 0:
                                            cw.append(wuld[u])
                                        if kc == 0:
                                            prev = act_cnt.get((Ic - 1, tb), 0)
                                            if prev:
                                                cw.append((sDVEa, prev))
                                            if icl == 0 and isup >= 1 and tb == 0:
                                                cw.append((sACTd, pd_evac_tail))
                                        rhs = big[:, kc * S + tb * 512:kc * S + tb * 512 + 512]
                                        inc = None
                                        if (kc == 15 and tb == TB - 1) or kc == HC - 1:
                                            inc = (sPEu, sPEu.take())
                                        I(p, "tensor", lambda t, o=ub[tb][:], l=lhs, r=rhs,
                                          c0=(kc == 0), c1=(kc == HC - 1):
                                          t.matmul(o, l, r, start=c0, stop=c1),
                                          waits=cw, inc=inc)
                                        if kc == 15 and tb == TB - 1:
                                            u_half[Ic * 2] = sPEu.n
                                        if kc == HC - 1:
                                            if tb == TB - 1:
                                                u_half[Ic * 2 + 1] = sPEu.n
                                            act_cnt[("pe", Ic, tb)] = sPEu.n
                                for tb in range(TB):
                                    js = tb % 2
                                    cw = [(sPEg, silu_cnt[("pe", Ic, tb)])]
                                    prevd = act_cnt.get((Ic, tb - 2) if tb >= 2 else
                                                        (Ic - 1, tb + 2), 0)
                                    if prevd:
                                        cw.append((sDVEa, prevd))
                                    I(p, "scalar", lambda a, d=s1[js][:], s=gb[tb][:]:
                                      a.activation(d, s, AF.Silu),
                                      waits=cw, inc=(sACTs, sACTs.take()))
                                    silu_cnt[(Ic, tb)] = sACTs.n
                                    cw = [(sACTs, silu_cnt[(Ic, tb)]),
                                          (sPEu, act_cnt[("pe", Ic, tb)])]
                                    if isup >= 1:
                                        cw.append((sPEd, pd_last_super))
                                    dst = bact[:, icl * S + tb * 512:icl * S + tb * 512 + 512]
                                    I(p, "vector", lambda v, d=dst, a=s1[js][:], b=ub[tb][:]:
                                      v.tensor_mul(d, a, b),
                                      waits=cw, inc=(sDVEa, sDVEa.take()))
                                    act_cnt[(Ic, tb)] = sDVEa.n
                            act_all = sDVEa.n

                            def em_wd(hc, isup=isup, sup=sup, nIc=nIc):
                                par = hc % 2
                                gw = isup * HC + hc
                                lw = [(sPEd, wdb_free[gw - 2])] if gw >= 2 else []
                                src = wd_t[hc:hc + 1, :, sup[0]:sup[0] + nIc, :].rearrange(
                                    "o p c i -> (o p) (c i)")
                                I(p, "gpsimd", lambda g, d=wdb[par][:, 0:nIc * 128], s2=src:
                                  g.dma_start(d, s2), waits=lw,
                                  inc=(sWd[par], sWd[par].take(16)))
                                cast_cnt[gw] = (sWd[par], sWd[par].n)

                            em_wd(0)
                            for hc in range(HC):
                                par = hc % 2
                                gw = isup * HC + hc
                                if hc + 1 < HC:
                                    em_wd(hc + 1)
                                banks = dbanks[hc % 2]
                                d_done = {}
                                for icl in range(nIc):
                                    lhs = wdb[par][:, icl * 128:(icl + 1) * 128]
                                    for tb in range(TB):
                                        dslot = gw * TB + tb
                                        cw = []
                                        if icl == 0:
                                            if tb == 0:
                                                cw.append(cast_cnt[gw])
                                                if hc == 0:
                                                    cw.append((sDVEa, act_all))
                                            prev = pd_evac.get(dslot - 2 * TB, 0)
                                            if prev:
                                                cw.append((sACTd, prev))
                                        rhs = bact[:, icl * S + tb * 512:icl * S + tb * 512 + 512]
                                        inc = None
                                        if icl == nIc - 1:
                                            inc = (sPEd, sPEd.take())
                                        I(p, "tensor", lambda t, o=banks[tb][:], l=lhs, r=rhs,
                                          c0=(icl == 0), c1=(icl == nIc - 1):
                                          t.matmul(o, l, r, start=c0, stop=c1),
                                          waits=cw, inc=inc)
                                        if icl == nIc - 1:
                                            d_done[tb] = sPEd.n
                                wdb_free[gw] = sPEd.n
                                for tb in range(TB):
                                    dslot = gw * TB + tb
                                    pp = (dslot // 2) % 2
                                    cw = [(sPEd, d_done[tb])]
                                    if tb % 2 == 0 and pst_free[pp]:
                                        cw.append(pst_free[pp])
                                    I(p, "vector",
                                      lambda v, d=pst[pp][:, (tb % 2) * 512:(tb % 2) * 512 + 512],
                                      s=banks[tb][:]: v.tensor_copy(d, s),
                                      waits=cw, inc=(sACTd, sACTd.take()))
                                    pd_evac[dslot] = sACTd.n
                                    if tb % 2 == 1:
                                        dsta = ac_s[hc:hc + 1, :,
                                                    (tb // 2) * 1024:(tb // 2) * 1024 + 1024
                                                    ].rearrange("o p t -> (o p) t")
                                        op = ALU.add if isup >= 1 else ALU.bypass
                                        I(p, "gpsimd", lambda g, d=dsta, s=pst[pp][:], o=op:
                                          g.dma_start(d, s, accum_op=o),
                                          waits=[(sACTd, pd_evac[dslot])],
                                          inc=(sAc[pp], sAc[pp].take(16)))
                                        pst_free[pp] = (sAc[pp], sAc[pp].n)
                            pd_last_super = sPEd.n
                            pd_evac_tail = sACTd.n
                        p.add("gpsimd", lambda eng, v=sAc[0].n, h=sAc[0].h: eng.wait_ge(h, v))
                        p.add("gpsimd", lambda eng, v=sAc[1].n, h=sAc[1].h: eng.wait_ge(h, v))
                        p.emit(nc)

        if n_phases >= 7:
            with contextlib.ExitStack() as _es:
                outb = _es.enter_context(nc.sbuf_tensor("outb", [128, HC * 512], F32))
                x1b6 = _es.enter_context(nc.sbuf_tensor("x1b6", [128, HC * 512], F32))
                ob16 = _es.enter_context(nc.sbuf_tensor("ob16", [128, 512], BF16))
                sq6a = _es.enter_context(nc.sbuf_tensor("sq6a", [128, 512], BF16))
                sq6b = _es.enter_context(nc.sbuf_tensor("sq6b", [128, 512], BF16))
                t6 = _es.enter_context(nc.sbuf_tensor("t6", [128, 512], F32))
                rows6 = _es.enter_context(nc.sbuf_tensor("rows6", [1, 512 * 3], F32))
                rowM6 = _es.enter_context(nc.sbuf_tensor("rowM6", [128, 512], BF16))
                rowR6 = _es.enter_context(nc.sbuf_tensor("rowR6", [128, 512], BF16))
                M6 = _es.enter_context(nc.sbuf_tensor("M6", [128, 512], F32))
                R6 = _es.enter_context(nc.sbuf_tensor("R6", [128, 512], F32))
                ps_61 = _es.enter_context(nc.psum_tensor("ps_61", [1, 512], F32))
                ps_62 = _es.enter_context(nc.psum_tensor("ps_62", [1, 512], F32))
                ps_6M = _es.enter_context(nc.psum_tensor("ps_6M", [128, 512], F32))
                ps_6R = _es.enter_context(nc.psum_tensor("ps_6R", [128, 512], F32))
                sq6 = [sq6a, sq6b]
                rA = rows6[:, 0:512]
                rB = rows6[:, 512:1024]
                rC = rows6[:, 1024:1536]
                p = Plan()
                I(p, "gpsimd", lambda g: g.memset(rowM6[:], 0.0))
                I(p, "gpsimd", lambda g: g.memset(rowR6[:], 0.0), inc=(sPL, sPL.take()))
                pad_ready = sPL.n
                rows_evac = 0
                y_math = 0
                bc_mm = 0
                bc_evac = 0
                y_store = 0
                for tb in range(TB):
                    t0 = tb * 512
                    lw = [(sY, y_store)] if tb else []
                    for g4 in range(4):
                        src = ac_s[g4 * 8:(g4 + 1) * 8, :, t0:t0 + 512].rearrange(
                            "c p t -> p c t")
                        I(p, "sync",
                          lambda s, d=sp3(outb[:, g4 * 4096:(g4 + 1) * 4096], 512), s2=src:
                          s.dma_start(d, s2),
                          waits=lw if g4 == 0 else (), inc=(sSP, sSP.take(16)))
                        src2 = x1_s[g4 * 8:(g4 + 1) * 8, :, t0:t0 + 512].rearrange(
                            "c p t -> p c t")
                        I(p, "sync",
                          lambda s, d=sp3(x1b6[:, g4 * 4096:(g4 + 1) * 4096], 512), s2=src2:
                          s.dma_start(d, s2), inc=(sSP, sSP.take(16)))
                    blk_ld = sSP.n
                    pe_c = {}
                    dve_c = {}
                    for c in range(HC):
                        osl = outb[:, c * 512:(c + 1) * 512]
                        cw = [(sSP, blk_ld)] if c == 0 else \
                            [(sDVE, dve_c[c - 1]), (sPE, pe_c[c - 1])]
                        I(p, "scalar", lambda a, s=osl: a.copy(ob16[:], s),
                          waits=cw, inc=(sACT, sACT.take()))
                        cd = sACT.n
                        I(p, "vector", lambda v, d=sq6[c % 2][:]:
                          v.tensor_mul(d, ob16[:], ob16[:]),
                          waits=[(sACT, cd)], inc=(sDVE, sDVE.take()))
                        dve_c[c] = sDVE.n
                        cw = [(sDVE, dve_c[c])]
                        if c == 0 and rows_evac:
                            cw.append((sACT, rows_evac))
                        I(p, "tensor", lambda t, c0=(c == 0), c1=(c == HC - 1):
                          t.matmul(ps_61[:], onescol, ob16[:], start=c0, stop=c1),
                          waits=cw)
                        I(p, "tensor", lambda t, r=sq6[c % 2][:],
                          c0=(c == 0), c1=(c == HC - 1):
                          t.matmul(ps_62[:], onescol, r, start=c0, stop=c1),
                          inc=(sPE, sPE.take()))
                        pe_c[c] = sPE.n
                    stats_done = sPE.n
                    rows_evac = ln_rows(p, ps_61, ps_62, rA, rB, rC, [(sPE, stats_done)])
                    rw = [(sACT, rows_evac)] + \
                         ([(sPE, bc_mm)] if tb else [(sPL, pad_ready)])
                    I(p, "scalar", lambda a: a.copy(rowM6[0:1, :], rA), waits=rw)
                    I(p, "scalar", lambda a: a.copy(rowR6[0:1, :], rB),
                      inc=(sACT, sACT.take()))
                    rows_b = sACT.n
                    bw = [(sACT, rows_b)] + ([(sACT, bc_evac)] if tb else [])
                    I(p, "tensor", lambda t: t.matmul(ps_6M[:], onesM[:], rowM6[:],
                                                      start=True, stop=True), waits=bw)
                    I(p, "tensor", lambda t: t.matmul(ps_6R[:], onesM[:], rowR6[:],
                                                      start=True, stop=True),
                      inc=(sPE, sPE.take()))
                    bc_mm = sPE.n
                    ew = [(sPE, bc_mm)] + ([(sDVE, y_math)] if tb else [])
                    I(p, "scalar", lambda a: a.copy(M6[:], ps_6M[:]), waits=ew)
                    I(p, "scalar", lambda a: a.copy(R6[:], ps_6R[:]),
                      inc=(sACT, sACT.take()))
                    bc_evac = sACT.n
                    for c in range(HC):
                        cw = [(sACT, bc_evac)] if c == 0 else []
                        osl = outb[:, c * 512:(c + 1) * 512]
                        xsl = x1b6[:, c * 512:(c + 1) * 512]
                        I(p, "vector", lambda v, a=osl: v.tensor_sub(t6[:], a, M6[:]),
                          waits=cw)
                        I(p, "vector", lambda v: v.tensor_mul(t6[:], t6[:], R6[:]))
                        inc = (sDVE, sDVE.take()) if c == HC - 1 else None
                        I(p, "vector", lambda v, d=osl, x=xsl: v.tensor_add(d, t6[:], x),
                          inc=inc)
                    y_math = sDVE.n
                    for g4 in range(4):
                        dst = y_t[g4 * 8:(g4 + 1) * 8, :, t0:t0 + 512].rearrange(
                            "c p t -> p c t")
                        I(p, "gpsimd",
                          lambda g, d=dst, s=sp3(outb[:, g4 * 4096:(g4 + 1) * 4096], 512):
                          g.dma_start(d, s),
                          waits=[(sDVE, y_math)] if g4 == 0 else (),
                          inc=(sY, sY.take(16)))
                    y_store = sY.n
                p.add("sync", lambda eng, v=sY.n, h=sY.h: eng.wait_ge(h, v))
                p.emit(nc)

    nc.finalize()
    return nc


def _alibi_slopes(n):
    cp2 = 2 ** math.floor(math.log2(n))
    base = 2.0 ** (-8.0 / cp2)
    slopes = base ** np.arange(1, cp2 + 1, dtype=np.float32)
    if cp2 != n:
        extra_base = 2.0 ** (-4.0 / cp2)
        extra = extra_base ** np.arange(1, 1 + 2 * (n - cp2), 2, dtype=np.float32)
        slopes = np.concatenate([slopes, extra])
    return slopes.astype(np.float32)


def host_inputs(L0_attn_w_qkv, L0_attn_w_o, L0_ffn_w_gate, L0_ffn_w_up, L0_ffn_w_down):
    bf = ml_dtypes.bfloat16
    wqkv = np.asarray(L0_attn_w_qkv, dtype=np.float32)
    wo = np.asarray(L0_attn_w_o, dtype=np.float32)
    wg = np.asarray(L0_ffn_w_gate, dtype=np.float32)
    wu = np.asarray(L0_ffn_w_up, dtype=np.float32)
    wd = np.asarray(L0_ffn_w_down, dtype=np.float32)
    shared = {}
    shared["wqk_t"] = np.ascontiguousarray(
        wqkv[:, :40 * 128].reshape(HC, 128, 40, 128).transpose(2, 1, 0, 3))
    shared["wv_t"] = np.ascontiguousarray(
        wqkv[:, 40 * 128:].reshape(HC, 128, 1024).transpose(1, 0, 2))
    shared["wo_t"] = np.ascontiguousarray(
        wo.reshape(HC, 128, HC, 128).transpose(2, 1, 0, 3))
    shared["wg_t"] = np.ascontiguousarray(
        wg.reshape(IC, 128, HC, 128).transpose(0, 3, 2, 1))
    shared["wu_t"] = np.ascontiguousarray(
        wu.reshape(IC, 128, HC, 128).transpose(0, 3, 2, 1))
    shared["wd_t"] = np.ascontiguousarray(
        wd.reshape(HC, 128, IC, 128).transpose(0, 3, 2, 1))
    q = np.arange(S, dtype=np.float32)
    rel = q[None, :] - q[:, None]
    Dt = np.sqrt(np.maximum(rel, 0.0))
    Dt[rel < 0] = MASK_D
    shared["dT_c"] = np.ascontiguousarray(Dt.astype(bf).reshape(16, 128, S))
    shared["onesM_c"] = np.ones((128, 128), dtype=bf)
    return shared


def kernel(L0_attn_hidden_states, L0_attn_w_qkv, L0_attn_w_o,
           L0_ffn_w_gate, L0_ffn_w_up, L0_ffn_w_down):
    x = np.asarray(L0_attn_hidden_states, dtype=np.float32)
    B = x.shape[0]
    shared = host_inputs(L0_attn_w_qkv, L0_attn_w_o, L0_ffn_w_gate,
                         L0_ffn_w_up, L0_ffn_w_down)
    nc = build_program()
    in_maps = []
    for c in range(B):
        m = dict(shared)
        m["x_t"] = np.ascontiguousarray(x[c].T).reshape(HC, 128, S)
        in_maps.append(m)
    res = run_bass_kernel_spmd(nc, in_maps, core_ids=list(range(B)))
    y = np.empty((B, S, H), dtype=np.float32)
    for c in range(B):
        y[c] = res.results[c]["y_t"].reshape(H, S).T
    return y


# version 09
# speedup vs baseline: 1.0175x; 1.0060x over previous
import math
import contextlib
import numpy as np
import ml_dtypes
import concourse.bass as bass
import concourse.mybir as mybir
from concourse.bass_utils import run_bass_kernel_spmd

F32 = mybir.dt.float32
BF16 = mybir.dt.bfloat16
AF = mybir.ActivationFunctionType
ALU = mybir.AluOpType
AX = mybir.AxisListType

S = 2048
H = 4096
NH, NKV, G, D = 32, 8, 4, 128
INTER = 11008
HC = H // 128
IC = INTER // 128
TB = 4
SCALE = D ** -0.5
INV_H = 1.0 / H
EPS = 1e-5
MASK_D = 30000.0

ENGINES = ("sync", "gpsimd", "tensor", "scalar", "vector")
SUPERS = [list(range(s, min(s + 8, IC))) for s in range(0, IC, 8)]


class Sem:
    __slots__ = ("h", "n", "name")

    def __init__(self, h, name):
        self.h = h
        self.n = 0
        self.name = name

    def bump(self, k=1):
        self.n += k
        return self.n

    def take(self, k=1):
        self.n += k
        return k


class Plan:
    def __init__(self):
        self.ev = {e: [] for e in ENGINES}

    def add(self, e, f):
        self.ev[e].append(f)

    def emit(self, nc):
        with nc.Block() as blk:
            for e in ENGINES:
                evs = self.ev[e]
                if not evs:
                    continue

                def mk(evs=evs):
                    def body(eng):
                        for f in evs:
                            f(eng)
                    return body
                getattr(blk, e)(mk())


def I(plan, e, build, waits=(), inc=None, inc2=None):
    waits = tuple((s.h, v) for (s, v) in waits)

    def f(eng, build=build, waits=waits, inc=inc, inc2=inc2):
        for h, v in waits:
            eng.wait_ge(h, v)
        ins = build(eng)
        if inc is not None:
            ins = ins.then_inc(inc[0].h, inc[1])
        if inc2 is not None:
            ins = ins.then_inc(inc2[0].h, inc2[1])
    plan.add(e, f)


def sp3(ap, w):
    return ap.rearrange("p (c t) -> p c t", t=w)


def build_program(n_phases=99, debug=False):
    nc = bass.Bass("TRN2", target_bir_lowering=False, debug=False)
    dbg_kind = "ExternalOutput" if debug else "Internal"

    x_t = nc.dram_tensor("x_t", [HC, 128, S], F32, kind="ExternalInput").ap()
    wqk_t = nc.dram_tensor("wqk_t", [40, 128, HC, 128], F32, kind="ExternalInput").ap()
    wv_t = nc.dram_tensor("wv_t", [128, HC, 1024], F32, kind="ExternalInput").ap()
    wo_t = nc.dram_tensor("wo_t", [HC, 128, HC, 128], F32, kind="ExternalInput").ap()
    wg_t = nc.dram_tensor("wg_t", [IC, 128, HC, 128], F32, kind="ExternalInput").ap()
    wu_t = nc.dram_tensor("wu_t", [IC, 128, HC, 128], F32, kind="ExternalInput").ap()
    wd_t = nc.dram_tensor("wd_t", [HC, 128, IC, 128], F32, kind="ExternalInput").ap()
    dT_c = nc.dram_tensor("dT_c", [16, 128, S], BF16, kind="ExternalInput").ap()
    dneg_c = nc.dram_tensor("dneg_c", [128, NH * 128], BF16, kind="ExternalInput").ap()
    onesM_c = nc.dram_tensor("onesM_c", [128, 128], BF16, kind="ExternalInput").ap()

    q_s = nc.dram_tensor("q_s", [NH, 128, S], BF16, kind=dbg_kind).ap()
    k_s = nc.dram_tensor("k_s", [NKV, 128, S], BF16, kind=dbg_kind).ap()
    v_s = nc.dram_tensor("v_s", [16, 128, 1024], BF16, kind=dbg_kind).ap()
    ao_s = nc.dram_tensor("ao_s", [NH, 128, S], BF16, kind=dbg_kind).ap()
    pj_s = nc.dram_tensor("pj_s", [HC, 128, S], BF16, kind=dbg_kind).ap()
    x1_s = nc.dram_tensor("x1_s", [HC, 128, S], F32, kind=dbg_kind).ap()
    ac_s = nc.dram_tensor("ac_s", [HC, 128, S], F32, kind=dbg_kind).ap()
    y_t = nc.dram_tensor("y_t", [HC, 128, S], F32, kind="ExternalOutput").ap()
    h1_d = h2_d = None
    if debug:
        h1_d = nc.dram_tensor("h1_d", [128, HC * S], BF16, kind="ExternalOutput").ap()
        h2_d = nc.dram_tensor("h2_d", [128, HC * S], BF16, kind="ExternalOutput").ap()

    ctx = contextlib.ExitStack()
    with ctx:
        semnames = ["sW", "sWf", "sSP", "sSPa", "sPL", "sY",
                    "sPE", "sPEn", "sPEs", "sPEp", "sPEr", "sPEd", "sPEg", "sPEu",
                    "sACT", "sACTn", "sACTs", "sACTr", "sACTc", "sACTd",
                    "sDVE", "sDVEb", "sDVEo", "sDVEa", "sDVEr",
                    "sWq0", "sWq1", "sSt0", "sSt1", "sAo0", "sAo1",
                    "sXl0", "sXl1", "sXr0", "sXr1",
                    "sWg0", "sWg1", "sWu0", "sWu1", "sWd0", "sWd1",
                    "sAc0", "sAc1", "sXs0", "sXs1"]
        sm = {n: Sem(ctx.enter_context(nc.semaphore(n)), n) for n in semnames}
        sW, sWf, sSP, sSPa, sPL, sY = (sm[n] for n in ("sW", "sWf", "sSP", "sSPa", "sPL", "sY"))
        sPE, sPEn, sPEs, sPEp, sPEr, sPEd, sPEg, sPEu = (
            sm[n] for n in ("sPE", "sPEn", "sPEs", "sPEp", "sPEr", "sPEd", "sPEg", "sPEu"))
        sACT, sACTn, sACTs, sACTr, sACTc, sACTd = (
            sm[n] for n in ("sACT", "sACTn", "sACTs", "sACTr", "sACTc", "sACTd"))
        sDVE, sDVEb, sDVEo, sDVEa, sDVEr = (sm[n] for n in ("sDVE", "sDVEb", "sDVEo", "sDVEa", "sDVEr"))
        sWq = [sm["sWq0"], sm["sWq1"]]
        sSt = [sm["sSt0"], sm["sSt1"]]
        sAo = [sm["sAo0"], sm["sAo1"]]
        sXl = [sm["sXl0"], sm["sXl1"]]
        sXr = [sm["sXr0"], sm["sXr1"]]
        sWg = [sm["sWg0"], sm["sWg1"]]
        sWu = [sm["sWu0"], sm["sWu1"]]
        sWd = [sm["sWd0"], sm["sWd1"]]
        sAc = [sm["sAc0"], sm["sAc1"]]
        sXs = [sm["sXs0"], sm["sXs1"]]

        onesM = ctx.enter_context(nc.sbuf_tensor("onesM", [128, 128], BF16))
        cst = ctx.enter_context(nc.sbuf_tensor("cst", [128, 48], F32))
        onescol = onesM[:, 0:1]

        slopes_f = [float(s) for s in _alibi_slopes(NH)]
        cvals = [EPS, SCALE, INV_H, -0.5, -1.0] + slopes_f
        for i, v in enumerate(cvals):
            nc.const_aps.register(F32, v, cst[:, i:i + 1])

        p = Plan()
        for i, v in enumerate(cvals):
            I(p, "gpsimd", lambda g, i=i, v=v: g.memset(cst[:, i:i + 1], v))
        I(p, "sync", lambda s: s.dma_start(onesM[:], onesM_c), inc=(sSP, sSP.take(16)))
        p.add("sync", lambda eng, v=sSP.n, h=sSP.h: eng.wait_ge(h, v))
        p.emit(nc)

        def ln_rows(p, ps1, ps2, rA, rB, rC, waits):
            I(p, "scalar", lambda a: a.copy(rA, ps1[:]), waits=waits)
            I(p, "scalar", lambda a: a.copy(rB, ps2[:]))
            I(p, "scalar", lambda a: a.mul(rA, rA, INV_H))
            I(p, "scalar", lambda a: a.mul(rB, rB, INV_H), inc=(sACT, sACT.take()))
            se = sACT.n
            I(p, "vector", lambda v: v.tensor_mul(rC, rA, rA), waits=[(sACT, se)])
            I(p, "vector", lambda v: v.tensor_sub(rB, rB, rC), inc=(sDVE, sDVE.take()))
            vd = sDVE.n
            I(p, "scalar", lambda a: a.activation(rB, rB, AF.Ln, bias=EPS),
              waits=[(sDVE, vd)])
            I(p, "scalar", lambda a: a.activation(rB, rB, AF.Exp, scale=-0.5),
              inc=(sACT, sACT.take()))
            return sACT.n

        if n_phases >= 1:
            big_ctx = contextlib.ExitStack()
            with big_ctx:
                big = big_ctx.enter_context(nc.sbuf_tensor("big1", [128, HC * S], BF16))
                with contextlib.ExitStack() as _es:
                    xblk = _es.enter_context(nc.sbuf_tensor("xblk", [128, HC * 512], BF16))
                    sqf = _es.enter_context(nc.sbuf_tensor("sqf", [128, HC * 512], BF16))
                    rows1 = _es.enter_context(nc.sbuf_tensor("rows1", [1, 512 * 3], F32))
                    rowM = _es.enter_context(nc.sbuf_tensor("rowM", [128, 512], BF16))
                    rowR = _es.enter_context(nc.sbuf_tensor("rowR", [128, 512], BF16))
                    Mt = _es.enter_context(nc.sbuf_tensor("Mt", [128, 512], BF16))
                    Rt = _es.enter_context(nc.sbuf_tensor("Rt", [128, 512], BF16))
                    t1b = _es.enter_context(nc.sbuf_tensor("t1b", [128, 512], BF16))
                    ps_st1 = _es.enter_context(nc.psum_tensor("ps_st1", [1, 512], F32))
                    ps_st2 = _es.enter_context(nc.psum_tensor("ps_st2", [1, 512], F32))
                    ps_bM = _es.enter_context(nc.psum_tensor("ps_bM", [128, 512], F32))
                    ps_bR = _es.enter_context(nc.psum_tensor("ps_bR", [128, 512], F32))
                    rA = rows1[:, 0:512]
                    rB = rows1[:, 512:1024]
                    rC = rows1[:, 1024:1536]
                    p = Plan()
                    I(p, "gpsimd", lambda g: g.memset(rowM[:], 0.0))
                    I(p, "gpsimd", lambda g: g.memset(rowR[:], 0.0), inc=(sPL, sPL.take()))
                    pad_ready = sPL.n
                    norm_end = 0
                    rows_evac = 0
                    bcast_done = 0
                    bcast_evac = 0
                    for tb in range(TB):
                        t0 = tb * 512
                        lw = [(sDVE, norm_end)] if tb else []
                        for g4 in range(4):
                            src = x_t[g4 * 8:(g4 + 1) * 8, :, t0:t0 + 512].rearrange(
                                "c p t -> p c t")
                            I(p, "gpsimd",
                              lambda g, d=sp3(xblk[:, g4 * 4096:(g4 + 1) * 4096], 512), s=src:
                              g.dma_start(d, s),
                              waits=lw if g4 == 0 else (), inc=(sW, sW.take(16)))
                        ld = sW.n
                        for c in range(HC):
                            cw = [(sW, ld)] if c == 0 else []
                            xs = xblk[:, c * 512:(c + 1) * 512]
                            sq = sqf[:, c * 512:(c + 1) * 512]
                            inc = (sDVE, sDVE.take()) if c == HC - 1 else None
                            I(p, "vector", lambda v, a=xs, b=sq: v.tensor_mul(b, a, a),
                              waits=cw, inc=inc)
                        sq_done = sDVE.n
                        for c in range(HC):
                            cw = []
                            if c == 0:
                                cw = [(sDVE, sq_done)]
                                if rows_evac:
                                    cw.append((sACT, rows_evac))
                            xs = xblk[:, c * 512:(c + 1) * 512]
                            sq = sqf[:, c * 512:(c + 1) * 512]
                            I(p, "tensor", lambda t, r=xs, c0=(c == 0), c1=(c == HC - 1):
                              t.matmul(ps_st1[:], onescol, r, start=c0, stop=c1), waits=cw)
                            inc = (sPE, sPE.take()) if c == HC - 1 else None
                            I(p, "tensor", lambda t, r=sq, c0=(c == 0), c1=(c == HC - 1):
                              t.matmul(ps_st2[:], onescol, r, start=c0, stop=c1), inc=inc)
                        stats_done = sPE.n
                        rows_evac = ln_rows(p, ps_st1, ps_st2, rA, rB, rC,
                                            [(sPE, stats_done)])
                        rw = [(sACT, rows_evac)] + \
                             ([(sPE, bcast_done)] if tb else [(sPL, pad_ready)])
                        I(p, "scalar", lambda a: a.copy(rowM[0:1, :], rA), waits=rw)
                        I(p, "scalar", lambda a: a.copy(rowR[0:1, :], rB),
                          inc=(sACT, sACT.take()))
                        rows_b = sACT.n
                        bw = [(sACT, rows_b)] + ([(sACT, bcast_evac)] if tb else [])
                        I(p, "tensor", lambda t: t.matmul(ps_bM[:], onesM[:], rowM[:],
                                                          start=True, stop=True), waits=bw)
                        I(p, "tensor", lambda t: t.matmul(ps_bR[:], onesM[:], rowR[:],
                                                          start=True, stop=True),
                          inc=(sPE, sPE.take()))
                        bcast_done = sPE.n
                        ew = [(sPE, bcast_done)] + ([(sDVE, norm_end)] if tb else [])
                        I(p, "scalar", lambda a: a.copy(Mt[:], ps_bM[:]), waits=ew)
                        I(p, "scalar", lambda a: a.copy(Rt[:], ps_bR[:]),
                          inc=(sACT, sACT.take()))
                        bcast_evac = sACT.n
                        for c in range(HC):
                            cw = [(sACT, bcast_evac)] if c == 0 else []
                            xs = xblk[:, c * 512:(c + 1) * 512]
                            dst = big[:, c * S + t0:c * S + t0 + 512]
                            I(p, "vector", lambda v, a=xs: v.tensor_sub(t1b[:], a, Mt[:]),
                              waits=cw)
                            inc = (sDVE, sDVE.take()) if c == HC - 1 else None
                            I(p, "vector", lambda v, d=dst: v.tensor_mul(d, t1b[:], Rt[:]),
                              inc=inc)
                        norm_end = sDVE.n
                    p.emit(nc)

                if n_phases >= 2:
                    with contextlib.ExitStack() as _es:
                        wqb0 = _es.enter_context(nc.sbuf_tensor("wqb0", [128, HC * 128], BF16))
                        wqb1 = _es.enter_context(nc.sbuf_tensor("wqb1", [128, HC * 128], BF16))
                        stg0 = _es.enter_context(nc.sbuf_tensor("stg0", [128, 2048], BF16))
                        stg1 = _es.enter_context(nc.sbuf_tensor("stg1", [128, 2048], BF16))
                        ps_q0 = _es.enter_context(nc.psum_tensor("ps_q0", [128, 512], F32))
                        ps_q1 = _es.enter_context(nc.psum_tensor("ps_q1", [128, 512], F32))
                        ps_q2 = _es.enter_context(nc.psum_tensor("ps_q2", [128, 512], F32))
                        ps_q3 = _es.enter_context(nc.psum_tensor("ps_q3", [128, 512], F32))
                        wqb = [wqb0, wqb1]
                        stg = [stg0, stg1]
                        psq = [ps_q0, ps_q1, ps_q2, ps_q3]
                        p = Plan()
                        wfree = [0, 0]
                        pe_done = {}
                        evac = {}
                        store = {}
                        for oc in range(40):
                            par = oc % 2
                            lw = [(sACT, evac[(oc - 2, TB - 1)])] if oc >= 2 else []
                            src = wqk_t[oc:oc + 1].rearrange("o p c i -> (o p) (c i)")
                            I(p, "gpsimd", lambda g, d=wqb[par][:], s=src: g.dma_start(d, s),
                              waits=lw, inc=(sWq[par], sWq[par].take(16)))
                            wld = (sWq[par], sWq[par].n)
                            for kc in range(HC):
                                for tb in range(TB):
                                    cw = []
                                    if kc == 0:
                                        cw = [wld]
                                        if oc >= 1:
                                            cw.append((sACT, evac[(oc - 1, tb)]))
                                    lhs = wqb[par][:, kc * 128:(kc + 1) * 128]
                                    rhs = big[:, kc * S + tb * 512:kc * S + tb * 512 + 512]
                                    inc = None
                                    if kc == HC - 1:
                                        inc = (sPE, sPE.take())
                                    I(p, "tensor", lambda t, o=psq[tb][:], l=lhs, r=rhs,
                                      c0=(kc == 0), c1=(kc == HC - 1):
                                      t.matmul(o, l, r, start=c0, stop=c1),
                                      waits=cw, inc=inc)
                                    if kc == HC - 1:
                                        pe_done[(oc, tb)] = sPE.n
                            for tb in range(TB):
                                cw = [(sPE, pe_done[(oc, tb)])]
                                if tb == 0 and oc >= 2:
                                    cw.append(store[oc - 2])
                                dst = stg[par][:, tb * 512:(tb + 1) * 512]
                                I(p, "scalar", lambda a, d=dst, s=psq[tb][:]: a.copy(d, s),
                                  waits=cw, inc=(sACT, sACT.take()))
                                evac[(oc, tb)] = sACT.n
                            tgt = (q_s[oc:oc + 1] if oc < NH else
                                   k_s[oc - NH:oc - NH + 1]).rearrange("o p t -> (o p) t")
                            I(p, "sync", lambda s, d=tgt, s2=stg[par][:]: s.dma_start(d, s2),
                              waits=[(sACT, evac[(oc, TB - 1)])],
                              inc=(sSt[par], sSt[par].take(16)))
                            store[oc] = (sSt[par], sSt[par].n)
                        p.emit(nc)

                    with contextlib.ExitStack() as _es:
                        wvb = _es.enter_context(nc.sbuf_tensor("wvb", [128, HC * 1024], BF16))
                        stv0 = _es.enter_context(nc.sbuf_tensor("stv0", [128, 1024], BF16))
                        stv1 = _es.enter_context(nc.sbuf_tensor("stv1", [128, 1024], BF16))
                        ps_vA0 = _es.enter_context(nc.psum_tensor("ps_vA0", [128, 512], F32))
                        ps_vB0 = _es.enter_context(nc.psum_tensor("ps_vB0", [128, 512], F32))
                        ps_vA1 = _es.enter_context(nc.psum_tensor("ps_vA1", [128, 512], F32))
                        ps_vB1 = _es.enter_context(nc.psum_tensor("ps_vB1", [128, 512], F32))
                        stv = [stv0, stv1]
                        psA = [ps_vA0, ps_vA1]
                        psB = [ps_vB0, ps_vB1]
                        p = Plan()
                        I(p, "gpsimd",
                          lambda g: g.dma_start(wvb[:], wv_t.rearrange("p c i -> p (c i)")),
                          inc=(sW, sW.take(16)))
                        wld = sW.n
                        vevac = {}
                        vstore = {}
                        for tt in range(16):
                            par = tt % 2
                            for kc in range(HC):
                                cw = []
                                if kc == 0:
                                    cw = [(sW, wld)]
                                    if tt >= 2:
                                        cw.append((sACT, vevac[tt - 2]))
                                lhs = big[:, kc * S + tt * 128:kc * S + tt * 128 + 128]
                                I(p, "tensor", lambda t, o=psA[par][:], l=lhs,
                                  r=wvb[:, kc * 1024:kc * 1024 + 512],
                                  c0=(kc == 0), c1=(kc == HC - 1):
                                  t.matmul(o, l, r, start=c0, stop=c1), waits=cw)
                                inc = (sPE, sPE.take()) if kc == HC - 1 else None
                                I(p, "tensor", lambda t, o=psB[par][:], l=lhs,
                                  r=wvb[:, kc * 1024 + 512:kc * 1024 + 1024],
                                  c0=(kc == 0), c1=(kc == HC - 1):
                                  t.matmul(o, l, r, start=c0, stop=c1), inc=inc)
                            vdone = sPE.n
                            cw = [(sPE, vdone)]
                            if tt >= 2:
                                cw.append(vstore[tt - 2])
                            I(p, "scalar",
                              lambda a, d=stv[par][:, 0:512], s=psA[par][:]: a.copy(d, s),
                              waits=cw)
                            I(p, "scalar",
                              lambda a, d=stv[par][:, 512:1024], s=psB[par][:]: a.copy(d, s),
                              inc=(sACT, sACT.take()))
                            vevac[tt] = sACT.n
                            dstv = v_s[tt:tt + 1].rearrange("o p t -> (o p) t")
                            I(p, "sync", lambda s, d=dstv, s2=stv[par][:]: s.dma_start(d, s2),
                              waits=[(sACT, vevac[tt])],
                              inc=(sSt[par], sSt[par].take(16)))
                            vstore[tt] = (sSt[par], sSt[par].n)
                        if debug:
                            I(p, "sync", lambda s: s.dma_start(h1_d, big[:]),
                              inc=(sSP, sSP.take(16)))
                            p.add("sync", lambda eng, v=sSP.n, h=sSP.h: eng.wait_ge(h, v))
                        p.emit(nc)

        if n_phases >= 3:
            KTOT = S // 128
            with contextlib.ExitStack() as _es:
                Ksb = _es.enter_context(nc.sbuf_tensor("Ksb", [128, NKV * S], BF16))
                Vsb = _es.enter_context(nc.sbuf_tensor("Vsb", [128, 16 * 1024], BF16))
                Qb = _es.enter_context(nc.sbuf_tensor("Qb", [128, NH * 512], BF16))
                Db = _es.enter_context(nc.sbuf_tensor("Db", [128, KTOT * 512], BF16))
                sqq = _es.enter_context(nc.sbuf_tensor("sqq", [128, 2 * 512], BF16))
                sqk = _es.enter_context(nc.sbuf_tensor("sqk", [128, S], BF16))
                nkr = _es.enter_context(nc.sbuf_tensor("nkr", [1, S], F32))
                nqr = _es.enter_context(nc.sbuf_tensor("nqr", [1, 2 * 512], F32))
                qmx = _es.enter_context(nc.sbuf_tensor("qmx", [1, 512], F32))
                mk2s = _es.enter_context(nc.sbuf_tensor("mk2s", [1, 8], F32))
                scr = _es.enter_context(nc.sbuf_tensor("scr", [1, 8], F32))
                colC = _es.enter_context(nc.sbuf_tensor("colC", [128, 1], BF16))
                bcolT = _es.enter_context(nc.sbuf_tensor("bcolT", [128, 2], F32))
                rowC0 = _es.enter_context(nc.sbuf_tensor("rowC0", [128, 512], BF16))
                rowC1 = _es.enter_context(nc.sbuf_tensor("rowC1", [128, 512], BF16))
                dnegs = _es.enter_context(nc.sbuf_tensor("dnegs", [128, NH * 128], BF16))
                pt0 = _es.enter_context(nc.sbuf_tensor("pt0", [128, 512], BF16))
                pt1 = _es.enter_context(nc.sbuf_tensor("pt1", [128, 512], BF16))
                pt2 = _es.enter_context(nc.sbuf_tensor("pt2", [128, 512], BF16))
                recb0 = _es.enter_context(nc.sbuf_tensor("recb0", [128, 512], F32))
                recb1 = _es.enter_context(nc.sbuf_tensor("recb1", [128, 512], F32))
                aot0 = _es.enter_context(nc.sbuf_tensor("aot0", [128, 512], BF16))
                aot1 = _es.enter_context(nc.sbuf_tensor("aot1", [128, 512], BF16))
                ps_s0 = _es.enter_context(nc.psum_tensor("ps_s0", [128, 512], F32))
                ps_s1 = _es.enter_context(nc.psum_tensor("ps_s1", [128, 512], F32))
                ps_s2 = _es.enter_context(nc.psum_tensor("ps_s2", [128, 512], F32))
                ps_o0 = _es.enter_context(nc.psum_tensor("ps_o0", [128, 512], F32))
                ps_o1 = _es.enter_context(nc.psum_tensor("ps_o1", [128, 512], F32))
                ps_sumB = _es.enter_context(nc.psum_tensor("ps_sumB", [128, 512], F32))
                ps_rb = _es.enter_context(nc.psum_tensor("ps_rb", [128, 512], F32))
                ps_pre = _es.enter_context(nc.psum_tensor("ps_pre", [128, 512], F32))
                pss = [ps_s0, ps_s1, ps_s2]
                pso = [ps_o0, ps_o1]
                pt = [pt0, pt1, pt2]
                rowC = [rowC0, rowC1]
                recb = [recb0, recb1]
                aot = [aot0, aot1]
                p = Plan()
                I(p, "gpsimd", lambda g: g.memset(colC[:], 0.0))
                I(p, "gpsimd", lambda g: g.memset(rowC0[:], 0.0))
                I(p, "gpsimd", lambda g: g.memset(rowC1[:], 0.0), inc=(sPL, sPL.take()))
                pad_ready = sPL.n
                I(p, "sync", lambda s: s.dma_start(dnegs[:], dneg_c),
                  inc=(sSP, sSP.take(16)))
                I(p, "sync", lambda s: s.dma_start(sp3(Ksb[:], S),
                                                   k_s.rearrange("h p t -> p h t")),
                  inc=(sSP, sSP.take(16)))
                I(p, "sync", lambda s: s.dma_start(sp3(Vsb[:], 1024),
                                                   v_s.rearrange("h p t -> p h t")),
                  inc=(sSP, sSP.take(16)))
                kv_ld = sSP.n

                row_evac = [0, 0, 0]
                rmax_done = 0
                for kv in range(NKV):
                    cw = [(sSP, kv_ld)] if kv == 0 else [(sPEn, sPEn.n)]
                    I(p, "vector", lambda v, a=Ksb[:, kv * S:(kv + 1) * S]:
                      v.tensor_mul(sqk[:], a, a), waits=cw, inc=(sDVE, sDVE.take()))
                    sqk_done = sDVE.n
                    ev = 0
                    for seg in range(4):
                        r = (kv * 4 + seg) % 3
                        cw = [(sDVE, sqk_done)]
                        if row_evac[r]:
                            cw.append((sACTn, row_evac[r]))
                        I(p, "tensor", lambda t, r=r,
                          s2=sqk[:, seg * 512:(seg + 1) * 512]:
                          t.matmul(ps_pre[r * 32:r * 32 + 1, :], onescol, s2,
                                   start=True, stop=True),
                          waits=cw, inc=(sPEn, sPEn.take()))
                        cw = [(sPEn, sPEn.n)]
                        if seg == 0 and rmax_done:
                            cw.append((sDVE, rmax_done))
                        I(p, "scalar", lambda a, r=r,
                          d=nkr[:, seg * 512:(seg + 1) * 512]:
                          a.copy(d, ps_pre[r * 32:r * 32 + 1, :]),
                          waits=cw, inc=(sACTn, sACTn.take()))
                        row_evac[r] = sACTn.n
                        ev = sACTn.n
                    I(p, "vector", lambda v, d=mk2s[:, kv:kv + 1]:
                      v.reduce_max(d, nkr[:], axis=AX.X),
                      waits=[(sACTn, ev)], inc=(sDVE, sDVE.take()))
                    rmax_done = sDVE.n
                I(p, "vector", lambda v: v.reduce_max(scr[:, 0:1], mk2s[:], axis=AX.X))

                bank_last_exp = [0, 0, 0]
                pt_last_pv = [0, 0, 0]
                pv_all = {}
                recip_done = {}
                rb_mm = {}
                rb_evac = {}
                ao_done = {}
                ao_store = {}
                nq_mm = {}
                last_scores = 0
                last_stt = 0
                bc_mm_prev = 0

                def em_tail(g):
                    if g < 0:
                        return
                    j = g % 2
                    cw = [(sPEp, pv_all[g]), (sPL, pad_ready)]
                    if g >= 2 and rb_mm.get(g - 2):
                        cw.append((sPEr, rb_mm[g - 2]))
                    def _recip(v, j=j, rb=(g % 2) * 32):
                        with nc.allow_low_precision(
                                reason="softmax 1/sum cast to bf16 for bcast matmul"):
                            return v.reciprocal(rowC[j][0:1, :], ps_sumB[rb:rb + 1, :])
                    I(p, "vector", _recip, waits=cw, inc=(sDVEr, sDVEr.take()))
                    recip_done[g] = sDVEr.n
                    cw = [(sDVEr, recip_done[g])]
                    if g >= 1 and rb_evac.get(g - 1):
                        cw.append((sACTc, rb_evac[g - 1]))
                    I(p, "tensor", lambda t, j=j:
                      t.matmul(ps_rb[:], onesM[:], rowC[j][:], start=True, stop=True),
                      waits=cw, inc=(sPEr, sPEr.take()))
                    rb_mm[g] = sPEr.n
                    cw = [(sPEr, rb_mm[g])]
                    if g >= 2 and ao_done.get(g - 2):
                        cw.append((sDVEo, ao_done[g - 2]))
                    I(p, "scalar", lambda a, j=j: a.copy(recb[j][:], ps_rb[:]),
                      waits=cw, inc=(sACTc, sACTc.take()))
                    rb_evac[g] = sACTc.n
                    cw = [(sACTc, rb_evac[g])]
                    if g >= 2 and ao_store.get(g - 2):
                        cw.append(ao_store[g - 2])
                    I(p, "vector", lambda v, j=j:
                      v.tensor_mul(aot[j][:], pso[j][:], recb[j][:]),
                      waits=cw, inc=(sDVEo, sDVEo.take()))
                    ao_done[g] = sDVEo.n
                    qq, hh = divmod(g, NH)
                    dsta = ao_s[hh:hh + 1, :, qq * 512:(qq + 1) * 512].rearrange(
                        "o p t -> (o p) t")
                    I(p, "sync", lambda s, j=j, d=dsta: s.dma_start(d, aot[j][:]),
                      waits=[(sDVEo, ao_done[g])], inc=(sAo[j], sAo[j].take(16)))
                    ao_store[g] = (sAo[j], sAo[j].n)

                for qb in range(TB):
                    KT = 4 * (qb + 1)
                    cw = [(sPEs, last_scores)] if qb else []
                    for g4 in range(4):
                        src = q_s[g4 * 8:(g4 + 1) * 8, :,
                                  qb * 512:(qb + 1) * 512].rearrange("c p t -> p c t")
                        I(p, "sync",
                          lambda s, d=sp3(Qb[:, g4 * 4096:(g4 + 1) * 4096], 512), s2=src:
                          s.dma_start(d, s2),
                          waits=cw if g4 == 0 else (), inc=(sSP, sSP.take(16)))
                    srcD = dT_c[0:KT, :, qb * 512:(qb + 1) * 512].rearrange(
                        "c p t -> p c t")
                    I(p, "sync", lambda s, d=sp3(Db[:, 0:KT * 512], 512), s2=srcD:
                      s.dma_start(d, s2),
                      inc=(sSP, sSP.take(16)))
                    qld = sSP.n
                    max_acc = {}
                    for h in range(NH):
                        j = h % 2
                        r = h % 3
                        cw = []
                        if h == 0:
                            cw.append((sSP, qld))
                        if nq_mm.get(h - 2):
                            cw.append((sPEn, nq_mm[h - 2]))
                        I(p, "vector", lambda v, j=j, a=Qb[:, h * 512:(h + 1) * 512]:
                          v.tensor_mul(sqq[:, j * 512:(j + 1) * 512], a, a),
                          waits=cw, inc=(sDVE, sDVE.take()))
                        sq_done = sDVE.n
                        cw = [(sDVE, sq_done)]
                        if row_evac[r]:
                            cw.append((sACTn, row_evac[r]))
                        I(p, "tensor", lambda t, j=j, r=r:
                          t.matmul(ps_pre[r * 32:r * 32 + 1, :], onescol,
                                   sqq[:, j * 512:(j + 1) * 512],
                                   start=True, stop=True),
                          waits=cw, inc=(sPEn, sPEn.take()))
                        nq_mm[h] = sPEn.n
                        cw = [(sPEn, nq_mm[h])]
                        if max_acc.get(h - 2):
                            cw.append((sDVE, max_acc[h - 2]))
                        I(p, "scalar", lambda a, j=j, r=r:
                          a.copy(nqr[:, j * 512:(j + 1) * 512],
                                 ps_pre[r * 32:r * 32 + 1, :]),
                          waits=cw, inc=(sACTn, sACTn.take()))
                        row_evac[r] = sACTn.n
                        cw = [(sACTn, row_evac[r])]
                        if h == 0:
                            I(p, "vector", lambda v: v.tensor_copy(qmx[:], nqr[:, 0:512]),
                              waits=cw, inc=(sDVE, sDVE.take()))
                        else:
                            I(p, "vector", lambda v, j=j:
                              v.tensor_max(qmx[:], qmx[:],
                                           nqr[:, j * 512:(j + 1) * 512]),
                              waits=cw, inc=(sDVE, sDVE.take()))
                        max_acc[h] = sDVE.n
                    I(p, "vector", lambda v: v.reduce_max(scr[:, 1:2], qmx[:], axis=AX.X))
                    I(p, "vector", lambda v: v.tensor_add(scr[:, 2:3], scr[:, 0:1],
                                                          scr[:, 1:2]))
                    cw = [(sPEn, bc_mm_prev)] if bc_mm_prev else []
                    I(p, "vector", lambda v:
                      v.tensor_scalar(colC[0:1, 0:1], scr[:, 2:3], -0.5 * SCALE, None,
                                      op0=ALU.mult),
                      waits=cw, inc=(sDVE, sDVE.take()))
                    cC_done = sDVE.n
                    cw = [(sDVE, cC_done)] + [(sACTn, row_evac[r]) for r in range(3)]
                    I(p, "tensor", lambda t: t.matmul(ps_pre[:, 0:1], onesM[:], colC[:],
                                                      start=True, stop=True),
                      waits=cw, inc=(sPEn, sPEn.take()))
                    bc_mm_prev = sPEn.n
                    I(p, "scalar", lambda a, c=qb % 2:
                      a.copy(bcolT[:, c:c + 1], ps_pre[:, 0:1]),
                      waits=[(sPEn, bc_mm_prev)], inc=(sACTn, sACTn.take()))

                    for h in range(NH):
                        gidx = qb * NH + h
                        kv = h // G
                        qsl = Qb[:, h * 512:(h + 1) * 512]
                        sumr0 = (h % 2) * 32
                        sc_done = {}
                        exp_done = {}

                        def em_scores(kt, qsl=qsl, kv=kv, h=h):
                            b = kt % 3
                            cw = []
                            if bank_last_exp[b]:
                                cw.append((sACTs, bank_last_exp[b]))
                            lhsK = Ksb[:, kv * S + kt * 128:kv * S + kt * 128 + 128]
                            I(p, "tensor", lambda t, o=pss[b][:], l=lhsK, r=qsl:
                              t.matmul(o, l, r, start=True, stop=False), waits=cw)
                            I(p, "tensor", lambda t, o=pss[b][:],
                              l=dnegs[:, h * 128:(h + 1) * 128],
                              r=Db[:, kt * 512:(kt + 1) * 512]:
                              t.matmul(o, l, r, start=False, stop=True),
                              inc=(sPEs, sPEs.take()))
                            sc_done[kt] = sPEs.n

                        def em_exp(kt, c=qb % 2):
                            b = kt % 3
                            cw = [(sPEs, sc_done[kt])]
                            if pt_last_pv[b]:
                                cw.append((sPEp, pt_last_pv[b]))
                            I(p, "scalar", lambda a, b=b, c=c:
                              a.activation(pt[b][:], pss[b][:], AF.Exp, scale=SCALE,
                                           bias=bcolT[:, c:c + 1]),
                              waits=cw, inc=(sACTs, sACTs.take()))
                            bank_last_exp[b] = sACTs.n
                            exp_done[kt] = sACTs.n

                        def em_pv(kt, KT=KT, kv=kv, po=(h % 2), gidx=gidx,
                                  sumr0=sumr0):
                            b = kt % 3
                            cw = [(sACTs, exp_done[kt])]
                            if kt == 0:
                                if ao_done.get(gidx - 2):
                                    cw.append((sDVEo, ao_done[gidx - 2]))
                                if recip_done.get(gidx - 2):
                                    cw.append((sDVEr, recip_done[gidx - 2]))
                            lhsV = Vsb[:, kt * 1024 + kv * 128:
                                       kt * 1024 + kv * 128 + 128]
                            I(p, "tensor", lambda t, o=pso[po][:], l=lhsV, r=pt[b][:],
                              c0=(kt == 0), c1=(kt == KT - 1):
                              t.matmul(o, l, r, start=c0, stop=c1), waits=cw)
                            I(p, "tensor", lambda t, r=pt[b][:],
                              c0=(kt == 0), c1=(kt == KT - 1):
                              t.matmul(ps_sumB[sumr0:sumr0 + 1, :], onescol, r,
                                       start=c0, stop=c1),
                              inc=(sPEp, sPEp.take()))
                            pt_last_pv[b] = sPEp.n
                            if kt == KT - 1:
                                pv_all[gidx] = sPEp.n

                        em_scores(0)
                        em_exp(0)
                        em_scores(1)
                        em_exp(1)
                        em_tail(gidx - 1)
                        for kt in range(2, KT):
                            em_scores(kt)
                            em_exp(kt)
                            em_pv(kt - 2)
                        em_pv(KT - 2)
                        em_pv(KT - 1)
                        last_scores = sPEs.n
                em_tail(TB * NH - 1)
                p.add("sync", lambda eng, v=sAo[0].n, h=sAo[0].h: eng.wait_ge(h, v))
                p.add("sync", lambda eng, v=sAo[1].n, h=sAo[1].h: eng.wait_ge(h, v))
                p.emit(nc)

        if n_phases >= 4:
            big_ctx = contextlib.ExitStack()
            with big_ctx:
                big = big_ctx.enter_context(nc.sbuf_tensor("big2", [128, HC * S], BF16))
                with contextlib.ExitStack() as _es:
                    wob0 = _es.enter_context(nc.sbuf_tensor("wob0", [128, HC * 128], BF16))
                    wob1 = _es.enter_context(nc.sbuf_tensor("wob1", [128, HC * 128], BF16))
                    stg0b = _es.enter_context(nc.sbuf_tensor("stg0b", [128, 2048], BF16))
                    stg1b = _es.enter_context(nc.sbuf_tensor("stg1b", [128, 2048], BF16))
                    ps_p0 = _es.enter_context(nc.psum_tensor("ps_p0", [128, 512], F32))
                    ps_p1 = _es.enter_context(nc.psum_tensor("ps_p1", [128, 512], F32))
                    ps_p2 = _es.enter_context(nc.psum_tensor("ps_p2", [128, 512], F32))
                    ps_p3 = _es.enter_context(nc.psum_tensor("ps_p3", [128, 512], F32))
                    wob = [wob0, wob1]
                    stg = [stg0b, stg1b]
                    psq = [ps_p0, ps_p1, ps_p2, ps_p3]
                    p = Plan()
                    for g4 in range(4):
                        src = ao_s[g4 * 8:(g4 + 1) * 8, :, :].rearrange("c p t -> p c t")
                        I(p, "sync", lambda s, d=sp3(big[:, g4 * 8 * S:(g4 + 1) * 8 * S], S), s2=src:
                          s.dma_start(d, s2), inc=(sSP, sSP.take(16)))
                    ao_ld = sSP.n
                    wfree = [0, 0]
                    pe_done = {}
                    evac = {}
                    store = {}
                    for oc in range(HC):
                        par = oc % 2
                        lw = [(sACT, evac[(oc - 2, TB - 1)])] if oc >= 2 else []
                        src = wo_t[oc:oc + 1].rearrange("o p c i -> (o p) (c i)")
                        I(p, "gpsimd", lambda g, d=wob[par][:], s=src: g.dma_start(d, s),
                          waits=lw, inc=(sWq[par], sWq[par].take(16)))
                        wld = (sWq[par], sWq[par].n)
                        for kc in range(HC):
                            for tb in range(TB):
                                cw = []
                                if kc == 0:
                                    cw = [wld, (sSP, ao_ld)]
                                    if oc >= 1:
                                        cw.append((sACT, evac[(oc - 1, tb)]))
                                lhs = wob[par][:, kc * 128:(kc + 1) * 128]
                                rhs = big[:, kc * S + tb * 512:kc * S + tb * 512 + 512]
                                inc = None
                                if kc == HC - 1:
                                    inc = (sPE, sPE.take())
                                I(p, "tensor", lambda t, o=psq[tb][:], l=lhs, r=rhs,
                                  c0=(kc == 0), c1=(kc == HC - 1):
                                  t.matmul(o, l, r, start=c0, stop=c1),
                                  waits=cw, inc=inc)
                                if kc == HC - 1:
                                    pe_done[(oc, tb)] = sPE.n
                        for tb in range(TB):
                            cw = [(sPE, pe_done[(oc, tb)])]
                            if tb == 0 and oc >= 2:
                                cw.append(store[oc - 2])
                            dst = stg[par][:, tb * 512:(tb + 1) * 512]
                            I(p, "scalar", lambda a, d=dst, s=psq[tb][:]: a.copy(d, s),
                              waits=cw, inc=(sACT, sACT.take()))
                            evac[(oc, tb)] = sACT.n
                        tgt = pj_s[oc:oc + 1].rearrange("o p t -> (o p) t")
                        I(p, "sync", lambda s, d=tgt, s2=stg[par][:]: s.dma_start(d, s2),
                          waits=[(sACT, evac[(oc, TB - 1)])],
                          inc=(sSt[par], sSt[par].take(16)))
                        store[oc] = (sSt[par], sSt[par].n)
                    p.emit(nc)

                if n_phases >= 5:
                    with contextlib.ExitStack() as _es:
                        pblock = _es.enter_context(nc.sbuf_tensor("pblock", [128, HC * 512], BF16))
                        xst0 = _es.enter_context(nc.sbuf_tensor("xst0", [128, 512], F32))
                        xst1 = _es.enter_context(nc.sbuf_tensor("xst1", [128, 512], F32))
                        x1c0 = _es.enter_context(nc.sbuf_tensor("x1c0", [128, 512], F32))
                        x1c1 = _es.enter_context(nc.sbuf_tensor("x1c1", [128, 512], F32))
                        sq3a = _es.enter_context(nc.sbuf_tensor("sq3a", [128, 512], BF16))
                        sq3b = _es.enter_context(nc.sbuf_tensor("sq3b", [128, 512], BF16))
                        t1c = _es.enter_context(nc.sbuf_tensor("t1c", [128, 512], BF16))
                        t2c = _es.enter_context(nc.sbuf_tensor("t2c", [128, 512], F32))
                        rows3 = _es.enter_context(nc.sbuf_tensor("rows3", [1, 512 * 3], F32))
                        rowM3 = _es.enter_context(nc.sbuf_tensor("rowM3", [128, 512], BF16))
                        rowR3 = _es.enter_context(nc.sbuf_tensor("rowR3", [128, 512], BF16))
                        Ma = _es.enter_context(nc.sbuf_tensor("Ma", [128, 512], BF16))
                        Ra = _es.enter_context(nc.sbuf_tensor("Ra", [128, 512], BF16))
                        M2 = _es.enter_context(nc.sbuf_tensor("M2", [128, 512], BF16))
                        R2 = _es.enter_context(nc.sbuf_tensor("R2", [128, 512], BF16))
                        ps_t1 = _es.enter_context(nc.psum_tensor("ps_t1", [1, 512], F32))
                        ps_t2 = _es.enter_context(nc.psum_tensor("ps_t2", [1, 512], F32))
                        ps_bM2 = _es.enter_context(nc.psum_tensor("ps_bM2", [128, 512], F32))
                        ps_bR2 = _es.enter_context(nc.psum_tensor("ps_bR2", [128, 512], F32))
                        sq3 = [sq3a, sq3b]
                        xst = [xst0, xst1]
                        x1c = [x1c0, x1c1]
                        rA = rows3[:, 0:512]
                        rB = rows3[:, 512:1024]
                        rC = rows3[:, 1024:1536]
                        p = Plan()
                        I(p, "gpsimd", lambda g: g.memset(rowM3[:], 0.0))
                        I(p, "gpsimd", lambda g: g.memset(rowR3[:], 0.0),
                          inc=(sPL, sPL.take()))
                        pad_ready = sPL.n
                        rows_evac = 0
                        h2_done = 0
                        bc_mm = 0
                        bc_evac = 0
                        x1m_prev = 0
                        x1done = {}
                        x1_store = {}
                        cast_done = {}
                        h2done = {}
                        for tb in range(TB):
                            t0 = tb * 512
                            lw = [(sDVE, h2_done)] if tb else []
                            for g4 in range(4):
                                src = pj_s[g4 * 8:(g4 + 1) * 8, :, t0:t0 + 512].rearrange(
                                    "c p t -> p c t")
                                I(p, "sync",
                                  lambda s, d=sp3(pblock[:, g4 * 4096:(g4 + 1) * 4096], 512), s2=src:
                                  s.dma_start(d, s2),
                                  waits=lw if g4 == 0 else (), inc=(sSP, sSP.take(16)))
                            blk_ld = sSP.n
                            pe_c = {}
                            for c in range(HC):
                                cw = [(sSP, blk_ld)] if c == 0 else []
                                if c >= 2:
                                    cw.append((sPE, pe_c[c - 2]))
                                psl = pblock[:, c * 512:(c + 1) * 512]
                                I(p, "vector",
                                  lambda v, a=psl, d=sq3[c % 2][:]: v.tensor_mul(d, a, a),
                                  waits=cw, inc=(sDVE, sDVE.take()))
                                sqd = sDVE.n
                                cw = [(sDVE, sqd)]
                                if c == 0 and rows_evac:
                                    cw.append((sACT, rows_evac))
                                I(p, "tensor", lambda t, r=psl,
                                  c0=(c == 0), c1=(c == HC - 1):
                                  t.matmul(ps_t1[:], onescol, r, start=c0, stop=c1),
                                  waits=cw)
                                I(p, "tensor", lambda t, r=sq3[c % 2][:],
                                  c0=(c == 0), c1=(c == HC - 1):
                                  t.matmul(ps_t2[:], onescol, r, start=c0, stop=c1),
                                  inc=(sPE, sPE.take()))
                                pe_c[c] = sPE.n
                            stats_done = sPE.n
                            rows_evac = ln_rows(p, ps_t1, ps_t2, rA, rB, rC,
                                                [(sPE, stats_done)])
                            rw = [(sACT, rows_evac)] + \
                                 ([(sPE, bc_mm)] if tb else [(sPL, pad_ready)])
                            I(p, "scalar", lambda a: a.copy(rowM3[0:1, :], rA), waits=rw)
                            I(p, "scalar", lambda a: a.copy(rowR3[0:1, :], rB),
                              inc=(sACT, sACT.take()))
                            rows_b = sACT.n
                            bw = [(sACT, rows_b)] + ([(sACT, bc_evac)] if tb else [])
                            I(p, "tensor", lambda t: t.matmul(
                                ps_bM2[:], onesM[:], rowM3[:], start=True, stop=True),
                              waits=bw)
                            I(p, "tensor", lambda t: t.matmul(
                                ps_bR2[:], onesM[:], rowR3[:], start=True, stop=True),
                              inc=(sPE, sPE.take()))
                            bc_mm = sPE.n
                            ew = [(sPE, bc_mm)] + ([(sDVE, x1m_prev)] if tb else [])
                            I(p, "scalar", lambda a: a.copy(Ma[:], ps_bM2[:]), waits=ew)
                            I(p, "scalar", lambda a: a.copy(Ra[:], ps_bR2[:]),
                              inc=(sACT, sACT.take()))
                            bc_evac = sACT.n
                            pe_c2 = {}
                            for c in range(HC):
                                gsl = tb * HC + c
                                j = c % 2
                                srcx = x_t[c:c + 1, :, t0:t0 + 512].rearrange(
                                    "o p t -> (o p) t")
                                lwx = []
                                if gsl >= 2:
                                    lwx.append((sDVE, x1done[gsl - 2]))
                                I(p, "sync", lambda s, d=xst[j][:], s2=srcx:
                                  s.dma_start(d, s2),
                                  waits=lwx, inc=(sXl[j], sXl[j].take(16)))
                                xld = (sXl[j], sXl[j].n)
                                psl = pblock[:, c * 512:(c + 1) * 512]
                                cw = [(sACT, bc_evac)] if c == 0 else []
                                I(p, "vector", lambda v, a=psl: v.tensor_sub(t1c[:], a, Ma[:]),
                                  waits=cw)
                                I(p, "vector", lambda v: v.tensor_mul(t2c[:], t1c[:], Ra[:]))
                                cw = [xld]
                                if gsl >= 2:
                                    cw.append(x1_store[gsl - 2])
                                    cw.append((sACT, cast_done[gsl - 2]))
                                I(p, "vector", lambda v, x=xst[j][:], d=x1c[j][:]:
                                  v.tensor_add(d, t2c[:], x),
                                  waits=cw, inc=(sDVE, sDVE.take()))
                                x1done[gsl] = sDVE.n
                                dstx1 = x1_s[c:c + 1, :, t0:t0 + 512].rearrange(
                                    "o p t -> (o p) t")
                                I(p, "gpsimd", lambda g, d=dstx1, s=x1c[j][:]:
                                  g.dma_start(d, s),
                                  waits=[(sDVE, x1done[gsl])],
                                  inc=(sXs[j], sXs[j].take(16)))
                                x1_store[gsl] = (sXs[j], sXs[j].n)
                                cwc = [(sDVE, x1done[gsl])]
                                if c >= 1:
                                    cwc.append((sPE, pe_c2[c - 1]))
                                I(p, "scalar", lambda a, s=x1c[j][:], d=psl: a.copy(d, s),
                                  waits=cwc, inc=(sACT, sACT.take()))
                                cast_done[gsl] = sACT.n
                                cd = sACT.n
                                I(p, "vector", lambda v, d=sq3[j][:], a=psl:
                                  v.tensor_mul(d, a, a),
                                  waits=[(sACT, cd)], inc=(sDVE, sDVE.take()))
                                sqd = sDVE.n
                                I(p, "tensor", lambda t, r=psl, c0=(c == 0), c1=(c == HC - 1):
                                  t.matmul(ps_t1[:], onescol, r, start=c0, stop=c1),
                                  waits=[(sDVE, sqd)])
                                I(p, "tensor", lambda t, r=sq3[j][:],
                                  c0=(c == 0), c1=(c == HC - 1):
                                  t.matmul(ps_t2[:], onescol, r, start=c0, stop=c1),
                                  inc=(sPE, sPE.take()))
                                pe_c2[c] = sPE.n
                            x1m_prev = sDVE.n
                            stats2_done = sPE.n
                            rows_evac = ln_rows(p, ps_t1, ps_t2, rA, rB, rC,
                                                [(sPE, stats2_done)])
                            rw = [(sACT, rows_evac), (sPE, bc_mm)]
                            I(p, "scalar", lambda a: a.copy(rowM3[0:1, :], rA), waits=rw)
                            I(p, "scalar", lambda a: a.copy(rowR3[0:1, :], rB),
                              inc=(sACT, sACT.take()))
                            rows_b = sACT.n
                            bw = [(sACT, rows_b), (sACT, bc_evac)]
                            I(p, "tensor", lambda t: t.matmul(
                                ps_bM2[:], onesM[:], rowM3[:], start=True, stop=True),
                              waits=bw)
                            I(p, "tensor", lambda t: t.matmul(
                                ps_bR2[:], onesM[:], rowR3[:], start=True, stop=True),
                              inc=(sPE, sPE.take()))
                            bc_mm = sPE.n
                            ew = [(sPE, bc_mm)] + ([(sDVE, h2_done)] if tb else [])
                            I(p, "scalar", lambda a: a.copy(M2[:], ps_bM2[:]), waits=ew)
                            I(p, "scalar", lambda a: a.copy(R2[:], ps_bR2[:]),
                              inc=(sACT, sACT.take()))
                            bc_evac = sACT.n
                            for c in range(HC):
                                gsl = tb * HC + c
                                psl = pblock[:, c * 512:(c + 1) * 512]
                                cw = [(sACT, bc_evac)] if c == 0 else []
                                dst = big[:, c * S + t0:c * S + t0 + 512]
                                I(p, "vector", lambda v, a=psl:
                                  v.tensor_sub(t1c[:], a, M2[:]), waits=cw)
                                I(p, "vector", lambda v, d=dst: v.tensor_mul(d, t1c[:], R2[:]),
                                  inc=(sDVE, sDVE.take()))
                                h2done[gsl] = sDVE.n
                            h2_done = sDVE.n
                        if debug:
                            I(p, "sync", lambda s: s.dma_start(h2_d, big[:]),
                              waits=[(sDVE, h2_done)], inc=(sSP, sSP.take(16)))
                            p.add("sync", lambda eng, v=sSP.n, h=sSP.h: eng.wait_ge(h, v))
                        p.emit(nc)

                if n_phases >= 6:
                    with contextlib.ExitStack() as _es:
                        bact = _es.enter_context(nc.sbuf_tensor("bact", [128, 8 * S], BF16))
                        wgb0 = _es.enter_context(nc.sbuf_tensor("wgb0", [128, 16 * 128], BF16))
                        wgb1 = _es.enter_context(nc.sbuf_tensor("wgb1", [128, 16 * 128], BF16))
                        wub0 = _es.enter_context(nc.sbuf_tensor("wub0", [128, 16 * 128], BF16))
                        wub1 = _es.enter_context(nc.sbuf_tensor("wub1", [128, 16 * 128], BF16))
                        wdf0 = _es.enter_context(nc.sbuf_tensor("wdf0", [128, 8 * 128], F32))
                        wdf1 = _es.enter_context(nc.sbuf_tensor("wdf1", [128, 8 * 128], F32))
                        wdb0 = _es.enter_context(nc.sbuf_tensor("wdb0", [128, 8 * 128], BF16))
                        wdb1 = _es.enter_context(nc.sbuf_tensor("wdb1", [128, 8 * 128], BF16))
                        s1a = _es.enter_context(nc.sbuf_tensor("s1a", [128, 512], BF16))
                        s1b = _es.enter_context(nc.sbuf_tensor("s1b", [128, 512], BF16))
                        pst0 = _es.enter_context(nc.sbuf_tensor("pst0", [128, 1024], F32))
                        pst1 = _es.enter_context(nc.sbuf_tensor("pst1", [128, 1024], F32))
                        ps_g0 = _es.enter_context(nc.psum_tensor("ps_g0", [128, 512], F32))
                        ps_g1 = _es.enter_context(nc.psum_tensor("ps_g1", [128, 512], F32))
                        ps_g2 = _es.enter_context(nc.psum_tensor("ps_g2", [128, 512], F32))
                        ps_g3 = _es.enter_context(nc.psum_tensor("ps_g3", [128, 512], F32))
                        ps_u0 = _es.enter_context(nc.psum_tensor("ps_u0", [128, 512], F32))
                        ps_u1 = _es.enter_context(nc.psum_tensor("ps_u1", [128, 512], F32))
                        ps_u2 = _es.enter_context(nc.psum_tensor("ps_u2", [128, 512], F32))
                        ps_u3 = _es.enter_context(nc.psum_tensor("ps_u3", [128, 512], F32))
                        wgb = [wgb0, wgb1]
                        wub = [wub0, wub1]
                        wdf = [wdf0, wdf1]
                        wdb = [wdb0, wdb1]
                        s1 = [s1a, s1b]
                        pst = [pst0, pst1]
                        gb = [ps_g0, ps_g1, ps_g2, ps_g3]
                        ub = [ps_u0, ps_u1, ps_u2, ps_u3]
                        dbanks = [gb, ub]
                        p = Plan()
                        silu_cnt = {}
                        act_cnt = {}
                        g_half = {}
                        u_half = {}
                        wgld = {}
                        wuld = {}
                        pd_evac = {}
                        cast_cnt = {}
                        wdb_free = {}
                        pst_free = [0, 0]
                        pd_last_super = 0
                        pd_evac_tail = 0
                        for isup, sup in enumerate(SUPERS):
                            nIc = len(sup)
                            for icl, Ic in enumerate(sup):
                                for half in range(2):
                                    u = Ic * 2 + half
                                    par = u % 2
                                    ksl = slice(half * 16, half * 16 + 16)
                                    lw = [(sPEg, g_half[u - 2])] if u >= 2 else []
                                    I(p, "gpsimd", lambda g, d=wgb[par][:],
                                      s=wg_t[Ic:Ic + 1, :, ksl, :].rearrange(
                                          "o p c i -> (o p) (c i)"):
                                      g.dma_start(d, s), waits=lw,
                                      inc=(sWg[par], sWg[par].take(16)))
                                    wgld[u] = (sWg[par], sWg[par].n)
                                    lw = [(sPEu, u_half[u - 2])] if u >= 2 else []
                                    I(p, "gpsimd", lambda g, d=wub[par][:],
                                      s=wu_t[Ic:Ic + 1, :, ksl, :].rearrange(
                                          "o p c i -> (o p) (c i)"):
                                      g.dma_start(d, s), waits=lw,
                                      inc=(sWu[par], sWu[par].take(16)))
                                    wuld[u] = (sWu[par], sWu[par].n)
                                for kc in range(HC):
                                    u = Ic * 2 + kc // 16
                                    lhs = wgb[u % 2][:, (kc % 16) * 128:(kc % 16) * 128 + 128]
                                    for tb in range(TB):
                                        cw = []
                                        if kc % 16 == 0 and tb == 0:
                                            cw.append(wgld[u])
                                        if kc == 0:
                                            prev = silu_cnt.get((Ic - 1, tb), 0)
                                            if prev:
                                                cw.append((sACTs, prev))
                                            if icl == 0 and isup >= 1 and tb == 0:
                                                cw.append((sACTd, pd_evac_tail))
                                        rhs = big[:, kc * S + tb * 512:kc * S + tb * 512 + 512]
                                        inc = None
                                        if (kc == 15 and tb == TB - 1) or kc == HC - 1:
                                            inc = (sPEg, sPEg.take())
                                        I(p, "tensor", lambda t, o=gb[tb][:], l=lhs, r=rhs,
                                          c0=(kc == 0), c1=(kc == HC - 1):
                                          t.matmul(o, l, r, start=c0, stop=c1),
                                          waits=cw, inc=inc)
                                        if kc == 15 and tb == TB - 1:
                                            g_half[Ic * 2] = sPEg.n
                                        if kc == HC - 1:
                                            if tb == TB - 1:
                                                g_half[Ic * 2 + 1] = sPEg.n
                                            silu_cnt[("pe", Ic, tb)] = sPEg.n
                                for kc in range(HC):
                                    u = Ic * 2 + kc // 16
                                    lhs = wub[u % 2][:, (kc % 16) * 128:(kc % 16) * 128 + 128]
                                    for tb in range(TB):
                                        cw = []
                                        if kc % 16 == 0 and tb == 0:
                                            cw.append(wuld[u])
                                        if kc == 0:
                                            prev = act_cnt.get((Ic - 1, tb), 0)
                                            if prev:
                                                cw.append((sDVEa, prev))
                                            if icl == 0 and isup >= 1 and tb == 0:
                                                cw.append((sACTd, pd_evac_tail))
                                        rhs = big[:, kc * S + tb * 512:kc * S + tb * 512 + 512]
                                        inc = None
                                        if (kc == 15 and tb == TB - 1) or kc == HC - 1:
                                            inc = (sPEu, sPEu.take())
                                        I(p, "tensor", lambda t, o=ub[tb][:], l=lhs, r=rhs,
                                          c0=(kc == 0), c1=(kc == HC - 1):
                                          t.matmul(o, l, r, start=c0, stop=c1),
                                          waits=cw, inc=inc)
                                        if kc == 15 and tb == TB - 1:
                                            u_half[Ic * 2] = sPEu.n
                                        if kc == HC - 1:
                                            if tb == TB - 1:
                                                u_half[Ic * 2 + 1] = sPEu.n
                                            act_cnt[("pe", Ic, tb)] = sPEu.n
                                for tb in range(TB):
                                    js = tb % 2
                                    cw = [(sPEg, silu_cnt[("pe", Ic, tb)])]
                                    prevd = act_cnt.get((Ic, tb - 2) if tb >= 2 else
                                                        (Ic - 1, tb + 2), 0)
                                    if prevd:
                                        cw.append((sDVEa, prevd))
                                    I(p, "scalar", lambda a, d=s1[js][:], s=gb[tb][:]:
                                      a.activation(d, s, AF.Silu),
                                      waits=cw, inc=(sACTs, sACTs.take()))
                                    silu_cnt[(Ic, tb)] = sACTs.n
                                    cw = [(sACTs, silu_cnt[(Ic, tb)]),
                                          (sPEu, act_cnt[("pe", Ic, tb)])]
                                    if isup >= 1:
                                        cw.append((sPEd, pd_last_super))
                                    dst = bact[:, icl * S + tb * 512:icl * S + tb * 512 + 512]
                                    I(p, "vector", lambda v, d=dst, a=s1[js][:], b=ub[tb][:]:
                                      v.tensor_mul(d, a, b),
                                      waits=cw, inc=(sDVEa, sDVEa.take()))
                                    act_cnt[(Ic, tb)] = sDVEa.n
                            act_all = sDVEa.n

                            def em_wd(hc, isup=isup, sup=sup, nIc=nIc):
                                par = hc % 2
                                gw = isup * HC + hc
                                lw = [(sACTn, cast_cnt[gw - 2])] if gw >= 2 else []
                                src = wd_t[hc:hc + 1, :, sup[0]:sup[0] + nIc, :].rearrange(
                                    "o p c i -> (o p) (c i)")
                                I(p, "sync", lambda s, d=wdf[par][:, 0:nIc * 128], s2=src:
                                  s.dma_start(d, s2), waits=lw,
                                  inc=(sWd[par], sWd[par].take(16)))
                                wld_d = (sWd[par], sWd[par].n)
                                cw = [wld_d]
                                if gw >= 2:
                                    cw.append((sPEd, wdb_free[gw - 2]))
                                I(p, "scalar", lambda a, d=wdb[par][:, 0:nIc * 128],
                                  s=wdf[par][:, 0:nIc * 128]: a.copy(d, s),
                                  waits=cw, inc=(sACTn, sACTn.take()))
                                cast_cnt[gw] = sACTn.n

                            em_wd(0)
                            for hc in range(HC):
                                par = hc % 2
                                gw = isup * HC + hc
                                if hc + 1 < HC:
                                    em_wd(hc + 1)
                                banks = dbanks[hc % 2]
                                d_done = {}
                                for icl in range(nIc):
                                    lhs = wdb[par][:, icl * 128:(icl + 1) * 128]
                                    for tb in range(TB):
                                        dslot = gw * TB + tb
                                        cw = []
                                        if icl == 0:
                                            if tb == 0:
                                                cw.append((sACTn, cast_cnt[gw]))
                                                cw.append((sDVEa, act_all))
                                            prev = pd_evac.get(dslot - 2 * TB, 0)
                                            if prev:
                                                cw.append((sACTd, prev))
                                        rhs = bact[:, icl * S + tb * 512:icl * S + tb * 512 + 512]
                                        inc = None
                                        if icl == nIc - 1:
                                            inc = (sPEd, sPEd.take())
                                        I(p, "tensor", lambda t, o=banks[tb][:], l=lhs, r=rhs,
                                          c0=(icl == 0), c1=(icl == nIc - 1):
                                          t.matmul(o, l, r, start=c0, stop=c1),
                                          waits=cw, inc=inc)
                                        if icl == nIc - 1:
                                            d_done[tb] = sPEd.n
                                wdb_free[gw] = sPEd.n
                                for tb in range(TB):
                                    dslot = gw * TB + tb
                                    pp = (dslot // 2) % 2
                                    cw = [(sPEd, d_done[tb])]
                                    if tb % 2 == 0 and pst_free[pp]:
                                        cw.append(pst_free[pp])
                                    I(p, "scalar",
                                      lambda a, d=pst[pp][:, (tb % 2) * 512:(tb % 2) * 512 + 512],
                                      s=banks[tb][:]: a.copy(d, s),
                                      waits=cw, inc=(sACTd, sACTd.take()))
                                    pd_evac[dslot] = sACTd.n
                                    if tb % 2 == 1:
                                        dsta = ac_s[hc:hc + 1, :,
                                                    (tb // 2) * 1024:(tb // 2) * 1024 + 1024
                                                    ].rearrange("o p t -> (o p) t")
                                        op = ALU.add if isup >= 1 else ALU.bypass
                                        I(p, "gpsimd", lambda g, d=dsta, s=pst[pp][:], o=op:
                                          g.dma_start(d, s, accum_op=o),
                                          waits=[(sACTd, pd_evac[dslot])],
                                          inc=(sAc[pp], sAc[pp].take(16)))
                                        pst_free[pp] = (sAc[pp], sAc[pp].n)
                            pd_last_super = sPEd.n
                            pd_evac_tail = sACTd.n
                        p.add("gpsimd", lambda eng, v=sAc[0].n, h=sAc[0].h: eng.wait_ge(h, v))
                        p.add("gpsimd", lambda eng, v=sAc[1].n, h=sAc[1].h: eng.wait_ge(h, v))
                        p.emit(nc)

        if n_phases >= 7:
            with contextlib.ExitStack() as _es:
                outb = _es.enter_context(nc.sbuf_tensor("outb", [128, HC * 512], F32))
                x1b6 = _es.enter_context(nc.sbuf_tensor("x1b6", [128, HC * 512], F32))
                ob16 = _es.enter_context(nc.sbuf_tensor("ob16", [128, 512], BF16))
                sq6a = _es.enter_context(nc.sbuf_tensor("sq6a", [128, 512], BF16))
                sq6b = _es.enter_context(nc.sbuf_tensor("sq6b", [128, 512], BF16))
                t6 = _es.enter_context(nc.sbuf_tensor("t6", [128, 512], F32))
                rows6 = _es.enter_context(nc.sbuf_tensor("rows6", [1, 512 * 3], F32))
                rowM6 = _es.enter_context(nc.sbuf_tensor("rowM6", [128, 512], BF16))
                rowR6 = _es.enter_context(nc.sbuf_tensor("rowR6", [128, 512], BF16))
                M6 = _es.enter_context(nc.sbuf_tensor("M6", [128, 512], F32))
                R6 = _es.enter_context(nc.sbuf_tensor("R6", [128, 512], F32))
                ps_61 = _es.enter_context(nc.psum_tensor("ps_61", [1, 512], F32))
                ps_62 = _es.enter_context(nc.psum_tensor("ps_62", [1, 512], F32))
                ps_6M = _es.enter_context(nc.psum_tensor("ps_6M", [128, 512], F32))
                ps_6R = _es.enter_context(nc.psum_tensor("ps_6R", [128, 512], F32))
                sq6 = [sq6a, sq6b]
                rA = rows6[:, 0:512]
                rB = rows6[:, 512:1024]
                rC = rows6[:, 1024:1536]
                p = Plan()
                I(p, "gpsimd", lambda g: g.memset(rowM6[:], 0.0))
                I(p, "gpsimd", lambda g: g.memset(rowR6[:], 0.0), inc=(sPL, sPL.take()))
                pad_ready = sPL.n
                rows_evac = 0
                y_math = 0
                bc_mm = 0
                bc_evac = 0
                y_store = 0
                for tb in range(TB):
                    t0 = tb * 512
                    lw = [(sY, y_store)] if tb else []
                    for g4 in range(4):
                        src = ac_s[g4 * 8:(g4 + 1) * 8, :, t0:t0 + 512].rearrange(
                            "c p t -> p c t")
                        I(p, "sync",
                          lambda s, d=sp3(outb[:, g4 * 4096:(g4 + 1) * 4096], 512), s2=src:
                          s.dma_start(d, s2),
                          waits=lw if g4 == 0 else (), inc=(sSP, sSP.take(16)))
                        src2 = x1_s[g4 * 8:(g4 + 1) * 8, :, t0:t0 + 512].rearrange(
                            "c p t -> p c t")
                        I(p, "sync",
                          lambda s, d=sp3(x1b6[:, g4 * 4096:(g4 + 1) * 4096], 512), s2=src2:
                          s.dma_start(d, s2), inc=(sSP, sSP.take(16)))
                    blk_ld = sSP.n
                    pe_c = {}
                    dve_c = {}
                    for c in range(HC):
                        osl = outb[:, c * 512:(c + 1) * 512]
                        cw = [(sSP, blk_ld)] if c == 0 else \
                            [(sDVE, dve_c[c - 1]), (sPE, pe_c[c - 1])]
                        I(p, "scalar", lambda a, s=osl: a.copy(ob16[:], s),
                          waits=cw, inc=(sACT, sACT.take()))
                        cd = sACT.n
                        I(p, "vector", lambda v, d=sq6[c % 2][:]:
                          v.tensor_mul(d, ob16[:], ob16[:]),
                          waits=[(sACT, cd)], inc=(sDVE, sDVE.take()))
                        dve_c[c] = sDVE.n
                        cw = [(sDVE, dve_c[c])]
                        if c == 0 and rows_evac:
                            cw.append((sACT, rows_evac))
                        I(p, "tensor", lambda t, c0=(c == 0), c1=(c == HC - 1):
                          t.matmul(ps_61[:], onescol, ob16[:], start=c0, stop=c1),
                          waits=cw)
                        I(p, "tensor", lambda t, r=sq6[c % 2][:],
                          c0=(c == 0), c1=(c == HC - 1):
                          t.matmul(ps_62[:], onescol, r, start=c0, stop=c1),
                          inc=(sPE, sPE.take()))
                        pe_c[c] = sPE.n
                    stats_done = sPE.n
                    rows_evac = ln_rows(p, ps_61, ps_62, rA, rB, rC, [(sPE, stats_done)])
                    rw = [(sACT, rows_evac)] + \
                         ([(sPE, bc_mm)] if tb else [(sPL, pad_ready)])
                    I(p, "scalar", lambda a: a.copy(rowM6[0:1, :], rA), waits=rw)
                    I(p, "scalar", lambda a: a.copy(rowR6[0:1, :], rB),
                      inc=(sACT, sACT.take()))
                    rows_b = sACT.n
                    bw = [(sACT, rows_b)] + ([(sACT, bc_evac)] if tb else [])
                    I(p, "tensor", lambda t: t.matmul(ps_6M[:], onesM[:], rowM6[:],
                                                      start=True, stop=True), waits=bw)
                    I(p, "tensor", lambda t: t.matmul(ps_6R[:], onesM[:], rowR6[:],
                                                      start=True, stop=True),
                      inc=(sPE, sPE.take()))
                    bc_mm = sPE.n
                    ew = [(sPE, bc_mm)] + ([(sDVE, y_math)] if tb else [])
                    I(p, "scalar", lambda a: a.copy(M6[:], ps_6M[:]), waits=ew)
                    I(p, "scalar", lambda a: a.copy(R6[:], ps_6R[:]),
                      inc=(sACT, sACT.take()))
                    bc_evac = sACT.n
                    for c in range(HC):
                        cw = [(sACT, bc_evac)] if c == 0 else []
                        osl = outb[:, c * 512:(c + 1) * 512]
                        xsl = x1b6[:, c * 512:(c + 1) * 512]
                        I(p, "vector", lambda v, a=osl: v.tensor_sub(t6[:], a, M6[:]),
                          waits=cw)
                        I(p, "vector", lambda v: v.tensor_mul(t6[:], t6[:], R6[:]))
                        inc = (sDVE, sDVE.take()) if c == HC - 1 else None
                        I(p, "vector", lambda v, d=osl, x=xsl: v.tensor_add(d, t6[:], x),
                          inc=inc)
                    y_math = sDVE.n
                    for g4 in range(4):
                        dst = y_t[g4 * 8:(g4 + 1) * 8, :, t0:t0 + 512].rearrange(
                            "c p t -> p c t")
                        I(p, "gpsimd",
                          lambda g, d=dst, s=sp3(outb[:, g4 * 4096:(g4 + 1) * 4096], 512):
                          g.dma_start(d, s),
                          waits=[(sDVE, y_math)] if g4 == 0 else (),
                          inc=(sY, sY.take(16)))
                    y_store = sY.n
                p.add("sync", lambda eng, v=sY.n, h=sY.h: eng.wait_ge(h, v))
                p.emit(nc)

    nc.finalize()
    return nc


def _alibi_slopes(n):
    cp2 = 2 ** math.floor(math.log2(n))
    base = 2.0 ** (-8.0 / cp2)
    slopes = base ** np.arange(1, cp2 + 1, dtype=np.float32)
    if cp2 != n:
        extra_base = 2.0 ** (-4.0 / cp2)
        extra = extra_base ** np.arange(1, 1 + 2 * (n - cp2), 2, dtype=np.float32)
        slopes = np.concatenate([slopes, extra])
    return slopes.astype(np.float32)


def host_inputs(L0_attn_w_qkv, L0_attn_w_o, L0_ffn_w_gate, L0_ffn_w_up, L0_ffn_w_down):
    bf = ml_dtypes.bfloat16
    wqkv = np.asarray(L0_attn_w_qkv, dtype=np.float32)
    wo = np.asarray(L0_attn_w_o, dtype=np.float32)
    wg = np.asarray(L0_ffn_w_gate, dtype=np.float32)
    wu = np.asarray(L0_ffn_w_up, dtype=np.float32)
    wd = np.asarray(L0_ffn_w_down, dtype=np.float32)
    shared = {}
    shared["wqk_t"] = np.ascontiguousarray(
        wqkv[:, :40 * 128].reshape(HC, 128, 40, 128).transpose(2, 1, 0, 3))
    shared["wv_t"] = np.ascontiguousarray(
        wqkv[:, 40 * 128:].reshape(HC, 128, 1024).transpose(1, 0, 2))
    shared["wo_t"] = np.ascontiguousarray(
        wo.reshape(HC, 128, HC, 128).transpose(2, 1, 0, 3))
    shared["wg_t"] = np.ascontiguousarray(
        wg.reshape(IC, 128, HC, 128).transpose(0, 3, 2, 1))
    shared["wu_t"] = np.ascontiguousarray(
        wu.reshape(IC, 128, HC, 128).transpose(0, 3, 2, 1))
    shared["wd_t"] = np.ascontiguousarray(
        wd.reshape(HC, 128, IC, 128).transpose(0, 3, 2, 1))
    q = np.arange(S, dtype=np.float32)
    rel = q[None, :] - q[:, None]
    Dt = np.sqrt(np.maximum(rel, 0.0))
    Dt[rel < 0] = MASK_D
    shared["dT_c"] = np.ascontiguousarray(Dt.astype(bf).reshape(16, 128, S))
    slopes = _alibi_slopes(NH)
    dneg = np.zeros((128, NH, 128), dtype=np.float32)
    for h in range(NH):
        dneg[np.arange(128), h, np.arange(128)] = -slopes[h] / SCALE
    shared["dneg_c"] = np.ascontiguousarray(dneg.reshape(128, NH * 128).astype(bf))
    shared["onesM_c"] = np.ones((128, 128), dtype=bf)
    return shared


def kernel(L0_attn_hidden_states, L0_attn_w_qkv, L0_attn_w_o,
           L0_ffn_w_gate, L0_ffn_w_up, L0_ffn_w_down):
    x = np.asarray(L0_attn_hidden_states, dtype=np.float32)
    B = x.shape[0]
    shared = host_inputs(L0_attn_w_qkv, L0_attn_w_o, L0_ffn_w_gate,
                         L0_ffn_w_up, L0_ffn_w_down)
    nc = build_program()
    in_maps = []
    for c in range(B):
        m = dict(shared)
        m["x_t"] = np.ascontiguousarray(x[c].T).reshape(HC, 128, S)
        in_maps.append(m)
    res = run_bass_kernel_spmd(nc, in_maps, core_ids=list(range(B)))
    y = np.empty((B, S, H), dtype=np.float32)
    for c in range(B):
        y[c] = res.results[c]["y_t"].reshape(H, S).T
    return y


# version 10
# speedup vs baseline: 1.1357x; 1.1162x over previous
import math
import contextlib
import numpy as np
import ml_dtypes
import concourse.bass as bass
import concourse.mybir as mybir
from concourse.bass_utils import run_bass_kernel_spmd

F32 = mybir.dt.float32
BF16 = mybir.dt.bfloat16
AF = mybir.ActivationFunctionType
ALU = mybir.AluOpType
AX = mybir.AxisListType

S = 2048
H = 4096
NH, NKV, G, D = 32, 8, 4, 128
INTER = 11008
HC = H // 128
IC = INTER // 128
TB = 4
SCALE = D ** -0.5
INV_H = 1.0 / H
EPS = 1e-5
MASK_D = 30000.0

ENGINES = ("sync", "gpsimd", "tensor", "scalar", "vector")
SUPERS = [list(range(s, min(s + 8, IC))) for s in range(0, IC, 8)]


class Sem:
    __slots__ = ("h", "n", "name")

    def __init__(self, h, name):
        self.h = h
        self.n = 0
        self.name = name

    def bump(self, k=1):
        self.n += k
        return self.n

    def take(self, k=1):
        self.n += k
        return k


class Plan:
    def __init__(self):
        self.ev = {e: [] for e in ENGINES}

    def add(self, e, f):
        self.ev[e].append(f)

    def emit(self, nc):
        with nc.Block() as blk:
            for e in ENGINES:
                evs = self.ev[e]
                if not evs:
                    continue

                def mk(evs=evs):
                    def body(eng):
                        for f in evs:
                            f(eng)
                    return body
                getattr(blk, e)(mk())


def I(plan, e, build, waits=(), inc=None, inc2=None):
    waits = tuple((s.h, v) for (s, v) in waits)

    def f(eng, build=build, waits=waits, inc=inc, inc2=inc2):
        for h, v in waits:
            eng.wait_ge(h, v)
        ins = build(eng)
        if inc is not None:
            ins = ins.then_inc(inc[0].h, inc[1])
        if inc2 is not None:
            ins = ins.then_inc(inc2[0].h, inc2[1])
    plan.add(e, f)


def sp3(ap, w):
    return ap.rearrange("p (c t) -> p c t", t=w)


def build_program(n_phases=99, debug=False):
    nc = bass.Bass("TRN2", target_bir_lowering=False, debug=False)
    dbg_kind = "ExternalOutput" if debug else "Internal"

    x_t = nc.dram_tensor("x_t", [HC, 128, S], F32, kind="ExternalInput").ap()
    wqk_t = nc.dram_tensor("wqk_t", [40, 128, HC, 128], F32, kind="ExternalInput").ap()
    wv_t = nc.dram_tensor("wv_t", [128, HC, 1024], F32, kind="ExternalInput").ap()
    wo_t = nc.dram_tensor("wo_t", [HC, 128, HC, 128], F32, kind="ExternalInput").ap()
    wg_t = nc.dram_tensor("wg_t", [IC, 128, HC, 128], F32, kind="ExternalInput").ap()
    wu_t = nc.dram_tensor("wu_t", [IC, 128, HC, 128], F32, kind="ExternalInput").ap()
    wd_t = nc.dram_tensor("wd_t", [HC, 128, IC, 128], F32, kind="ExternalInput").ap()
    dT_c = nc.dram_tensor("dT_c", [16, 128, S], BF16, kind="ExternalInput").ap()
    dneg_c = nc.dram_tensor("dneg_c", [128, NH * 128], BF16, kind="ExternalInput").ap()
    onesM_c = nc.dram_tensor("onesM_c", [128, 128], BF16, kind="ExternalInput").ap()

    q_s = nc.dram_tensor("q_s", [NH, 128, S], BF16, kind=dbg_kind).ap()
    k_s = nc.dram_tensor("k_s", [NKV, 128, S], BF16, kind=dbg_kind).ap()
    v_s = nc.dram_tensor("v_s", [16, 128, 1024], BF16, kind=dbg_kind).ap()
    ao_s = nc.dram_tensor("ao_s", [NH, 128, S], BF16, kind=dbg_kind).ap()
    pj_s = nc.dram_tensor("pj_s", [HC, 128, S], BF16, kind=dbg_kind).ap()
    x1_s = nc.dram_tensor("x1_s", [HC, 128, S], F32, kind=dbg_kind).ap()
    ac_s = nc.dram_tensor("ac_s", [HC, 128, S], F32, kind=dbg_kind).ap()
    y_t = nc.dram_tensor("y_t", [HC, 128, S], F32, kind="ExternalOutput").ap()
    h1_d = h2_d = None
    if debug:
        h1_d = nc.dram_tensor("h1_d", [128, HC * S], BF16, kind="ExternalOutput").ap()
        h2_d = nc.dram_tensor("h2_d", [128, HC * S], BF16, kind="ExternalOutput").ap()

    ctx = contextlib.ExitStack()
    with ctx:
        semnames = ["sW", "sWf", "sSP", "sSPa", "sPL", "sY",
                    "sPE", "sPEn", "sPEs", "sPEp", "sPEr", "sPEd", "sPEg", "sPEu",
                    "sACT", "sACTn", "sACTs", "sACTr", "sACTc", "sACTd",
                    "sDVE", "sDVEb", "sDVEo", "sDVEa", "sDVEr",
                    "sWq0", "sWq1", "sSt0", "sSt1", "sAo0", "sAo1",
                    "sXl0", "sXl1", "sXr0", "sXr1",
                    "sWg0", "sWg1", "sWu0", "sWu1", "sWd0", "sWd1",
                    "sAc0", "sAc1", "sXs0", "sXs1"]
        sm = {n: Sem(ctx.enter_context(nc.semaphore(n)), n) for n in semnames}
        sW, sWf, sSP, sSPa, sPL, sY = (sm[n] for n in ("sW", "sWf", "sSP", "sSPa", "sPL", "sY"))
        sPE, sPEn, sPEs, sPEp, sPEr, sPEd, sPEg, sPEu = (
            sm[n] for n in ("sPE", "sPEn", "sPEs", "sPEp", "sPEr", "sPEd", "sPEg", "sPEu"))
        sACT, sACTn, sACTs, sACTr, sACTc, sACTd = (
            sm[n] for n in ("sACT", "sACTn", "sACTs", "sACTr", "sACTc", "sACTd"))
        sDVE, sDVEb, sDVEo, sDVEa, sDVEr = (sm[n] for n in ("sDVE", "sDVEb", "sDVEo", "sDVEa", "sDVEr"))
        sWq = [sm["sWq0"], sm["sWq1"]]
        sSt = [sm["sSt0"], sm["sSt1"]]
        sAo = [sm["sAo0"], sm["sAo1"]]
        sXl = [sm["sXl0"], sm["sXl1"]]
        sXr = [sm["sXr0"], sm["sXr1"]]
        sWg = [sm["sWg0"], sm["sWg1"]]
        sWu = [sm["sWu0"], sm["sWu1"]]
        sWd = [sm["sWd0"], sm["sWd1"]]
        sAc = [sm["sAc0"], sm["sAc1"]]
        sXs = [sm["sXs0"], sm["sXs1"]]

        onesM = ctx.enter_context(nc.sbuf_tensor("onesM", [128, 128], BF16))
        cst = ctx.enter_context(nc.sbuf_tensor("cst", [128, 48], F32))
        onescol = onesM[:, 0:1]

        slopes_f = [float(s) for s in _alibi_slopes(NH)]
        cvals = [EPS, SCALE, INV_H, -0.5, -1.0] + slopes_f
        for i, v in enumerate(cvals):
            nc.const_aps.register(F32, v, cst[:, i:i + 1])

        p = Plan()
        for i, v in enumerate(cvals):
            I(p, "gpsimd", lambda g, i=i, v=v: g.memset(cst[:, i:i + 1], v))
        I(p, "sync", lambda s: s.dma_start(onesM[:], onesM_c), inc=(sSP, sSP.take(16)))
        p.add("sync", lambda eng, v=sSP.n, h=sSP.h: eng.wait_ge(h, v))
        p.emit(nc)

        def ln_rows(p, ps1, ps2, rA, rB, rC, waits):
            I(p, "scalar", lambda a: a.copy(rA, ps1[:]), waits=waits)
            I(p, "scalar", lambda a: a.copy(rB, ps2[:]))
            I(p, "scalar", lambda a: a.mul(rA, rA, INV_H))
            I(p, "scalar", lambda a: a.mul(rB, rB, INV_H), inc=(sACT, sACT.take()))
            se = sACT.n
            I(p, "vector", lambda v: v.tensor_mul(rC, rA, rA), waits=[(sACT, se)])
            I(p, "vector", lambda v: v.tensor_sub(rB, rB, rC), inc=(sDVE, sDVE.take()))
            vd = sDVE.n
            I(p, "scalar", lambda a: a.activation(rB, rB, AF.Ln, bias=EPS),
              waits=[(sDVE, vd)])
            I(p, "scalar", lambda a: a.activation(rB, rB, AF.Exp, scale=-0.5),
              inc=(sACT, sACT.take()))
            return sACT.n

        if n_phases >= 1:
            big_ctx = contextlib.ExitStack()
            with big_ctx:
                big = big_ctx.enter_context(nc.sbuf_tensor("big1", [128, HC * S], BF16))
                with contextlib.ExitStack() as _es:
                    xblk = _es.enter_context(nc.sbuf_tensor("xblk", [128, HC * 512], BF16))
                    sqf = _es.enter_context(nc.sbuf_tensor("sqf", [128, HC * 512], BF16))
                    rows1 = _es.enter_context(nc.sbuf_tensor("rows1", [1, 512 * 3], F32))
                    rowM = _es.enter_context(nc.sbuf_tensor("rowM", [128, 512], BF16))
                    rowR = _es.enter_context(nc.sbuf_tensor("rowR", [128, 512], BF16))
                    Mt = _es.enter_context(nc.sbuf_tensor("Mt", [128, 512], BF16))
                    Rt = _es.enter_context(nc.sbuf_tensor("Rt", [128, 512], BF16))
                    t1b = _es.enter_context(nc.sbuf_tensor("t1b", [128, 512], BF16))
                    ps_st1 = _es.enter_context(nc.psum_tensor("ps_st1", [1, 512], F32))
                    ps_st2 = _es.enter_context(nc.psum_tensor("ps_st2", [1, 512], F32))
                    ps_bM = _es.enter_context(nc.psum_tensor("ps_bM", [128, 512], F32))
                    ps_bR = _es.enter_context(nc.psum_tensor("ps_bR", [128, 512], F32))
                    rA = rows1[:, 0:512]
                    rB = rows1[:, 512:1024]
                    rC = rows1[:, 1024:1536]
                    p = Plan()
                    I(p, "gpsimd", lambda g: g.memset(rowM[:], 0.0))
                    I(p, "gpsimd", lambda g: g.memset(rowR[:], 0.0), inc=(sPL, sPL.take()))
                    pad_ready = sPL.n
                    norm_end = 0
                    rows_evac = 0
                    bcast_done = 0
                    bcast_evac = 0
                    for tb in range(TB):
                        t0 = tb * 512
                        lw = [(sDVE, norm_end)] if tb else []
                        for g4 in range(4):
                            src = x_t[g4 * 8:(g4 + 1) * 8, :, t0:t0 + 512].rearrange(
                                "c p t -> p c t")
                            I(p, "gpsimd",
                              lambda g, d=sp3(xblk[:, g4 * 4096:(g4 + 1) * 4096], 512), s=src:
                              g.dma_start(d, s),
                              waits=lw if g4 == 0 else (), inc=(sW, sW.take(16)))
                        ld = sW.n
                        for c in range(HC):
                            cw = [(sW, ld)] if c == 0 else []
                            xs = xblk[:, c * 512:(c + 1) * 512]
                            sq = sqf[:, c * 512:(c + 1) * 512]
                            inc = (sDVE, sDVE.take()) if c == HC - 1 else None
                            I(p, "vector", lambda v, a=xs, b=sq: v.tensor_mul(b, a, a),
                              waits=cw, inc=inc)
                        sq_done = sDVE.n
                        for c in range(HC):
                            cw = []
                            if c == 0:
                                cw = [(sDVE, sq_done)]
                                if rows_evac:
                                    cw.append((sACT, rows_evac))
                            xs = xblk[:, c * 512:(c + 1) * 512]
                            sq = sqf[:, c * 512:(c + 1) * 512]
                            I(p, "tensor", lambda t, r=xs, c0=(c == 0), c1=(c == HC - 1):
                              t.matmul(ps_st1[:], onescol, r, start=c0, stop=c1), waits=cw)
                            inc = (sPE, sPE.take()) if c == HC - 1 else None
                            I(p, "tensor", lambda t, r=sq, c0=(c == 0), c1=(c == HC - 1):
                              t.matmul(ps_st2[:], onescol, r, start=c0, stop=c1), inc=inc)
                        stats_done = sPE.n
                        rows_evac = ln_rows(p, ps_st1, ps_st2, rA, rB, rC,
                                            [(sPE, stats_done)])
                        rw = [(sACT, rows_evac)] + \
                             ([(sPE, bcast_done)] if tb else [(sPL, pad_ready)])
                        I(p, "scalar", lambda a: a.copy(rowM[0:1, :], rA), waits=rw)
                        I(p, "scalar", lambda a: a.copy(rowR[0:1, :], rB),
                          inc=(sACT, sACT.take()))
                        rows_b = sACT.n
                        bw = [(sACT, rows_b)] + ([(sACT, bcast_evac)] if tb else [])
                        I(p, "tensor", lambda t: t.matmul(ps_bM[:], onesM[:], rowM[:],
                                                          start=True, stop=True), waits=bw)
                        I(p, "tensor", lambda t: t.matmul(ps_bR[:], onesM[:], rowR[:],
                                                          start=True, stop=True),
                          inc=(sPE, sPE.take()))
                        bcast_done = sPE.n
                        ew = [(sPE, bcast_done)] + ([(sDVE, norm_end)] if tb else [])
                        I(p, "scalar", lambda a: a.copy(Mt[:], ps_bM[:]), waits=ew)
                        I(p, "scalar", lambda a: a.copy(Rt[:], ps_bR[:]),
                          inc=(sACT, sACT.take()))
                        bcast_evac = sACT.n
                        for c in range(HC):
                            cw = [(sACT, bcast_evac)] if c == 0 else []
                            xs = xblk[:, c * 512:(c + 1) * 512]
                            dst = big[:, c * S + t0:c * S + t0 + 512]
                            I(p, "vector", lambda v, a=xs: v.tensor_sub(t1b[:], a, Mt[:]),
                              waits=cw)
                            inc = (sDVE, sDVE.take()) if c == HC - 1 else None
                            I(p, "vector", lambda v, d=dst: v.tensor_mul(d, t1b[:], Rt[:]),
                              inc=inc)
                        norm_end = sDVE.n
                    p.emit(nc)

                if n_phases >= 2:
                    with contextlib.ExitStack() as _es:
                        wqb0 = _es.enter_context(nc.sbuf_tensor("wqb0", [128, HC * 128], BF16))
                        wqb1 = _es.enter_context(nc.sbuf_tensor("wqb1", [128, HC * 128], BF16))
                        stg0 = _es.enter_context(nc.sbuf_tensor("stg0", [128, 2048], BF16))
                        stg1 = _es.enter_context(nc.sbuf_tensor("stg1", [128, 2048], BF16))
                        ps_q0 = _es.enter_context(nc.psum_tensor("ps_q0", [128, 512], F32))
                        ps_q1 = _es.enter_context(nc.psum_tensor("ps_q1", [128, 512], F32))
                        ps_q2 = _es.enter_context(nc.psum_tensor("ps_q2", [128, 512], F32))
                        ps_q3 = _es.enter_context(nc.psum_tensor("ps_q3", [128, 512], F32))
                        wqb = [wqb0, wqb1]
                        stg = [stg0, stg1]
                        psq = [ps_q0, ps_q1, ps_q2, ps_q3]
                        p = Plan()
                        wfree = [0, 0]
                        pe_done = {}
                        evac = {}
                        store = {}
                        for oc in range(40):
                            par = oc % 2
                            lw = [(sACT, evac[(oc - 2, TB - 1)])] if oc >= 2 else []
                            src = wqk_t[oc:oc + 1].rearrange("o p c i -> (o p) (c i)")
                            I(p, "gpsimd", lambda g, d=wqb[par][:], s=src: g.dma_start(d, s),
                              waits=lw, inc=(sWq[par], sWq[par].take(16)))
                            wld = (sWq[par], sWq[par].n)
                            for kc in range(HC):
                                for tb in range(TB):
                                    cw = []
                                    if kc == 0:
                                        cw = [wld]
                                        if oc >= 1:
                                            cw.append((sACT, evac[(oc - 1, tb)]))
                                    lhs = wqb[par][:, kc * 128:(kc + 1) * 128]
                                    rhs = big[:, kc * S + tb * 512:kc * S + tb * 512 + 512]
                                    inc = None
                                    if kc == HC - 1:
                                        inc = (sPE, sPE.take())
                                    I(p, "tensor", lambda t, o=psq[tb][:], l=lhs, r=rhs,
                                      c0=(kc == 0), c1=(kc == HC - 1):
                                      t.matmul(o, l, r, start=c0, stop=c1),
                                      waits=cw, inc=inc)
                                    if kc == HC - 1:
                                        pe_done[(oc, tb)] = sPE.n
                            for tb in range(TB):
                                cw = [(sPE, pe_done[(oc, tb)])]
                                if tb == 0 and oc >= 2:
                                    cw.append(store[oc - 2])
                                dst = stg[par][:, tb * 512:(tb + 1) * 512]
                                I(p, "scalar", lambda a, d=dst, s=psq[tb][:]: a.copy(d, s),
                                  waits=cw, inc=(sACT, sACT.take()))
                                evac[(oc, tb)] = sACT.n
                            tgt = (q_s[oc:oc + 1] if oc < NH else
                                   k_s[oc - NH:oc - NH + 1]).rearrange("o p t -> (o p) t")
                            I(p, "sync", lambda s, d=tgt, s2=stg[par][:]: s.dma_start(d, s2),
                              waits=[(sACT, evac[(oc, TB - 1)])],
                              inc=(sSt[par], sSt[par].take(16)))
                            store[oc] = (sSt[par], sSt[par].n)
                        p.emit(nc)

                    with contextlib.ExitStack() as _es:
                        wvb = _es.enter_context(nc.sbuf_tensor("wvb", [128, HC * 1024], BF16))
                        stv0 = _es.enter_context(nc.sbuf_tensor("stv0", [128, 1024], BF16))
                        stv1 = _es.enter_context(nc.sbuf_tensor("stv1", [128, 1024], BF16))
                        ps_vA0 = _es.enter_context(nc.psum_tensor("ps_vA0", [128, 512], F32))
                        ps_vB0 = _es.enter_context(nc.psum_tensor("ps_vB0", [128, 512], F32))
                        ps_vA1 = _es.enter_context(nc.psum_tensor("ps_vA1", [128, 512], F32))
                        ps_vB1 = _es.enter_context(nc.psum_tensor("ps_vB1", [128, 512], F32))
                        stv = [stv0, stv1]
                        psA = [ps_vA0, ps_vA1]
                        psB = [ps_vB0, ps_vB1]
                        p = Plan()
                        I(p, "gpsimd",
                          lambda g: g.dma_start(wvb[:], wv_t.rearrange("p c i -> p (c i)")),
                          inc=(sW, sW.take(16)))
                        wld = sW.n
                        vevac = {}
                        vstore = {}
                        for tt in range(16):
                            par = tt % 2
                            for kc in range(HC):
                                cw = []
                                if kc == 0:
                                    cw = [(sW, wld)]
                                    if tt >= 2:
                                        cw.append((sACT, vevac[tt - 2]))
                                lhs = big[:, kc * S + tt * 128:kc * S + tt * 128 + 128]
                                I(p, "tensor", lambda t, o=psA[par][:], l=lhs,
                                  r=wvb[:, kc * 1024:kc * 1024 + 512],
                                  c0=(kc == 0), c1=(kc == HC - 1):
                                  t.matmul(o, l, r, start=c0, stop=c1), waits=cw)
                                inc = (sPE, sPE.take()) if kc == HC - 1 else None
                                I(p, "tensor", lambda t, o=psB[par][:], l=lhs,
                                  r=wvb[:, kc * 1024 + 512:kc * 1024 + 1024],
                                  c0=(kc == 0), c1=(kc == HC - 1):
                                  t.matmul(o, l, r, start=c0, stop=c1), inc=inc)
                            vdone = sPE.n
                            cw = [(sPE, vdone)]
                            if tt >= 2:
                                cw.append(vstore[tt - 2])
                            I(p, "scalar",
                              lambda a, d=stv[par][:, 0:512], s=psA[par][:]: a.copy(d, s),
                              waits=cw)
                            I(p, "scalar",
                              lambda a, d=stv[par][:, 512:1024], s=psB[par][:]: a.copy(d, s),
                              inc=(sACT, sACT.take()))
                            vevac[tt] = sACT.n
                            dstv = v_s[tt:tt + 1].rearrange("o p t -> (o p) t")
                            I(p, "sync", lambda s, d=dstv, s2=stv[par][:]: s.dma_start(d, s2),
                              waits=[(sACT, vevac[tt])],
                              inc=(sSt[par], sSt[par].take(16)))
                            vstore[tt] = (sSt[par], sSt[par].n)
                        if debug:
                            I(p, "sync", lambda s: s.dma_start(h1_d, big[:]),
                              inc=(sSP, sSP.take(16)))
                            p.add("sync", lambda eng, v=sSP.n, h=sSP.h: eng.wait_ge(h, v))
                        p.emit(nc)

        if n_phases >= 3:
            KTOT = S // 128
            with contextlib.ExitStack() as _es:
                Ksb = _es.enter_context(nc.sbuf_tensor("Ksb", [128, NKV * S], BF16))
                Vsb = _es.enter_context(nc.sbuf_tensor("Vsb", [128, 16 * 1024], BF16))
                Qb = _es.enter_context(nc.sbuf_tensor("Qb", [128, NH * 512], BF16))
                Db = _es.enter_context(nc.sbuf_tensor("Db", [128, KTOT * 512], BF16))
                sqq = _es.enter_context(nc.sbuf_tensor("sqq", [128, 2 * 512], BF16))
                sqk = _es.enter_context(nc.sbuf_tensor("sqk", [128, S], BF16))
                nkr = _es.enter_context(nc.sbuf_tensor("nkr", [1, S], F32))
                nqr = _es.enter_context(nc.sbuf_tensor("nqr", [1, 2 * 512], F32))
                qmx = _es.enter_context(nc.sbuf_tensor("qmx", [1, 512], F32))
                mk2s = _es.enter_context(nc.sbuf_tensor("mk2s", [1, 8], F32))
                scr = _es.enter_context(nc.sbuf_tensor("scr", [1, 8], F32))
                colC = _es.enter_context(nc.sbuf_tensor("colC", [128, 1], BF16))
                bcolT = _es.enter_context(nc.sbuf_tensor("bcolT", [128, 2], F32))
                rowC0 = _es.enter_context(nc.sbuf_tensor("rowC0", [128, 512], BF16))
                rowC1 = _es.enter_context(nc.sbuf_tensor("rowC1", [128, 512], BF16))
                dnegs = _es.enter_context(nc.sbuf_tensor("dnegs", [128, NH * 128], BF16))
                pt0 = _es.enter_context(nc.sbuf_tensor("pt0", [128, 512], BF16))
                pt1 = _es.enter_context(nc.sbuf_tensor("pt1", [128, 512], BF16))
                pt2 = _es.enter_context(nc.sbuf_tensor("pt2", [128, 512], BF16))
                recb0 = _es.enter_context(nc.sbuf_tensor("recb0", [128, 512], F32))
                recb1 = _es.enter_context(nc.sbuf_tensor("recb1", [128, 512], F32))
                aot0 = _es.enter_context(nc.sbuf_tensor("aot0", [128, 512], BF16))
                aot1 = _es.enter_context(nc.sbuf_tensor("aot1", [128, 512], BF16))
                ps_s0 = _es.enter_context(nc.psum_tensor("ps_s0", [128, 512], F32))
                ps_s1 = _es.enter_context(nc.psum_tensor("ps_s1", [128, 512], F32))
                ps_s2 = _es.enter_context(nc.psum_tensor("ps_s2", [128, 512], F32))
                ps_o0 = _es.enter_context(nc.psum_tensor("ps_o0", [128, 512], F32))
                ps_o1 = _es.enter_context(nc.psum_tensor("ps_o1", [128, 512], F32))
                ps_sumB = _es.enter_context(nc.psum_tensor("ps_sumB", [128, 512], F32))
                ps_rb = _es.enter_context(nc.psum_tensor("ps_rb", [128, 512], F32))
                ps_pre = _es.enter_context(nc.psum_tensor("ps_pre", [128, 512], F32))
                pss = [ps_s0, ps_s1, ps_s2]
                pso = [ps_o0, ps_o1]
                pt = [pt0, pt1, pt2]
                rowC = [rowC0, rowC1]
                recb = [recb0, recb1]
                aot = [aot0, aot1]
                p = Plan()
                I(p, "gpsimd", lambda g: g.memset(colC[:], 0.0))
                I(p, "gpsimd", lambda g: g.memset(rowC0[:], 0.0))
                I(p, "gpsimd", lambda g: g.memset(rowC1[:], 0.0), inc=(sPL, sPL.take()))
                pad_ready = sPL.n
                I(p, "sync", lambda s: s.dma_start(dnegs[:], dneg_c),
                  inc=(sSP, sSP.take(16)))
                I(p, "sync", lambda s: s.dma_start(sp3(Ksb[:], S),
                                                   k_s.rearrange("h p t -> p h t")),
                  inc=(sSP, sSP.take(16)))
                I(p, "sync", lambda s: s.dma_start(sp3(Vsb[:], 1024),
                                                   v_s.rearrange("h p t -> p h t")),
                  inc=(sSP, sSP.take(16)))
                kv_ld = sSP.n

                row_evac = [0, 0, 0]
                rmax_done = 0
                for kv in range(NKV):
                    cw = [(sSP, kv_ld)] if kv == 0 else [(sPEn, sPEn.n)]
                    I(p, "vector", lambda v, a=Ksb[:, kv * S:(kv + 1) * S]:
                      v.tensor_mul(sqk[:], a, a), waits=cw, inc=(sDVE, sDVE.take()))
                    sqk_done = sDVE.n
                    ev = 0
                    for seg in range(4):
                        r = (kv * 4 + seg) % 3
                        cw = [(sDVE, sqk_done)]
                        if row_evac[r]:
                            cw.append((sACTn, row_evac[r]))
                        I(p, "tensor", lambda t, r=r,
                          s2=sqk[:, seg * 512:(seg + 1) * 512]:
                          t.matmul(ps_pre[r * 32:r * 32 + 1, :], onescol, s2,
                                   start=True, stop=True),
                          waits=cw, inc=(sPEn, sPEn.take()))
                        cw = [(sPEn, sPEn.n)]
                        if seg == 0 and rmax_done:
                            cw.append((sDVE, rmax_done))
                        I(p, "scalar", lambda a, r=r,
                          d=nkr[:, seg * 512:(seg + 1) * 512]:
                          a.copy(d, ps_pre[r * 32:r * 32 + 1, :]),
                          waits=cw, inc=(sACTn, sACTn.take()))
                        row_evac[r] = sACTn.n
                        ev = sACTn.n
                    I(p, "vector", lambda v, d=mk2s[:, kv:kv + 1]:
                      v.reduce_max(d, nkr[:], axis=AX.X),
                      waits=[(sACTn, ev)], inc=(sDVE, sDVE.take()))
                    rmax_done = sDVE.n
                I(p, "vector", lambda v: v.reduce_max(scr[:, 0:1], mk2s[:], axis=AX.X))

                bank_last_exp = [0, 0, 0]
                pt_last_pv = [0, 0, 0]
                pv_all = {}
                rowcopy_done = {}
                recip_done = {}
                rb_mm = {}
                rb_evac = {}
                ao_done = {}
                ao_store = {}
                nq_mm = {}
                last_scores = 0
                last_stt = 0
                bc_mm_prev = 0

                def em_tail(g):
                    if g < 0:
                        return
                    j = g % 2
                    cw = [(sPEp, pv_all[g]), (sPL, pad_ready)]
                    if g >= 2 and rb_mm.get(g - 2):
                        cw.append((sPEr, rb_mm[g - 2]))
                    I(p, "scalar", lambda a, j=j, rb=(g % 2) * 32:
                      a.copy(rowC[j][0:1, :], ps_sumB[rb:rb + 1, :]),
                      waits=cw, inc=(sACTc, sACTc.take()))
                    rowcopy_done[g] = sACTc.n
                    cw = [(sACTc, rowcopy_done[g])]
                    if g >= 1 and recip_done.get(g - 1):
                        cw.append((sDVEr, recip_done[g - 1]))
                    I(p, "tensor", lambda t, j=j:
                      t.matmul(ps_rb[:], onesM[:], rowC[j][:], start=True, stop=True),
                      waits=cw, inc=(sPEr, sPEr.take()))
                    rb_mm[g] = sPEr.n
                    cw = [(sPEr, rb_mm[g])]
                    if g >= 2 and ao_done.get(g - 2):
                        cw.append((sDVEo, ao_done[g - 2]))
                    def _recip(v, j=j):
                        with nc.allow_low_precision(
                                reason="softmax normalizer reciprocal in f32"):
                            return v.reciprocal(recb[j][:], ps_rb[:])
                    I(p, "vector", _recip, waits=cw, inc=(sDVEr, sDVEr.take()))
                    recip_done[g] = sDVEr.n
                    cw = [(sDVEr, recip_done[g])]
                    if g >= 2 and ao_store.get(g - 2):
                        cw.append(ao_store[g - 2])
                    I(p, "vector", lambda v, j=j:
                      v.tensor_mul(aot[j][:], pso[j][:], recb[j][:]),
                      waits=cw, inc=(sDVEo, sDVEo.take()))
                    ao_done[g] = sDVEo.n
                    qq, hh = divmod(g, NH)
                    dsta = ao_s[hh:hh + 1, :, qq * 512:(qq + 1) * 512].rearrange(
                        "o p t -> (o p) t")
                    I(p, "sync", lambda s, j=j, d=dsta: s.dma_start(d, aot[j][:]),
                      waits=[(sDVEo, ao_done[g])], inc=(sAo[j], sAo[j].take(16)))
                    ao_store[g] = (sAo[j], sAo[j].n)

                for qb in range(TB):
                    KT = 4 * (qb + 1)
                    cw = [(sPEs, last_scores)] if qb else []
                    for g4 in range(4):
                        src = q_s[g4 * 8:(g4 + 1) * 8, :,
                                  qb * 512:(qb + 1) * 512].rearrange("c p t -> p c t")
                        I(p, "sync",
                          lambda s, d=sp3(Qb[:, g4 * 4096:(g4 + 1) * 4096], 512), s2=src:
                          s.dma_start(d, s2),
                          waits=cw if g4 == 0 else (), inc=(sSP, sSP.take(16)))
                    srcD = dT_c[0:KT, :, qb * 512:(qb + 1) * 512].rearrange(
                        "c p t -> p c t")
                    I(p, "sync", lambda s, d=sp3(Db[:, 0:KT * 512], 512), s2=srcD:
                      s.dma_start(d, s2),
                      inc=(sSP, sSP.take(16)))
                    qld = sSP.n
                    max_acc = {}
                    for h in range(NH):
                        j = h % 2
                        r = h % 3
                        cw = []
                        if h == 0:
                            cw.append((sSP, qld))
                        if nq_mm.get(h - 2):
                            cw.append((sPEn, nq_mm[h - 2]))
                        I(p, "vector", lambda v, j=j, a=Qb[:, h * 512:(h + 1) * 512]:
                          v.tensor_mul(sqq[:, j * 512:(j + 1) * 512], a, a),
                          waits=cw, inc=(sDVE, sDVE.take()))
                        sq_done = sDVE.n
                        cw = [(sDVE, sq_done)]
                        if row_evac[r]:
                            cw.append((sACTn, row_evac[r]))
                        I(p, "tensor", lambda t, j=j, r=r:
                          t.matmul(ps_pre[r * 32:r * 32 + 1, :], onescol,
                                   sqq[:, j * 512:(j + 1) * 512],
                                   start=True, stop=True),
                          waits=cw, inc=(sPEn, sPEn.take()))
                        nq_mm[h] = sPEn.n
                        cw = [(sPEn, nq_mm[h])]
                        if max_acc.get(h - 2):
                            cw.append((sDVE, max_acc[h - 2]))
                        I(p, "scalar", lambda a, j=j, r=r:
                          a.copy(nqr[:, j * 512:(j + 1) * 512],
                                 ps_pre[r * 32:r * 32 + 1, :]),
                          waits=cw, inc=(sACTn, sACTn.take()))
                        row_evac[r] = sACTn.n
                        cw = [(sACTn, row_evac[r])]
                        if h == 0:
                            I(p, "vector", lambda v: v.tensor_copy(qmx[:], nqr[:, 0:512]),
                              waits=cw, inc=(sDVE, sDVE.take()))
                        else:
                            I(p, "vector", lambda v, j=j:
                              v.tensor_max(qmx[:], qmx[:],
                                           nqr[:, j * 512:(j + 1) * 512]),
                              waits=cw, inc=(sDVE, sDVE.take()))
                        max_acc[h] = sDVE.n
                    I(p, "vector", lambda v: v.reduce_max(scr[:, 1:2], qmx[:], axis=AX.X))
                    I(p, "vector", lambda v: v.tensor_add(scr[:, 2:3], scr[:, 0:1],
                                                          scr[:, 1:2]))
                    cw = [(sPEn, bc_mm_prev)] if bc_mm_prev else []
                    I(p, "vector", lambda v:
                      v.tensor_scalar(colC[0:1, 0:1], scr[:, 2:3], -0.5 * SCALE, None,
                                      op0=ALU.mult),
                      waits=cw, inc=(sDVE, sDVE.take()))
                    cC_done = sDVE.n
                    cw = [(sDVE, cC_done)] + [(sACTn, row_evac[r]) for r in range(3)]
                    I(p, "tensor", lambda t: t.matmul(ps_pre[:, 0:1], onesM[:], colC[:],
                                                      start=True, stop=True),
                      waits=cw, inc=(sPEn, sPEn.take()))
                    bc_mm_prev = sPEn.n
                    I(p, "scalar", lambda a, c=qb % 2:
                      a.copy(bcolT[:, c:c + 1], ps_pre[:, 0:1]),
                      waits=[(sPEn, bc_mm_prev)], inc=(sACTn, sACTn.take()))

                    for h in range(NH):
                        gidx = qb * NH + h
                        kv = h // G
                        qsl = Qb[:, h * 512:(h + 1) * 512]
                        sumr0 = (h % 2) * 32
                        sc_done = {}
                        exp_done = {}

                        def em_scores(kt, qsl=qsl, kv=kv, h=h):
                            b = kt % 3
                            cw = []
                            if bank_last_exp[b]:
                                cw.append((sACTs, bank_last_exp[b]))
                            lhsK = Ksb[:, kv * S + kt * 128:kv * S + kt * 128 + 128]
                            I(p, "tensor", lambda t, o=pss[b][:], l=lhsK, r=qsl:
                              t.matmul(o, l, r, start=True, stop=False), waits=cw)
                            I(p, "tensor", lambda t, o=pss[b][:],
                              l=dnegs[:, h * 128:(h + 1) * 128],
                              r=Db[:, kt * 512:(kt + 1) * 512]:
                              t.matmul(o, l, r, start=False, stop=True),
                              inc=(sPEs, sPEs.take()))
                            sc_done[kt] = sPEs.n

                        def em_exp(kt, c=qb % 2):
                            b = kt % 3
                            cw = [(sPEs, sc_done[kt])]
                            if pt_last_pv[b]:
                                cw.append((sPEp, pt_last_pv[b]))
                            I(p, "scalar", lambda a, b=b, c=c:
                              a.activation(pt[b][:], pss[b][:], AF.Exp, scale=SCALE,
                                           bias=bcolT[:, c:c + 1]),
                              waits=cw, inc=(sACTs, sACTs.take()))
                            bank_last_exp[b] = sACTs.n
                            exp_done[kt] = sACTs.n

                        def em_pv(kt, KT=KT, kv=kv, po=(h % 2), gidx=gidx,
                                  sumr0=sumr0):
                            b = kt % 3
                            cw = [(sACTs, exp_done[kt])]
                            if kt == 0:
                                if ao_done.get(gidx - 2):
                                    cw.append((sDVEo, ao_done[gidx - 2]))
                                if rowcopy_done.get(gidx - 2):
                                    cw.append((sACTc, rowcopy_done[gidx - 2]))
                            lhsV = Vsb[:, kt * 1024 + kv * 128:
                                       kt * 1024 + kv * 128 + 128]
                            I(p, "tensor", lambda t, o=pso[po][:], l=lhsV, r=pt[b][:],
                              c0=(kt == 0), c1=(kt == KT - 1):
                              t.matmul(o, l, r, start=c0, stop=c1), waits=cw)
                            I(p, "tensor", lambda t, r=pt[b][:],
                              c0=(kt == 0), c1=(kt == KT - 1):
                              t.matmul(ps_sumB[sumr0:sumr0 + 1, :], onescol, r,
                                       start=c0, stop=c1),
                              inc=(sPEp, sPEp.take()))
                            pt_last_pv[b] = sPEp.n
                            if kt == KT - 1:
                                pv_all[gidx] = sPEp.n

                        em_scores(0)
                        em_exp(0)
                        em_scores(1)
                        em_exp(1)
                        for kt in range(2, KT):
                            em_scores(kt)
                            em_exp(kt)
                            em_pv(kt - 2)
                            if kt == 2:
                                em_tail(gidx - 1)
                        em_pv(KT - 2)
                        em_pv(KT - 1)
                        last_scores = sPEs.n
                em_tail(TB * NH - 1)
                p.add("sync", lambda eng, v=sAo[0].n, h=sAo[0].h: eng.wait_ge(h, v))
                p.add("sync", lambda eng, v=sAo[1].n, h=sAo[1].h: eng.wait_ge(h, v))
                p.emit(nc)

        if n_phases >= 4:
            big_ctx = contextlib.ExitStack()
            with big_ctx:
                big = big_ctx.enter_context(nc.sbuf_tensor("big2", [128, HC * S], BF16))
                with contextlib.ExitStack() as _es:
                    wob0 = _es.enter_context(nc.sbuf_tensor("wob0", [128, HC * 128], BF16))
                    wob1 = _es.enter_context(nc.sbuf_tensor("wob1", [128, HC * 128], BF16))
                    stg0b = _es.enter_context(nc.sbuf_tensor("stg0b", [128, 2048], BF16))
                    stg1b = _es.enter_context(nc.sbuf_tensor("stg1b", [128, 2048], BF16))
                    ps_p0 = _es.enter_context(nc.psum_tensor("ps_p0", [128, 512], F32))
                    ps_p1 = _es.enter_context(nc.psum_tensor("ps_p1", [128, 512], F32))
                    ps_p2 = _es.enter_context(nc.psum_tensor("ps_p2", [128, 512], F32))
                    ps_p3 = _es.enter_context(nc.psum_tensor("ps_p3", [128, 512], F32))
                    wob = [wob0, wob1]
                    stg = [stg0b, stg1b]
                    psq = [ps_p0, ps_p1, ps_p2, ps_p3]
                    p = Plan()
                    for g4 in range(4):
                        src = ao_s[g4 * 8:(g4 + 1) * 8, :, :].rearrange("c p t -> p c t")
                        I(p, "sync", lambda s, d=sp3(big[:, g4 * 8 * S:(g4 + 1) * 8 * S], S), s2=src:
                          s.dma_start(d, s2), inc=(sSP, sSP.take(16)))
                    ao_ld = sSP.n
                    wfree = [0, 0]
                    pe_done = {}
                    evac = {}
                    store = {}
                    for oc in range(HC):
                        par = oc % 2
                        lw = [(sACT, evac[(oc - 2, TB - 1)])] if oc >= 2 else []
                        src = wo_t[oc:oc + 1].rearrange("o p c i -> (o p) (c i)")
                        I(p, "gpsimd", lambda g, d=wob[par][:], s=src: g.dma_start(d, s),
                          waits=lw, inc=(sWq[par], sWq[par].take(16)))
                        wld = (sWq[par], sWq[par].n)
                        for kc in range(HC):
                            for tb in range(TB):
                                cw = []
                                if kc == 0:
                                    cw = [wld, (sSP, ao_ld)]
                                    if oc >= 1:
                                        cw.append((sACT, evac[(oc - 1, tb)]))
                                lhs = wob[par][:, kc * 128:(kc + 1) * 128]
                                rhs = big[:, kc * S + tb * 512:kc * S + tb * 512 + 512]
                                inc = None
                                if kc == HC - 1:
                                    inc = (sPE, sPE.take())
                                I(p, "tensor", lambda t, o=psq[tb][:], l=lhs, r=rhs,
                                  c0=(kc == 0), c1=(kc == HC - 1):
                                  t.matmul(o, l, r, start=c0, stop=c1),
                                  waits=cw, inc=inc)
                                if kc == HC - 1:
                                    pe_done[(oc, tb)] = sPE.n
                        for tb in range(TB):
                            cw = [(sPE, pe_done[(oc, tb)])]
                            if tb == 0 and oc >= 2:
                                cw.append(store[oc - 2])
                            dst = stg[par][:, tb * 512:(tb + 1) * 512]
                            I(p, "scalar", lambda a, d=dst, s=psq[tb][:]: a.copy(d, s),
                              waits=cw, inc=(sACT, sACT.take()))
                            evac[(oc, tb)] = sACT.n
                        tgt = pj_s[oc:oc + 1].rearrange("o p t -> (o p) t")
                        I(p, "sync", lambda s, d=tgt, s2=stg[par][:]: s.dma_start(d, s2),
                          waits=[(sACT, evac[(oc, TB - 1)])],
                          inc=(sSt[par], sSt[par].take(16)))
                        store[oc] = (sSt[par], sSt[par].n)
                    p.emit(nc)

                if n_phases >= 5:
                    with contextlib.ExitStack() as _es:
                        pblock = _es.enter_context(nc.sbuf_tensor("pblock", [128, HC * 512], BF16))
                        xst0 = _es.enter_context(nc.sbuf_tensor("xst0", [128, 512], F32))
                        xst1 = _es.enter_context(nc.sbuf_tensor("xst1", [128, 512], F32))
                        x1c0 = _es.enter_context(nc.sbuf_tensor("x1c0", [128, 512], F32))
                        x1c1 = _es.enter_context(nc.sbuf_tensor("x1c1", [128, 512], F32))
                        sq3a = _es.enter_context(nc.sbuf_tensor("sq3a", [128, 512], BF16))
                        sq3b = _es.enter_context(nc.sbuf_tensor("sq3b", [128, 512], BF16))
                        t1c = _es.enter_context(nc.sbuf_tensor("t1c", [128, 512], BF16))
                        t2c = _es.enter_context(nc.sbuf_tensor("t2c", [128, 512], F32))
                        rows3 = _es.enter_context(nc.sbuf_tensor("rows3", [1, 512 * 3], F32))
                        rowM3 = _es.enter_context(nc.sbuf_tensor("rowM3", [128, 512], BF16))
                        rowR3 = _es.enter_context(nc.sbuf_tensor("rowR3", [128, 512], BF16))
                        Ma = _es.enter_context(nc.sbuf_tensor("Ma", [128, 512], BF16))
                        Ra = _es.enter_context(nc.sbuf_tensor("Ra", [128, 512], BF16))
                        M2 = _es.enter_context(nc.sbuf_tensor("M2", [128, 512], BF16))
                        R2 = _es.enter_context(nc.sbuf_tensor("R2", [128, 512], BF16))
                        ps_t1 = _es.enter_context(nc.psum_tensor("ps_t1", [1, 512], F32))
                        ps_t2 = _es.enter_context(nc.psum_tensor("ps_t2", [1, 512], F32))
                        ps_bM2 = _es.enter_context(nc.psum_tensor("ps_bM2", [128, 512], F32))
                        ps_bR2 = _es.enter_context(nc.psum_tensor("ps_bR2", [128, 512], F32))
                        sq3 = [sq3a, sq3b]
                        xst = [xst0, xst1]
                        x1c = [x1c0, x1c1]
                        rA = rows3[:, 0:512]
                        rB = rows3[:, 512:1024]
                        rC = rows3[:, 1024:1536]
                        p = Plan()
                        I(p, "gpsimd", lambda g: g.memset(rowM3[:], 0.0))
                        I(p, "gpsimd", lambda g: g.memset(rowR3[:], 0.0),
                          inc=(sPL, sPL.take()))
                        pad_ready = sPL.n
                        rows_evac = 0
                        h2_done = 0
                        bc_mm = 0
                        bc_evac = 0
                        x1m_prev = 0
                        x1done = {}
                        x1_store = {}
                        cast_done = {}
                        h2done = {}
                        for tb in range(TB):
                            t0 = tb * 512
                            lw = [(sDVE, h2_done)] if tb else []
                            for g4 in range(4):
                                src = pj_s[g4 * 8:(g4 + 1) * 8, :, t0:t0 + 512].rearrange(
                                    "c p t -> p c t")
                                I(p, "sync",
                                  lambda s, d=sp3(pblock[:, g4 * 4096:(g4 + 1) * 4096], 512), s2=src:
                                  s.dma_start(d, s2),
                                  waits=lw if g4 == 0 else (), inc=(sSP, sSP.take(16)))
                            blk_ld = sSP.n
                            pe_c = {}
                            for c in range(HC):
                                cw = [(sSP, blk_ld)] if c == 0 else []
                                if c >= 2:
                                    cw.append((sPE, pe_c[c - 2]))
                                psl = pblock[:, c * 512:(c + 1) * 512]
                                I(p, "vector",
                                  lambda v, a=psl, d=sq3[c % 2][:]: v.tensor_mul(d, a, a),
                                  waits=cw, inc=(sDVE, sDVE.take()))
                                sqd = sDVE.n
                                cw = [(sDVE, sqd)]
                                if c == 0 and rows_evac:
                                    cw.append((sACT, rows_evac))
                                I(p, "tensor", lambda t, r=psl,
                                  c0=(c == 0), c1=(c == HC - 1):
                                  t.matmul(ps_t1[:], onescol, r, start=c0, stop=c1),
                                  waits=cw)
                                I(p, "tensor", lambda t, r=sq3[c % 2][:],
                                  c0=(c == 0), c1=(c == HC - 1):
                                  t.matmul(ps_t2[:], onescol, r, start=c0, stop=c1),
                                  inc=(sPE, sPE.take()))
                                pe_c[c] = sPE.n
                            stats_done = sPE.n
                            rows_evac = ln_rows(p, ps_t1, ps_t2, rA, rB, rC,
                                                [(sPE, stats_done)])
                            rw = [(sACT, rows_evac)] + \
                                 ([(sPE, bc_mm)] if tb else [(sPL, pad_ready)])
                            I(p, "scalar", lambda a: a.copy(rowM3[0:1, :], rA), waits=rw)
                            I(p, "scalar", lambda a: a.copy(rowR3[0:1, :], rB),
                              inc=(sACT, sACT.take()))
                            rows_b = sACT.n
                            bw = [(sACT, rows_b)] + ([(sACT, bc_evac)] if tb else [])
                            I(p, "tensor", lambda t: t.matmul(
                                ps_bM2[:], onesM[:], rowM3[:], start=True, stop=True),
                              waits=bw)
                            I(p, "tensor", lambda t: t.matmul(
                                ps_bR2[:], onesM[:], rowR3[:], start=True, stop=True),
                              inc=(sPE, sPE.take()))
                            bc_mm = sPE.n
                            ew = [(sPE, bc_mm)] + ([(sDVE, x1m_prev)] if tb else [])
                            I(p, "scalar", lambda a: a.copy(Ma[:], ps_bM2[:]), waits=ew)
                            I(p, "scalar", lambda a: a.copy(Ra[:], ps_bR2[:]),
                              inc=(sACT, sACT.take()))
                            bc_evac = sACT.n
                            pe_c2 = {}
                            for c in range(HC):
                                gsl = tb * HC + c
                                j = c % 2
                                srcx = x_t[c:c + 1, :, t0:t0 + 512].rearrange(
                                    "o p t -> (o p) t")
                                lwx = []
                                if gsl >= 2:
                                    lwx.append((sDVE, x1done[gsl - 2]))
                                I(p, "sync", lambda s, d=xst[j][:], s2=srcx:
                                  s.dma_start(d, s2),
                                  waits=lwx, inc=(sXl[j], sXl[j].take(16)))
                                xld = (sXl[j], sXl[j].n)
                                psl = pblock[:, c * 512:(c + 1) * 512]
                                cw = [(sACT, bc_evac)] if c == 0 else []
                                I(p, "vector", lambda v, a=psl: v.tensor_sub(t1c[:], a, Ma[:]),
                                  waits=cw)
                                I(p, "vector", lambda v: v.tensor_mul(t2c[:], t1c[:], Ra[:]))
                                cw = [xld]
                                if gsl >= 2:
                                    cw.append(x1_store[gsl - 2])
                                    cw.append((sACT, cast_done[gsl - 2]))
                                I(p, "vector", lambda v, x=xst[j][:], d=x1c[j][:]:
                                  v.tensor_add(d, t2c[:], x),
                                  waits=cw, inc=(sDVE, sDVE.take()))
                                x1done[gsl] = sDVE.n
                                dstx1 = x1_s[c:c + 1, :, t0:t0 + 512].rearrange(
                                    "o p t -> (o p) t")
                                I(p, "gpsimd", lambda g, d=dstx1, s=x1c[j][:]:
                                  g.dma_start(d, s),
                                  waits=[(sDVE, x1done[gsl])],
                                  inc=(sXs[j], sXs[j].take(16)))
                                x1_store[gsl] = (sXs[j], sXs[j].n)
                                cwc = [(sDVE, x1done[gsl])]
                                if c >= 1:
                                    cwc.append((sPE, pe_c2[c - 1]))
                                I(p, "scalar", lambda a, s=x1c[j][:], d=psl: a.copy(d, s),
                                  waits=cwc, inc=(sACT, sACT.take()))
                                cast_done[gsl] = sACT.n
                                cd = sACT.n
                                I(p, "vector", lambda v, d=sq3[j][:], a=psl:
                                  v.tensor_mul(d, a, a),
                                  waits=[(sACT, cd)], inc=(sDVE, sDVE.take()))
                                sqd = sDVE.n
                                I(p, "tensor", lambda t, r=psl, c0=(c == 0), c1=(c == HC - 1):
                                  t.matmul(ps_t1[:], onescol, r, start=c0, stop=c1),
                                  waits=[(sDVE, sqd)])
                                I(p, "tensor", lambda t, r=sq3[j][:],
                                  c0=(c == 0), c1=(c == HC - 1):
                                  t.matmul(ps_t2[:], onescol, r, start=c0, stop=c1),
                                  inc=(sPE, sPE.take()))
                                pe_c2[c] = sPE.n
                            x1m_prev = sDVE.n
                            stats2_done = sPE.n
                            rows_evac = ln_rows(p, ps_t1, ps_t2, rA, rB, rC,
                                                [(sPE, stats2_done)])
                            rw = [(sACT, rows_evac), (sPE, bc_mm)]
                            I(p, "scalar", lambda a: a.copy(rowM3[0:1, :], rA), waits=rw)
                            I(p, "scalar", lambda a: a.copy(rowR3[0:1, :], rB),
                              inc=(sACT, sACT.take()))
                            rows_b = sACT.n
                            bw = [(sACT, rows_b), (sACT, bc_evac)]
                            I(p, "tensor", lambda t: t.matmul(
                                ps_bM2[:], onesM[:], rowM3[:], start=True, stop=True),
                              waits=bw)
                            I(p, "tensor", lambda t: t.matmul(
                                ps_bR2[:], onesM[:], rowR3[:], start=True, stop=True),
                              inc=(sPE, sPE.take()))
                            bc_mm = sPE.n
                            ew = [(sPE, bc_mm)] + ([(sDVE, h2_done)] if tb else [])
                            I(p, "scalar", lambda a: a.copy(M2[:], ps_bM2[:]), waits=ew)
                            I(p, "scalar", lambda a: a.copy(R2[:], ps_bR2[:]),
                              inc=(sACT, sACT.take()))
                            bc_evac = sACT.n
                            for c in range(HC):
                                gsl = tb * HC + c
                                psl = pblock[:, c * 512:(c + 1) * 512]
                                cw = [(sACT, bc_evac)] if c == 0 else []
                                dst = big[:, c * S + t0:c * S + t0 + 512]
                                I(p, "vector", lambda v, a=psl:
                                  v.tensor_sub(t1c[:], a, M2[:]), waits=cw)
                                I(p, "vector", lambda v, d=dst: v.tensor_mul(d, t1c[:], R2[:]),
                                  inc=(sDVE, sDVE.take()))
                                h2done[gsl] = sDVE.n
                            h2_done = sDVE.n
                        if debug:
                            I(p, "sync", lambda s: s.dma_start(h2_d, big[:]),
                              waits=[(sDVE, h2_done)], inc=(sSP, sSP.take(16)))
                            p.add("sync", lambda eng, v=sSP.n, h=sSP.h: eng.wait_ge(h, v))
                        p.emit(nc)

                if n_phases >= 6:
                    with contextlib.ExitStack() as _es:
                        bact = _es.enter_context(nc.sbuf_tensor("bact", [128, 8 * S], BF16))
                        wgb0 = _es.enter_context(nc.sbuf_tensor("wgb0", [128, 16 * 128], BF16))
                        wgb1 = _es.enter_context(nc.sbuf_tensor("wgb1", [128, 16 * 128], BF16))
                        wub0 = _es.enter_context(nc.sbuf_tensor("wub0", [128, 16 * 128], BF16))
                        wub1 = _es.enter_context(nc.sbuf_tensor("wub1", [128, 16 * 128], BF16))
                        wdf0 = _es.enter_context(nc.sbuf_tensor("wdf0", [128, 8 * 128], F32))
                        wdf1 = _es.enter_context(nc.sbuf_tensor("wdf1", [128, 8 * 128], F32))
                        wdb0 = _es.enter_context(nc.sbuf_tensor("wdb0", [128, 8 * 128], BF16))
                        wdb1 = _es.enter_context(nc.sbuf_tensor("wdb1", [128, 8 * 128], BF16))
                        s1a = _es.enter_context(nc.sbuf_tensor("s1a", [128, 512], BF16))
                        s1b = _es.enter_context(nc.sbuf_tensor("s1b", [128, 512], BF16))
                        pst0 = _es.enter_context(nc.sbuf_tensor("pst0", [128, 1024], F32))
                        pst1 = _es.enter_context(nc.sbuf_tensor("pst1", [128, 1024], F32))
                        ps_g0 = _es.enter_context(nc.psum_tensor("ps_g0", [128, 512], F32))
                        ps_g1 = _es.enter_context(nc.psum_tensor("ps_g1", [128, 512], F32))
                        ps_g2 = _es.enter_context(nc.psum_tensor("ps_g2", [128, 512], F32))
                        ps_g3 = _es.enter_context(nc.psum_tensor("ps_g3", [128, 512], F32))
                        ps_u0 = _es.enter_context(nc.psum_tensor("ps_u0", [128, 512], F32))
                        ps_u1 = _es.enter_context(nc.psum_tensor("ps_u1", [128, 512], F32))
                        ps_u2 = _es.enter_context(nc.psum_tensor("ps_u2", [128, 512], F32))
                        ps_u3 = _es.enter_context(nc.psum_tensor("ps_u3", [128, 512], F32))
                        wgb = [wgb0, wgb1]
                        wub = [wub0, wub1]
                        wdf = [wdf0, wdf1]
                        wdb = [wdb0, wdb1]
                        s1 = [s1a, s1b]
                        pst = [pst0, pst1]
                        gb = [ps_g0, ps_g1, ps_g2, ps_g3]
                        ub = [ps_u0, ps_u1, ps_u2, ps_u3]
                        dbanks = [gb, ub]
                        p = Plan()
                        silu_cnt = {}
                        act_cnt = {}
                        g_half = {}
                        u_half = {}
                        wgld = {}
                        wuld = {}
                        pd_evac = {}
                        cast_cnt = {}
                        wdb_free = {}
                        pst_free = [0, 0]
                        pd_last_super = 0
                        pd_evac_tail = 0
                        for isup, sup in enumerate(SUPERS):
                            nIc = len(sup)
                            for icl, Ic in enumerate(sup):
                                for half in range(2):
                                    u = Ic * 2 + half
                                    par = u % 2
                                    ksl = slice(half * 16, half * 16 + 16)
                                    lw = [(sPEg, g_half[u - 2])] if u >= 2 else []
                                    I(p, "gpsimd", lambda g, d=wgb[par][:],
                                      s=wg_t[Ic:Ic + 1, :, ksl, :].rearrange(
                                          "o p c i -> (o p) (c i)"):
                                      g.dma_start(d, s), waits=lw,
                                      inc=(sWg[par], sWg[par].take(16)))
                                    wgld[u] = (sWg[par], sWg[par].n)
                                    lw = [(sPEu, u_half[u - 2])] if u >= 2 else []
                                    I(p, "gpsimd", lambda g, d=wub[par][:],
                                      s=wu_t[Ic:Ic + 1, :, ksl, :].rearrange(
                                          "o p c i -> (o p) (c i)"):
                                      g.dma_start(d, s), waits=lw,
                                      inc=(sWu[par], sWu[par].take(16)))
                                    wuld[u] = (sWu[par], sWu[par].n)
                                for kc in range(HC):
                                    u = Ic * 2 + kc // 16
                                    lhs = wgb[u % 2][:, (kc % 16) * 128:(kc % 16) * 128 + 128]
                                    for tb in range(TB):
                                        cw = []
                                        if kc % 16 == 0 and tb == 0:
                                            cw.append(wgld[u])
                                        if kc == 0:
                                            prev = silu_cnt.get((Ic - 1, tb), 0)
                                            if prev:
                                                cw.append((sACTs, prev))
                                            if icl == 0 and isup >= 1 and tb == 0:
                                                cw.append((sACTd, pd_evac_tail))
                                        rhs = big[:, kc * S + tb * 512:kc * S + tb * 512 + 512]
                                        inc = None
                                        if (kc == 15 and tb == TB - 1) or kc == HC - 1:
                                            inc = (sPEg, sPEg.take())
                                        I(p, "tensor", lambda t, o=gb[tb][:], l=lhs, r=rhs,
                                          c0=(kc == 0), c1=(kc == HC - 1):
                                          t.matmul(o, l, r, start=c0, stop=c1),
                                          waits=cw, inc=inc)
                                        if kc == 15 and tb == TB - 1:
                                            g_half[Ic * 2] = sPEg.n
                                        if kc == HC - 1:
                                            if tb == TB - 1:
                                                g_half[Ic * 2 + 1] = sPEg.n
                                            silu_cnt[("pe", Ic, tb)] = sPEg.n
                                for kc in range(HC):
                                    u = Ic * 2 + kc // 16
                                    lhs = wub[u % 2][:, (kc % 16) * 128:(kc % 16) * 128 + 128]
                                    for tb in range(TB):
                                        cw = []
                                        if kc % 16 == 0 and tb == 0:
                                            cw.append(wuld[u])
                                        if kc == 0:
                                            prev = act_cnt.get((Ic - 1, tb), 0)
                                            if prev:
                                                cw.append((sDVEa, prev))
                                            if icl == 0 and isup >= 1 and tb == 0:
                                                cw.append((sACTd, pd_evac_tail))
                                        rhs = big[:, kc * S + tb * 512:kc * S + tb * 512 + 512]
                                        inc = None
                                        if (kc == 15 and tb == TB - 1) or kc == HC - 1:
                                            inc = (sPEu, sPEu.take())
                                        I(p, "tensor", lambda t, o=ub[tb][:], l=lhs, r=rhs,
                                          c0=(kc == 0), c1=(kc == HC - 1):
                                          t.matmul(o, l, r, start=c0, stop=c1),
                                          waits=cw, inc=inc)
                                        if kc == 15 and tb == TB - 1:
                                            u_half[Ic * 2] = sPEu.n
                                        if kc == HC - 1:
                                            if tb == TB - 1:
                                                u_half[Ic * 2 + 1] = sPEu.n
                                            act_cnt[("pe", Ic, tb)] = sPEu.n
                                for tb in range(TB):
                                    js = tb % 2
                                    cw = [(sPEg, silu_cnt[("pe", Ic, tb)])]
                                    prevd = act_cnt.get((Ic, tb - 2) if tb >= 2 else
                                                        (Ic - 1, tb + 2), 0)
                                    if prevd:
                                        cw.append((sDVEa, prevd))
                                    I(p, "scalar", lambda a, d=s1[js][:], s=gb[tb][:]:
                                      a.activation(d, s, AF.Silu),
                                      waits=cw, inc=(sACTs, sACTs.take()))
                                    silu_cnt[(Ic, tb)] = sACTs.n
                                    cw = [(sACTs, silu_cnt[(Ic, tb)]),
                                          (sPEu, act_cnt[("pe", Ic, tb)])]
                                    if isup >= 1:
                                        cw.append((sPEd, pd_last_super))
                                    dst = bact[:, icl * S + tb * 512:icl * S + tb * 512 + 512]
                                    I(p, "vector", lambda v, d=dst, a=s1[js][:], b=ub[tb][:]:
                                      v.tensor_mul(d, a, b),
                                      waits=cw, inc=(sDVEa, sDVEa.take()))
                                    act_cnt[(Ic, tb)] = sDVEa.n
                            act_all = sDVEa.n

                            def em_wd(hc, isup=isup, sup=sup, nIc=nIc):
                                par = hc % 2
                                gw = isup * HC + hc
                                lw = [(sACTn, cast_cnt[gw - 2])] if gw >= 2 else []
                                src = wd_t[hc:hc + 1, :, sup[0]:sup[0] + nIc, :].rearrange(
                                    "o p c i -> (o p) (c i)")
                                I(p, "sync", lambda s, d=wdf[par][:, 0:nIc * 128], s2=src:
                                  s.dma_start(d, s2), waits=lw,
                                  inc=(sWd[par], sWd[par].take(16)))
                                wld_d = (sWd[par], sWd[par].n)
                                cw = [wld_d]
                                if gw >= 2:
                                    cw.append((sPEd, wdb_free[gw - 2]))
                                I(p, "scalar", lambda a, d=wdb[par][:, 0:nIc * 128],
                                  s=wdf[par][:, 0:nIc * 128]: a.copy(d, s),
                                  waits=cw, inc=(sACTn, sACTn.take()))
                                cast_cnt[gw] = sACTn.n

                            em_wd(0)
                            for hc in range(HC):
                                par = hc % 2
                                gw = isup * HC + hc
                                if hc + 1 < HC:
                                    em_wd(hc + 1)
                                banks = dbanks[hc % 2]
                                d_done = {}
                                for icl in range(nIc):
                                    lhs = wdb[par][:, icl * 128:(icl + 1) * 128]
                                    for tb in range(TB):
                                        dslot = gw * TB + tb
                                        cw = []
                                        if icl == 0:
                                            if tb == 0:
                                                cw.append((sACTn, cast_cnt[gw]))
                                                cw.append((sDVEa, act_all))
                                            prev = pd_evac.get(dslot - 2 * TB, 0)
                                            if prev:
                                                cw.append((sACTd, prev))
                                        rhs = bact[:, icl * S + tb * 512:icl * S + tb * 512 + 512]
                                        inc = None
                                        if icl == nIc - 1:
                                            inc = (sPEd, sPEd.take())
                                        I(p, "tensor", lambda t, o=banks[tb][:], l=lhs, r=rhs,
                                          c0=(icl == 0), c1=(icl == nIc - 1):
                                          t.matmul(o, l, r, start=c0, stop=c1),
                                          waits=cw, inc=inc)
                                        if icl == nIc - 1:
                                            d_done[tb] = sPEd.n
                                wdb_free[gw] = sPEd.n
                                for tb in range(TB):
                                    dslot = gw * TB + tb
                                    pp = (dslot // 2) % 2
                                    cw = [(sPEd, d_done[tb])]
                                    if tb % 2 == 0 and pst_free[pp]:
                                        cw.append(pst_free[pp])
                                    I(p, "scalar",
                                      lambda a, d=pst[pp][:, (tb % 2) * 512:(tb % 2) * 512 + 512],
                                      s=banks[tb][:]: a.copy(d, s),
                                      waits=cw, inc=(sACTd, sACTd.take()))
                                    pd_evac[dslot] = sACTd.n
                                    if tb % 2 == 1:
                                        dsta = ac_s[hc:hc + 1, :,
                                                    (tb // 2) * 1024:(tb // 2) * 1024 + 1024
                                                    ].rearrange("o p t -> (o p) t")
                                        op = ALU.add if isup >= 1 else ALU.bypass
                                        I(p, "gpsimd", lambda g, d=dsta, s=pst[pp][:], o=op:
                                          g.dma_start(d, s, accum_op=o),
                                          waits=[(sACTd, pd_evac[dslot])],
                                          inc=(sAc[pp], sAc[pp].take(16)))
                                        pst_free[pp] = (sAc[pp], sAc[pp].n)
                            pd_last_super = sPEd.n
                            pd_evac_tail = sACTd.n
                        p.add("gpsimd", lambda eng, v=sAc[0].n, h=sAc[0].h: eng.wait_ge(h, v))
                        p.add("gpsimd", lambda eng, v=sAc[1].n, h=sAc[1].h: eng.wait_ge(h, v))
                        p.emit(nc)

        if n_phases >= 7:
            with contextlib.ExitStack() as _es:
                outb = _es.enter_context(nc.sbuf_tensor("outb", [128, HC * 512], F32))
                x1b6 = _es.enter_context(nc.sbuf_tensor("x1b6", [128, HC * 512], F32))
                ob16 = _es.enter_context(nc.sbuf_tensor("ob16", [128, 1024], BF16))
                sq6a = _es.enter_context(nc.sbuf_tensor("sq6a", [128, 512], BF16))
                sq6b = _es.enter_context(nc.sbuf_tensor("sq6b", [128, 512], BF16))
                t6 = _es.enter_context(nc.sbuf_tensor("t6", [128, 512], F32))
                rows6 = _es.enter_context(nc.sbuf_tensor("rows6", [1, 512 * 3], F32))
                rowM6 = _es.enter_context(nc.sbuf_tensor("rowM6", [128, 512], BF16))
                rowR6 = _es.enter_context(nc.sbuf_tensor("rowR6", [128, 512], BF16))
                M6 = _es.enter_context(nc.sbuf_tensor("M6", [128, 512], F32))
                R6 = _es.enter_context(nc.sbuf_tensor("R6", [128, 512], F32))
                ps_61 = _es.enter_context(nc.psum_tensor("ps_61", [1, 512], F32))
                ps_62 = _es.enter_context(nc.psum_tensor("ps_62", [1, 512], F32))
                ps_6M = _es.enter_context(nc.psum_tensor("ps_6M", [128, 512], F32))
                ps_6R = _es.enter_context(nc.psum_tensor("ps_6R", [128, 512], F32))
                sq6 = [sq6a, sq6b]
                rA = rows6[:, 0:512]
                rB = rows6[:, 512:1024]
                rC = rows6[:, 1024:1536]
                p = Plan()
                I(p, "gpsimd", lambda g: g.memset(rowM6[:], 0.0))
                I(p, "gpsimd", lambda g: g.memset(rowR6[:], 0.0), inc=(sPL, sPL.take()))
                pad_ready = sPL.n
                rows_evac = 0
                y_math = 0
                bc_mm = 0
                bc_evac = 0
                y_store = 0
                for tb in range(TB):
                    t0 = tb * 512
                    lw = [(sY, y_store)] if tb else []
                    for g4 in range(4):
                        src = ac_s[g4 * 8:(g4 + 1) * 8, :, t0:t0 + 512].rearrange(
                            "c p t -> p c t")
                        I(p, "sync",
                          lambda s, d=sp3(outb[:, g4 * 4096:(g4 + 1) * 4096], 512), s2=src:
                          s.dma_start(d, s2),
                          waits=lw if g4 == 0 else (), inc=(sSP, sSP.take(16)))
                        src2 = x1_s[g4 * 8:(g4 + 1) * 8, :, t0:t0 + 512].rearrange(
                            "c p t -> p c t")
                        I(p, "sync",
                          lambda s, d=sp3(x1b6[:, g4 * 4096:(g4 + 1) * 4096], 512), s2=src2:
                          s.dma_start(d, s2), inc=(sSP, sSP.take(16)))
                    blk_ld = sSP.n
                    pe_c = {}
                    dve_c = {}
                    for c in range(HC):
                        osl = outb[:, c * 512:(c + 1) * 512]
                        ob = ob16[:, (c % 2) * 512:(c % 2) * 512 + 512]
                        cw = [(sSP, blk_ld)] if c == 0 else []
                        if c >= 2:
                            cw.append((sPE, pe_c[c - 2]))
                        I(p, "scalar", lambda a, s=osl, d=ob: a.copy(d, s),
                          waits=cw, inc=(sACT, sACT.take()))
                        cd = sACT.n
                        cw = [(sACT, cd)]
                        if c >= 2:
                            cw.append((sPE, pe_c[c - 2]))
                        I(p, "vector", lambda v, d=sq6[c % 2][:], a=ob:
                          v.tensor_mul(d, a, a),
                          waits=cw, inc=(sDVE, sDVE.take()))
                        dve_c[c] = sDVE.n
                        cw = [(sDVE, dve_c[c])]
                        if c == 0 and rows_evac:
                            cw.append((sACT, rows_evac))
                        I(p, "tensor", lambda t, a=ob, c0=(c == 0), c1=(c == HC - 1):
                          t.matmul(ps_61[:], onescol, a, start=c0, stop=c1),
                          waits=cw)
                        I(p, "tensor", lambda t, r=sq6[c % 2][:],
                          c0=(c == 0), c1=(c == HC - 1):
                          t.matmul(ps_62[:], onescol, r, start=c0, stop=c1),
                          inc=(sPE, sPE.take()))
                        pe_c[c] = sPE.n
                    stats_done = sPE.n
                    rows_evac = ln_rows(p, ps_61, ps_62, rA, rB, rC, [(sPE, stats_done)])
                    rw = [(sACT, rows_evac)] + \
                         ([(sPE, bc_mm)] if tb else [(sPL, pad_ready)])
                    I(p, "scalar", lambda a: a.copy(rowM6[0:1, :], rA), waits=rw)
                    I(p, "scalar", lambda a: a.copy(rowR6[0:1, :], rB),
                      inc=(sACT, sACT.take()))
                    rows_b = sACT.n
                    bw = [(sACT, rows_b)] + ([(sACT, bc_evac)] if tb else [])
                    I(p, "tensor", lambda t: t.matmul(ps_6M[:], onesM[:], rowM6[:],
                                                      start=True, stop=True), waits=bw)
                    I(p, "tensor", lambda t: t.matmul(ps_6R[:], onesM[:], rowR6[:],
                                                      start=True, stop=True),
                      inc=(sPE, sPE.take()))
                    bc_mm = sPE.n
                    ew = [(sPE, bc_mm)] + ([(sDVE, y_math)] if tb else [])
                    I(p, "scalar", lambda a: a.copy(M6[:], ps_6M[:]), waits=ew)
                    I(p, "scalar", lambda a: a.copy(R6[:], ps_6R[:]),
                      inc=(sACT, sACT.take()))
                    bc_evac = sACT.n
                    for c in range(HC):
                        cw = [(sACT, bc_evac)] if c == 0 else []
                        osl = outb[:, c * 512:(c + 1) * 512]
                        xsl = x1b6[:, c * 512:(c + 1) * 512]
                        I(p, "vector", lambda v, a=osl: v.tensor_sub(t6[:], a, M6[:]),
                          waits=cw)
                        I(p, "vector", lambda v: v.tensor_mul(t6[:], t6[:], R6[:]))
                        inc = (sDVE, sDVE.take()) if c == HC - 1 else None
                        I(p, "vector", lambda v, d=osl, x=xsl: v.tensor_add(d, t6[:], x),
                          inc=inc)
                    y_math = sDVE.n
                    for g4 in range(4):
                        dst = y_t[g4 * 8:(g4 + 1) * 8, :, t0:t0 + 512].rearrange(
                            "c p t -> p c t")
                        I(p, "gpsimd",
                          lambda g, d=dst, s=sp3(outb[:, g4 * 4096:(g4 + 1) * 4096], 512):
                          g.dma_start(d, s),
                          waits=[(sDVE, y_math)] if g4 == 0 else (),
                          inc=(sY, sY.take(16)))
                    y_store = sY.n
                p.add("sync", lambda eng, v=sY.n, h=sY.h: eng.wait_ge(h, v))
                p.emit(nc)

    nc.finalize()
    return nc


def _alibi_slopes(n):
    cp2 = 2 ** math.floor(math.log2(n))
    base = 2.0 ** (-8.0 / cp2)
    slopes = base ** np.arange(1, cp2 + 1, dtype=np.float32)
    if cp2 != n:
        extra_base = 2.0 ** (-4.0 / cp2)
        extra = extra_base ** np.arange(1, 1 + 2 * (n - cp2), 2, dtype=np.float32)
        slopes = np.concatenate([slopes, extra])
    return slopes.astype(np.float32)


def host_inputs(L0_attn_w_qkv, L0_attn_w_o, L0_ffn_w_gate, L0_ffn_w_up, L0_ffn_w_down):
    bf = ml_dtypes.bfloat16
    wqkv = np.asarray(L0_attn_w_qkv, dtype=np.float32)
    wo = np.asarray(L0_attn_w_o, dtype=np.float32)
    wg = np.asarray(L0_ffn_w_gate, dtype=np.float32)
    wu = np.asarray(L0_ffn_w_up, dtype=np.float32)
    wd = np.asarray(L0_ffn_w_down, dtype=np.float32)
    shared = {}
    shared["wqk_t"] = np.ascontiguousarray(
        wqkv[:, :40 * 128].reshape(HC, 128, 40, 128).transpose(2, 1, 0, 3))
    shared["wv_t"] = np.ascontiguousarray(
        wqkv[:, 40 * 128:].reshape(HC, 128, 1024).transpose(1, 0, 2))
    shared["wo_t"] = np.ascontiguousarray(
        wo.reshape(HC, 128, HC, 128).transpose(2, 1, 0, 3))
    shared["wg_t"] = np.ascontiguousarray(
        wg.reshape(IC, 128, HC, 128).transpose(0, 3, 2, 1))
    shared["wu_t"] = np.ascontiguousarray(
        wu.reshape(IC, 128, HC, 128).transpose(0, 3, 2, 1))
    shared["wd_t"] = np.ascontiguousarray(
        wd.reshape(HC, 128, IC, 128).transpose(0, 3, 2, 1))
    q = np.arange(S, dtype=np.float32)
    rel = q[None, :] - q[:, None]
    Dt = np.sqrt(np.maximum(rel, 0.0))
    Dt[rel < 0] = MASK_D
    shared["dT_c"] = np.ascontiguousarray(Dt.astype(bf).reshape(16, 128, S))
    slopes = _alibi_slopes(NH)
    dneg = np.zeros((128, NH, 128), dtype=np.float32)
    for h in range(NH):
        dneg[np.arange(128), h, np.arange(128)] = -slopes[h] / SCALE
    shared["dneg_c"] = np.ascontiguousarray(dneg.reshape(128, NH * 128).astype(bf))
    shared["onesM_c"] = np.ones((128, 128), dtype=bf)
    return shared


def kernel(L0_attn_hidden_states, L0_attn_w_qkv, L0_attn_w_o,
           L0_ffn_w_gate, L0_ffn_w_up, L0_ffn_w_down):
    x = np.asarray(L0_attn_hidden_states, dtype=np.float32)
    B = x.shape[0]
    shared = host_inputs(L0_attn_w_qkv, L0_attn_w_o, L0_ffn_w_gate,
                         L0_ffn_w_up, L0_ffn_w_down)
    nc = build_program()
    in_maps = []
    for c in range(B):
        m = dict(shared)
        m["x_t"] = np.ascontiguousarray(x[c].T).reshape(HC, 128, S)
        in_maps.append(m)
    res = run_bass_kernel_spmd(nc, in_maps, core_ids=list(range(B)))
    y = np.empty((B, S, H), dtype=np.float32)
    for c in range(B):
        y[c] = res.results[c]["y_t"].reshape(H, S).T
    return y
